# version 03
# baseline (speedup 1.0000x reference)
import numpy as np
import ml_dtypes
import concourse.bass as bass
import concourse.mybir as mybir
from concourse.bass_utils import run_bass_kernel_spmd
from contextlib import ExitStack

F32 = mybir.dt.float32
BF16 = mybir.dt.bfloat16
U32 = mybir.dt.uint32
AF = mybir.ActivationFunctionType
ALU = mybir.AluOpType
AX = mybir.AxisListType.X

P = 128
T = 2048
D = 1024
NCH = D // P
NB = 8
LN_EPS = 1e-5

bf16 = ml_dtypes.bfloat16
SEGS = [slice(j * 512, (j + 1) * 512) for j in range(4)]



def build_s1():
    nc = bass.Bass("TRN2", target_bir_lowering=False)
    xT = nc.dram_tensor("xT", [D, T], F32, kind="ExternalInput").ap()
    Wq = nc.dram_tensor("Wq", [D, D], F32, kind="ExternalInput").ap()
    Wk = nc.dram_tensor("Wk", [D, D], F32, kind="ExternalInput").ap()
    Wv = nc.dram_tensor("Wv", [D, D], F32, kind="ExternalInput").ap()
    Wdown = nc.dram_tensor("Wdown", [D, 256], F32, kind="ExternalInput").ap()
    Wupk = nc.dram_tensor("Wupk", [256, D], F32, kind="ExternalInput").ap()
    Wupv = nc.dram_tensor("Wupv", [256, D], F32, kind="ExternalInput").ap()
    o_xn = nc.dram_tensor("xn_f", [D, T], F32, kind="ExternalOutput").ap()
    o_xnb = nc.dram_tensor("xn_b", [D, T], BF16, kind="ExternalOutput").ap()
    o_q = nc.dram_tensor("q_b", [D, T], BF16, kind="ExternalOutput").ap()
    o_krec = nc.dram_tensor("krec_b", [D, T], BF16, kind="ExternalOutput").ap()
    o_vrec = nc.dram_tensor("vrec_b", [D, T], BF16, kind="ExternalOutput").ap()

    es = ExitStack()
    sb = lambda n, s, d: es.enter_context(nc.sbuf_tensor(n, s, d))
    x2 = [sb(f"x{i}", [P, T], F32) for i in range(2)]
    scr = sb("scr", [P, T], F32)
    xn32 = sb("xn32", [P, T], F32)
    ones = sb("ones", [P, P], F32)
    mu = sb("mu", [P, T], F32)
    aux = sb("aux", [P, T], F32)
    mrs = sb("mrs", [P, T], F32)
    xnb = [sb(f"xnb{c}", [P, T], BF16) for c in range(NCH)]
    kvb = [sb(f"kvb{c}", [P, T], BF16) for c in range(NCH)]
    wA = sb("wA", [P, NCH * D], BF16)
    wB = sb("wB", [P, NCH * D], BF16)
    lat = [sb(f"lat{i}", [P, T], BF16) for i in range(2)]
    outb = sb("outb", [P, T], BF16)
    ps_a = es.enter_context(nc.psum_tensor("ps_a", [P, T], F32))
    ps_b = es.enter_context(nc.psum_tensor("ps_b", [P, T], F32))
    s_x0 = es.enter_context(nc.semaphore("s_x0"))
    s_x1 = es.enter_context(nc.semaphore("s_x1"))
    s_lw = es.enter_context(nc.semaphore("s_lw"))
    s_lw2 = es.enter_context(nc.semaphore("s_lw2"))
    s_lw3 = es.enter_context(nc.semaphore("s_lw3"))
    s_mm = es.enter_context(nc.semaphore("s_mm"))
    s_ev = es.enter_context(nc.semaphore("s_ev"))
    s_sq = es.enter_context(nc.semaphore("s_sq"))
    s_st = es.enter_context(nc.semaphore("s_st"))
    s_so = es.enter_context(nc.semaphore("s_so"))

    def wt(buf, kc, mc):
        return buf[:, kc * D + mc * P: kc * D + (mc + 1) * P]

    with nc.Block() as block:
        @block.sync
        def _(sync):
            for c in range(NCH):
                if c >= 2:
                    sync.wait_ge(s_mm, 2 * (c - 1))
                    sync.wait_ge(s_sq, c - 1)
                sync.dma_start(out=x2[c % 2][:], in_=xT[c * P:(c + 1) * P, :]).then_inc(s_x0 if c % 2 == 0 else s_x1, 16)

        @block.gpsimd
        def _(g):
            for kc in range(NCH):
                g.dma_start(out=wA[:, kc * D:(kc + 1) * D], in_=Wk[kc * P:(kc + 1) * P, :]).then_inc(s_lw, 16)
                g.dma_start(out=wB[:, kc * D:(kc + 1) * D], in_=Wv[kc * P:(kc + 1) * P, :]).then_inc(s_lw, 16)

        @block.vector
        def _(v):
            v.memset(ones[:], 1.0).then_inc(s_ev, 1)

        @block.scalar
        def _(s):
            for c in range(NCH):
                s.wait_ge(s_x0 if c % 2 == 0 else s_x1, 16 * (c // 2 + 1))
                if c >= 1:
                    s.wait_ge(s_mm, 2 * c)
                s.activation(scr[:], x2[c % 2][:], AF.Square).then_inc(s_sq, 1)

        @block.tensor
        def _(t):
            t.wait_ge(s_ev, 1)
            for c in range(NCH):
                t.wait_ge(s_x0 if c % 2 == 0 else s_x1, 16 * (c // 2 + 1))
                for j in range(4):
                    mm = t.matmul(ps_a[:, SEGS[j]], ones[:], x2[c % 2][:, SEGS[j]],
                                  start=(c == 0), stop=(c == NCH - 1))
                mm.then_inc(s_mm, 1)
                t.wait_ge(s_sq, c + 1)
                for j in range(4):
                    mm = t.matmul(ps_b[:, SEGS[j]], ones[:], scr[:, SEGS[j]],
                                  start=(c == 0), stop=(c == NCH - 1))
                mm.then_inc(s_mm, 1)

        @block.scalar
        def _(s):
            s.wait_ge(s_mm, 16)
            s.activation(mu[:], ps_a[:], AF.Copy, scale=1.0 / D).then_inc(s_ev, 1)
            s.activation(aux[:], ps_b[:], AF.Copy, scale=1.0 / D).then_inc(s_ev, 1)
            s.wait_ge(s_ev, 2)
            s.activation(scr[:], mu[:], AF.Square).then_inc(s_ev, 1)

        @block.vector
        def _(v):
            v.wait_ge(s_ev, 4)
            v.tensor_sub(aux[:], aux[:], scr[:]).then_inc(s_ev, 1)
            v.wait_ge(s_ev, 5)
            v.tensor_scalar_add(aux[:], aux[:], float(LN_EPS)).then_inc(s_ev, 1)

        @block.scalar
        def _(s):
            s.wait_ge(s_ev, 6)
            s.activation(aux[:], aux[:], AF.Ln).then_inc(s_ev, 1)
            s.wait_ge(s_ev, 7)
            s.activation(aux[:], aux[:], AF.Exp, scale=-0.5).then_inc(s_ev, 1)

        @block.vector
        def _(v):
            v.wait_ge(s_ev, 8)
            v.tensor_mul(mrs[:], mu[:], aux[:]).then_inc(s_ev, 1)

        @block.sync
        def _(sync):
            for c in range(NCH):
                if c < 2:
                    sync.wait_ge(s_sq, 7 + c)
                    sync.wait_ge(s_mm, 14 + 2 * c)
                else:
                    sync.wait_ge(s_ev, 9 + 2 * (c - 1))
                sync.dma_start(out=x2[c % 2][:], in_=xT[c * P:(c + 1) * P, :]).then_inc(s_x0 if c % 2 == 0 else s_x1, 16)

        @block.vector
        def _(v):
            for c in range(NCH):
                v.wait_ge(s_x0 if c % 2 == 0 else s_x1, 64 + 16 * (c // 2 + 1))
                if c >= 1:
                    v.wait_ge(s_sq, 8 + c)
                    v.wait_ge(s_so, 32 * c)
                v.tensor_mul(xn32[:], x2[c % 2][:], aux[:]).then_inc(s_ev, 1)
                v.wait_ge(s_ev, 9 + 2 * c + 1)
                v.tensor_sub(xn32[:], xn32[:], mrs[:]).then_inc(s_ev, 1)

        @block.scalar
        def _(s):
            for c in range(NCH):
                s.wait_ge(s_ev, 9 + 2 * (c + 1))
                s.activation(xnb[c][:], xn32[:], AF.Copy).then_inc(s_sq, 1)

        @block.gpsimd
        def _(g):
            for c in range(NCH):
                g.wait_ge(s_sq, 8 + (c + 1))
                g.dma_start(out=o_xn[c * P:(c + 1) * P, :], in_=xn32[:]).then_inc(s_so, 16)
                g.dma_start(out=o_xnb[c * P:(c + 1) * P, :], in_=xnb[c][:]).then_inc(s_so, 16)

        @block.tensor
        def _(t):
            t.wait_ge(s_lw, 256)
            t.wait_ge(s_sq, 16)
            for mc in range(NCH):
                for j in range(4):
                    for kc in range(NCH):
                        t.matmul(ps_a[:, SEGS[j]], wt(wA, kc, mc), xnb[kc][:, SEGS[j]],
                                 start=(kc == 0), stop=False)
                        mm = t.matmul(ps_a[:, SEGS[j]], wt(wB, kc, mc), xnb[kc][:, SEGS[j]],
                                      start=False, stop=(kc == NCH - 1))
                mm.then_inc(s_mm, 1)
                t.wait_ge(s_ev, 25 + mc + 1)

        @block.scalar
        def _(s):
            for mc in range(NCH):
                s.wait_ge(s_mm, 16 + mc + 1)
                s.activation(kvb[mc][:], ps_a[:], AF.Copy).then_inc(s_ev, 1)

        @block.gpsimd
        def _(g):
            g.wait_ge(s_mm, 24)
            for kc in range(NCH):
                g.dma_start(out=wA[:, kc * 256:(kc + 1) * 256], in_=Wdown[kc * P:(kc + 1) * P, :]).then_inc(s_lw2, 16)
            for kc in range(2):
                g.dma_start(out=wA[:, 2048 + kc * D: 2048 + (kc + 1) * D], in_=Wupk[kc * P:(kc + 1) * P, :]).then_inc(s_lw2, 16)
                g.dma_start(out=wA[:, 4096 + kc * D: 4096 + (kc + 1) * D], in_=Wupv[kc * P:(kc + 1) * P, :]).then_inc(s_lw2, 16)
            for kc in range(NCH):
                g.dma_start(out=wB[:, kc * D:(kc + 1) * D], in_=Wq[kc * P:(kc + 1) * P, :]).then_inc(s_lw3, 16)

        @block.tensor
        def _(t):
            t.wait_ge(s_lw2, 192)
            t.wait_ge(s_ev, 33)
            for mc in range(2):
                for j in range(4):
                    for kc in range(NCH):
                        mm = t.matmul(ps_b[:, SEGS[j]],
                                      wA[:, kc * 256 + mc * P: kc * 256 + (mc + 1) * P],
                                      kvb[kc][:, SEGS[j]], start=(kc == 0), stop=(kc == NCH - 1))
                mm.then_inc(s_mm, 1)
                t.wait_ge(s_ev, 33 + mc + 1)

        @block.scalar
        def _(s):
            for mc in range(2):
                s.wait_ge(s_mm, 24 + mc + 1)
                s.activation(lat[mc][:], ps_b[:], AF.Copy).then_inc(s_ev, 1)

        @block.tensor
        def _(t):
            t.wait_ge(s_ev, 35)
            for mc in range(NCH):
                for j in range(4):
                    for kc in range(2):
                        mm = t.matmul(ps_a[:, SEGS[j]],
                                      wA[:, 2048 + kc * D + mc * P: 2048 + kc * D + (mc + 1) * P],
                                      lat[kc][:, SEGS[j]], start=(kc == 0), stop=(kc == 1))
                mm.then_inc(s_mm, 1)
                t.wait_ge(s_ev, 35 + mc + 1)
            for mc in range(NCH):
                for j in range(4):
                    for kc in range(2):
                        mm = t.matmul(ps_a[:, SEGS[j]],
                                      wA[:, 4096 + kc * D + mc * P: 4096 + kc * D + (mc + 1) * P],
                                      lat[kc][:, SEGS[j]], start=(kc == 0), stop=(kc == 1))
                mm.then_inc(s_mm, 1)
                t.wait_ge(s_ev, 35 + 8 + mc + 1)
            t.wait_ge(s_lw3, 128)
            for mc in range(NCH):
                for j in range(4):
                    for kc in range(NCH):
                        mm = t.matmul(ps_a[:, SEGS[j]], wt(wB, kc, mc), xnb[kc][:, SEGS[j]],
                                      start=(kc == 0), stop=(kc == NCH - 1))
                mm.then_inc(s_mm, 1)
                t.wait_ge(s_ev, 35 + 16 + mc + 1)

        @block.scalar
        def _(s):
            for i in range(24):
                s.wait_ge(s_mm, 26 + i + 1)
                if i >= 1:
                    s.wait_ge(s_st, 16 * i)
                s.activation(outb[:], ps_a[:], AF.Copy).then_inc(s_ev, 1)

        @block.sync
        def _(sync):
            outs = [o_krec, o_vrec, o_q]
            for i in range(24):
                sync.wait_ge(s_ev, 35 + i + 1)
                dst = outs[i // 8]
                mc = i % 8
                sync.dma_start(out=dst[mc * P:(mc + 1) * P, :], in_=outb[:]).then_inc(s_st, 16)
            sync.wait_ge(s_st, 16 * 24)
            sync.wait_ge(s_so, 256)
    es.close()
    return nc



def build_s2():
    nc = bass.Bass("TRN2", target_bir_lowering=False)
    q_b = nc.dram_tensor("q_b", [D, T], BF16, kind="ExternalInput").ap()
    krec = nc.dram_tensor("krec_b", [D, T], BF16, kind="ExternalInput").ap()
    vrec = nc.dram_tensor("vrec_b", [D, T], BF16, kind="ExternalInput").ap()
    Wo = nc.dram_tensor("Wo", [D, D], F32, kind="ExternalInput").ap()
    o_aw = nc.dram_tensor("attnwo", [D, T], F32, kind="ExternalOutput").ap()

    NH = 16
    HD = 64
    VW = 65
    SCALE = float(1.0 / (32.0 + 1e-8))

    es = ExitStack()
    sb = lambda n, s, d: es.enter_context(nc.sbuf_tensor(n, s, d))
    qf = [sb(f"qf{c}", [P, T], BF16) for c in range(NCH)]
    kf = [sb(f"kf{c}", [P, T], BF16) for c in range(NCH)]
    vT = sb("vT", [P, NH * 16 * VW], BF16)
    vst = sb("vst", [P, T], BF16)
    B = sb("B", [P, 2 * T], BF16)
    t1 = [sb(f"t1{i}", [P, 1024], F32) for i in range(2)]
    PT = [sb(f"PT{i}", [P, 1024], BF16) for i in range(2)]
    apre = [sb(f"apre{c}", [P, T], BF16) for c in range(NCH)]
    zrow = sb("zrow", [1, T], F32)
    rz = sb("rz", [1, T], F32)
    rzb = [sb(f"rzb{i}", [64, 1024], F32) for i in range(2)]
    ident = sb("ident", [P, P], F32)
    identb = sb("identb", [P, P], BF16)
    onesP = sb("onesP", [P, P], F32)
    ones1 = sb("ones1", [1, 64], F32)
    wo = sb("wo", [P, NCH * D], BF16)
    outw = sb("outw", [P, T], F32)
    ps_s = [es.enter_context(nc.psum_tensor(f"ps_s{i}", [P, 1024], F32)) for i in range(2)]
    ps_av = es.enter_context(nc.psum_tensor("ps_av", [P, T], F32))
    sems = {}
    names = ("s_qk", "s_vs", "s_tr", "s_trev", "s_id", "s_dk", "s_vo", "s_sg",
             "s_t1", "s_pt", "s_ag", "s_zc", "s_rr", "s_zb", "s_hf", "s_wo",
             "s_ow", "s_st")
    for name in names:
        sems[name] = es.enter_context(nc.semaphore(name))
    (s_qk, s_vs, s_tr, s_trev, s_id, s_dk, s_vo, s_sg, s_t1, s_pt, s_ag,
     s_zc, s_rr, s_zb, s_hf, s_wo, s_ow, s_st) = [sems[n] for n in names]

    def vtile(h, kc):
        return vT[:, (h * 16 + kc) * VW: (h * 16 + kc) * VW + VW]

    def hrows(buflist, h):
        return buflist[h // 2][(h % 2) * HD:(h % 2) * HD + HD, :]

    with nc.Block() as block:
        @block.sync
        def _(sync):
            for c in range(NCH):
                sync.dma_start(out=qf[c][:], in_=q_b[c * P:(c + 1) * P, :]).then_inc(s_qk, 16)
                sync.dma_start(out=kf[c][:], in_=krec[c * P:(c + 1) * P, :]).then_inc(s_qk, 16)

        @block.gpsimd
        def _(g):
            g.iota(ident[:], pattern=[[-1, P]], base=0, channel_multiplier=1,
                   allow_small_or_imprecise_dtypes=True).then_inc(s_id, 1)
            g.iota(t1[0][:], pattern=[[-1, 1024]], base=2048, channel_multiplier=1,
                   allow_small_or_imprecise_dtypes=True)
            g.iota(t1[1][:], pattern=[[-1, 1024]], base=1024, channel_multiplier=1,
                   allow_small_or_imprecise_dtypes=True).then_inc(s_dk, 1)
            for c in range(NCH):
                g.dma_start(out=wo[:, c * D:(c + 1) * D], in_=Wo[c * P:(c + 1) * P, :]).then_inc(s_wo, 16)

        @block.scalar
        def _(s):
            s.wait_ge(s_id, 1)
            s.activation(ident[:], ident[:], AF.Square).then_inc(s_id, 1)
            s.wait_ge(s_dk, 1)
            s.activation(t1[0][:], t1[0][:], AF.Abs).then_inc(s_dk, 1)
            s.wait_ge(s_dk, 2)
            s.activation(B[:, 0:1024], t1[0][:], AF.Exp, scale=-1e-4).then_inc(s_dk, 1)
            s.activation(t1[1][:], t1[1][:], AF.Abs).then_inc(s_dk, 1)
            s.wait_ge(s_dk, 4)
            s.activation(B[:, 1024:2048], t1[1][:], AF.Exp, scale=-1e-4).then_inc(s_dk, 1)

        @block.vector
        def _(v):
            v.memset(ones1[:], 1.0)
            v.memset(vT[:].rearrange("p (t c) -> p t c", c=VW)[:, :, 64:65], 1.0).then_inc(s_vo, 1)
            v.wait_ge(s_id, 2)
            v.tensor_scalar(ident[:], ident[:], -1.0, 1.0, ALU.mult, ALU.add).then_inc(s_id, 1)
            v.wait_ge(s_id, 3)
            v.tensor_relu(ident[:], ident[:]).then_inc(s_id, 1)

        @block.scalar
        def _(s):
            s.wait_ge(s_id, 4)
            s.activation(identb[:], ident[:], AF.Copy).then_inc(s_id, 1)

        @block.gpsimd
        def _(g):
            g.wait_ge(s_dk, 5)
            g.iota(t1[0][:], pattern=[[-1, 1024]], base=0, channel_multiplier=1,
                   allow_small_or_imprecise_dtypes=True)
            g.iota(t1[1][:], pattern=[[-1, 1024]], base=-1024, channel_multiplier=1,
                   allow_small_or_imprecise_dtypes=True).then_inc(s_dk, 1)
            for c in range(NCH):
                if c >= 1:
                    g.wait_ge(s_tr, 32 * c)
                g.dma_start(out=vst[:], in_=vrec[c * P:(c + 1) * P, :]).then_inc(s_vs, 16)

        @block.scalar
        def _(s):
            s.wait_ge(s_dk, 6)
            s.activation(t1[0][:], t1[0][:], AF.Abs).then_inc(s_dk, 1)
            s.wait_ge(s_dk, 7)
            s.activation(B[:, 2048:3072], t1[0][:], AF.Exp, scale=-1e-4).then_inc(s_dk, 1)
            s.activation(t1[1][:], t1[1][:], AF.Abs).then_inc(s_dk, 1)
            s.wait_ge(s_dk, 9)
            s.activation(B[:, 3072:4096], t1[1][:], AF.Exp, scale=-1e-4).then_inc(s_dk, 1)

        @block.tensor
        def _(t):
            t.wait_ge(s_id, 5)
            t.wait_ge(s_vo, 1)
            n_tr = 0
            for c in range(NCH):
                t.wait_ge(s_vs, 16 * (c + 1))
                for half in range(2):
                    h = 2 * c + half
                    for kcg in range(2):
                        for k8 in range(8):
                            kc = kcg * 8 + k8
                            tr = t.transpose(ps_av[:, 0:256].bitcast(BF16)[:, k8 * 64:(k8 + 1) * 64],
                                             vst[half * HD:(half + 1) * HD, kc * P:(kc + 1) * P],
                                             identb[half * HD:(half + 1) * HD, half * HD:(half + 1) * HD])
                            n_tr += 1
                        tr.then_inc(s_tr, 8)
                        t.wait_ge(s_trev, (n_tr // 8))

        @block.vector
        def _(v):
            ng = 0
            for c in range(NCH):
                for half in range(2):
                    h = 2 * c + half
                    for kcg in range(2):
                        ng += 1
                        v.wait_ge(s_tr, ng * 8)
                        src = ps_av[:, 0:256].bitcast(BF16).rearrange("p (k e) -> p k e", e=64)
                        dst = vT[:, (h * 16 + kcg * 8) * VW: (h * 16 + kcg * 8 + 8) * VW] \
                            .rearrange("p (k e) -> p k e", e=VW)[:, :, 0:64]
                        v.tensor_copy(dst, src).then_inc(s_trev, 1)

        @block.tensor
        def _(t):
            t.wait_ge(s_qk, 256)
            t.wait_ge(s_dk, 10)
            t.wait_ge(s_trev, 32)
            g = 0
            for h in range(NH):
                if h >= 1:
                    t.wait_ge(s_hf, 2 * h)
                    t.wait_ge(s_zc, 3 * h)
                for qh in range(2):
                    for kc in range(16):
                        if g >= 2:
                            t.wait_ge(s_t1, g - 1)
                        for jj in range(2):
                            mm = t.matmul(ps_s[g % 2][:, jj * 512:(jj + 1) * 512],
                                          hrows(kf, h)[:, kc * P:(kc + 1) * P],
                                          hrows(qf, h)[:, qh * 1024 + jj * 512: qh * 1024 + (jj + 1) * 512],
                                          start=True, stop=True)
                        mm.then_inc(s_sg, 1)
                        t.wait_ge(s_pt, g + 1)
                        for jj in range(2):
                            mm = t.matmul(ps_av[0:VW, qh * 1024 + jj * 512: qh * 1024 + (jj + 1) * 512],
                                          vtile(h, kc), PT[g % 2][:, jj * 512:(jj + 1) * 512],
                                          start=(kc == 0), stop=(kc == 15))
                        mm.then_inc(s_ag, 1)
                        g += 1
                t.wait_ge(s_rr, h + 1)
                for qh in range(2):
                    for jj in range(2):
                        mm = t.matmul(ps_s[qh][0:64, jj * 512:(jj + 1) * 512], ones1[:],
                                      rz[0:1, qh * 1024 + jj * 512: qh * 1024 + (jj + 1) * 512],
                                      start=True, stop=True)
                mm.then_inc(s_zb, 1)

        @block.vector
        def _(v):
            v.wait_ge(s_dk, 10)
            g = 0
            for h in range(NH):
                for qh in range(2):
                    for kc in range(16):
                        v.wait_ge(s_sg, g + 1)
                        if g >= 2:
                            v.wait_ge(s_pt, g - 1)
                        dslice = B[:, 2048 - kc * P + qh * 1024: 2048 - kc * P + qh * 1024 + 1024]
                        v.scalar_tensor_tensor(t1[g % 2][:], ps_s[g % 2][:], SCALE,
                                               dslice, ALU.mult, ALU.mult).then_inc(s_t1, 1)
                        g += 1
                v.wait_ge(s_zc, 3 * h + 1)
                v.reciprocal(rz[:], zrow[:]).then_inc(s_rr, 1)
                v.wait_ge(s_zc, 3 * h + 3)
                v.tensor_tensor(hrows(apre, h)[:, 0:1024], ps_av[0:64, 0:1024], rzb[0][:], ALU.mult).then_inc(s_hf, 1)
                v.wait_ge(s_hf, 2 * h + 1)
                v.tensor_tensor(hrows(apre, h)[:, 1024:2048], ps_av[0:64, 1024:2048], rzb[1][:], ALU.mult).then_inc(s_hf, 1)

        @block.scalar
        def _(s):
            g = 0
            for h in range(NH):
                for qh in range(2):
                    for kc in range(16):
                        s.wait_ge(s_t1, g + 1)
                        if g >= 2:
                            s.wait_ge(s_ag, g - 1)
                        s.activation(PT[g % 2][:], t1[g % 2][:], AF.Exp).then_inc(s_pt, 1)
                        g += 1
                s.wait_ge(s_ag, 32 * (h + 1))
                if h >= 1:
                    s.wait_ge(s_rr, h)
                s.activation(zrow[:], ps_av[64:65, :], AF.Copy).then_inc(s_zc, 1)
                s.wait_ge(s_zb, h + 1)
                if h >= 1:
                    s.wait_ge(s_hf, 2 * h)
                s.activation(rzb[0][:], ps_s[0][0:64, :], AF.Copy).then_inc(s_zc, 1)
                s.activation(rzb[1][:], ps_s[1][0:64, :], AF.Copy).then_inc(s_zc, 1)

        @block.tensor
        def _(t):
            t.wait_ge(s_hf, 32)
            t.wait_ge(s_wo, 128)
            for mc in range(NCH):
                for jj in range(4):
                    for kc in range(NCH):
                        mm = t.matmul(ps_av[:, SEGS[jj]], wo[:, kc * D + mc * P: kc * D + (mc + 1) * P],
                                      apre[kc][:, SEGS[jj]], start=(kc == 0), stop=(kc == NCH - 1))
                mm.then_inc(s_sg, 1)
                t.wait_ge(s_ow, mc + 1)

        @block.scalar
        def _(s):
            for mc in range(NCH):
                s.wait_ge(s_sg, 512 + mc + 1)
                if mc >= 1:
                    s.wait_ge(s_st, 16 * mc)
                s.activation(outw[:], ps_av[:], AF.Copy).then_inc(s_ow, 1)

        @block.sync
        def _(sync):
            for mc in range(NCH):
                sync.wait_ge(s_ow, mc + 1)
                sync.dma_start(out=o_aw[mc * P:(mc + 1) * P, :], in_=outw[:]).then_inc(s_st, 16)
            sync.wait_ge(s_st, 128)
    es.close()
    return nc



def build_s3():
    nc = bass.Bass("TRN2", target_bir_lowering=False)
    xn = nc.dram_tensor("xn_f", [D, T], F32, kind="ExternalInput").ap()
    lng = nc.dram_tensor("lng", [P, NCH], F32, kind="ExternalInput").ap()
    lnb = nc.dram_tensor("lnb", [P, NCH], F32, kind="ExternalInput").ap()
    aw1 = nc.dram_tensor("aw1", [D, 4 * D], F32, kind="ExternalInput").ap()
    ab1 = nc.dram_tensor("ab1", [P, 32], F32, kind="ExternalInput").ap()
    aw2 = nc.dram_tensor("aw2", [4 * D, D], F32, kind="ExternalInput").ap()
    ab2 = nc.dram_tensor("ab2", [P, NCH], F32, kind="ExternalInput").ap()
    gwt = nc.dram_tensor("gwt", [P, 48], F32, kind="ExternalInput").ap()
    gb = nc.dram_tensor("gate_b", [1, 6], F32, kind="ExternalInput").ap()
    o_xn2 = nc.dram_tensor("xn2_b", [D, T], BF16, kind="ExternalOutput").ap()
    o_abel = nc.dram_tensor("abel_b", [D, T], BF16, kind="ExternalOutput").ap()
    o_wt6 = nc.dram_tensor("wT6", [8, T], F32, kind="ExternalOutput").ap()

    es = ExitStack()
    sb = lambda n, s, d: es.enter_context(nc.sbuf_tensor(n, s, d))
    xn2b = [sb(f"xn2b{c}", [P, T], BF16) for c in range(NCH)]
    aw1b = sb("aw1b", [P, NCH * 4 * D], BF16)
    biases = sb("biases", [P, 64], F32)
    ones = sb("ones", [P, P], F32)
    identt = sb("identt", [P, P], F32)
    gwf = sb("gwf", [P, 48], F32)
    gwg = sb("gwg", [P, 48], F32)
    brow = sb("brow", [1, 8], F32)
    gbr = sb("gbr", [1, 8], F32)
    gbb = sb("gbb", [P, 8], F32)
    lgt = sb("lgt", [P, 8], F32)
    m8 = sb("m8", [P, 8], F32)
    w1 = sb("w1", [P, 1], F32)
    w1s = sb("w1s", [P, 1], F32)
    w2 = sb("w2", [P, 1], F32)
    dd = sb("dd", [P, 1], F32)
    mA = sb("mA", [P, 6], F32)
    mB = sb("mB", [P, 6], F32)
    wfull2 = [sb(f"wfull{i}", [P, 6], F32) for i in range(2)]
    wt6sb = sb("wt6sb", [8, T], F32)
    lacc = sb("lacc", [P, 256], F32)
    lscr = sb("lscr", [P, 256], F32)
    ps_a = es.enter_context(nc.psum_tensor("ps_a", [P, T], F32))
    ps_b = es.enter_context(nc.psum_tensor("ps_b", [P, T], F32))
    names = ("s_x0", "s_x1", "s_lw", "s_lw2", "s_lw3", "s_mm", "s_ev", "s_sq",
             "s_st", "s_am", "s_ae", "s_ao", "s_gt", "s_gv", "s_lg", "s_sig",
             "s_tp", "s_wc", "s_gg", "s_le", "s_la")
    sems = {n: es.enter_context(nc.semaphore(n)) for n in names}
    (s_x0, s_x1, s_lw, s_lw2, s_lw3, s_mm, s_ev, s_sq, s_st, s_am, s_ae, s_ao,
     s_gt, s_gv, s_lg, s_sig, s_tp, s_wc, s_gg, s_le, s_la) = [sems[n] for n in names]

    ln_es = ExitStack()
    sbl = lambda n, s, d: ln_es.enter_context(nc.sbuf_tensor(n, s, d))
    x2 = [sbl(f"x{i}", [P, T], F32) for i in range(2)]
    scr = sbl("scr", [P, T], F32)
    xn32 = sbl("xn32", [P, T], F32)
    mu = sbl("mu", [P, T], F32)
    aux = sbl("aux", [P, T], F32)
    mrs = sbl("mrs", [P, T], F32)

    with nc.Block() as block:
        @block.sync
        def _(sync):
            for c in range(NCH):
                if c >= 2:
                    sync.wait_ge(s_mm, 2 * (c - 1))
                    sync.wait_ge(s_sq, c - 1)
                sync.dma_start(out=x2[c % 2][:], in_=xn[c * P:(c + 1) * P, :]).then_inc(s_x0 if c % 2 == 0 else s_x1, 16)

        @block.gpsimd
        def _(g):
            for kc in range(NCH):
                g.dma_start(out=aw1b[:, kc * 4 * D:(kc + 1) * 4 * D], in_=aw1[kc * P:(kc + 1) * P, :]).then_inc(s_lw, 16)
            g.dma_start(out=biases[:, 0:32], in_=ab1).then_inc(s_lw, 16)
            g.dma_start(out=biases[:, 32:40], in_=ab2).then_inc(s_lw, 16)
            g.dma_start(out=biases[:, 40:48], in_=lng).then_inc(s_lw, 16)
            g.dma_start(out=biases[:, 48:56], in_=lnb).then_inc(s_lw, 16)
            g.dma_start(out=gwf[:], in_=gwt).then_inc(s_lw, 16)
            g.dma_start(out=gbr[0:1, 0:6], in_=gb).then_inc(s_lw, 16)
            g.iota(identt[:], pattern=[[-1, P]], base=0, channel_multiplier=1,
                   allow_small_or_imprecise_dtypes=True).then_inc(s_gt, 1)

        @block.vector
        def _(v):
            v.memset(ones[:], 1.0)
            v.memset(lgt[:], -1e30)
            v.memset(gbr[0:1, 6:8], 0.0)
            v.memset(brow[:], 0.0)
            v.memset(lacc[:], 0.0)
            v.memset(wt6sb[:], 0.0).then_inc(s_ev, 1)

        @block.scalar
        def _(s):
            s.wait_ge(s_gt, 1)
            s.activation(identt[:], identt[:], AF.Square).then_inc(s_gt, 1)
            for c in range(NCH):
                s.wait_ge(s_x0 if c % 2 == 0 else s_x1, 16 * (c // 2 + 1))
                if c >= 1:
                    s.wait_ge(s_mm, 2 * c)
                s.activation(scr[:], x2[c % 2][:], AF.Square).then_inc(s_sq, 1)

        @block.vector
        def _(v):
            v.wait_ge(s_gt, 2)
            v.tensor_scalar(identt[:], identt[:], -1.0, 1.0, ALU.mult, ALU.add).then_inc(s_gt, 1)
            v.wait_ge(s_gt, 3)
            v.tensor_relu(identt[:], identt[:]).then_inc(s_gt, 1)

        @block.tensor
        def _(t):
            t.wait_ge(s_ev, 1)
            for c in range(NCH):
                t.wait_ge(s_x0 if c % 2 == 0 else s_x1, 16 * (c // 2 + 1))
                for j in range(4):
                    mm = t.matmul(ps_a[:, SEGS[j]], ones[:], x2[c % 2][:, SEGS[j]],
                                  start=(c == 0), stop=(c == NCH - 1))
                mm.then_inc(s_mm, 1)
                t.wait_ge(s_sq, c + 1)
                for j in range(4):
                    mm = t.matmul(ps_b[:, SEGS[j]], ones[:], scr[:, SEGS[j]],
                                  start=(c == 0), stop=(c == NCH - 1))
                mm.then_inc(s_mm, 1)

        @block.scalar
        def _(s):
            s.wait_ge(s_mm, 16)
            s.activation(mu[:], ps_a[:], AF.Copy, scale=1.0 / D).then_inc(s_ev, 1)
            s.activation(aux[:], ps_b[:], AF.Copy, scale=1.0 / D).then_inc(s_ev, 1)
            s.wait_ge(s_ev, 2)
            s.activation(scr[:], mu[:], AF.Square).then_inc(s_ev, 1)

        @block.vector
        def _(v):
            v.wait_ge(s_ev, 4)
            v.tensor_sub(aux[:], aux[:], scr[:]).then_inc(s_ev, 1)
            v.wait_ge(s_ev, 5)
            v.tensor_scalar_add(aux[:], aux[:], float(LN_EPS)).then_inc(s_ev, 1)

        @block.scalar
        def _(s):
            s.wait_ge(s_ev, 6)
            s.activation(aux[:], aux[:], AF.Ln).then_inc(s_ev, 1)
            s.wait_ge(s_ev, 7)
            s.activation(aux[:], aux[:], AF.Exp, scale=-0.5).then_inc(s_ev, 1)

        @block.vector
        def _(v):
            v.wait_ge(s_ev, 8)
            v.tensor_mul(mrs[:], mu[:], aux[:]).then_inc(s_ev, 1)
            v.wait_ge(s_lw, 224)
            for c in range(NCH):
                v.tensor_scalar(gwg[:, 6 * c:6 * (c + 1)], gwf[:, 6 * c:6 * (c + 1)],
                                biases[:, 40 + c:41 + c], None, ALU.mult).then_inc(s_gg, 1)

        @block.sync
        def _(sync):
            for c in range(NCH):
                if c < 2:
                    sync.wait_ge(s_sq, 7 + c)
                    sync.wait_ge(s_mm, 14 + 2 * c)
                else:
                    sync.wait_ge(s_ev, 9 + 3 * (c - 1))
                sync.dma_start(out=x2[c % 2][:], in_=xn[c * P:(c + 1) * P, :]).then_inc(s_x0 if c % 2 == 0 else s_x1, 16)

        @block.vector
        def _(v):
            v.wait_ge(s_lw, 224)
            for c in range(NCH):
                v.wait_ge(s_x0 if c % 2 == 0 else s_x1, 64 + 16 * (c // 2 + 1))
                if c >= 1:
                    v.wait_ge(s_ev, 9 + 3 * c)
                    v.wait_ge(s_lg, c)
                v.tensor_mul(xn32[:], x2[c % 2][:], aux[:]).then_inc(s_ev, 1)
                v.wait_ge(s_ev, 9 + 3 * c + 1)
                v.tensor_sub(xn32[:], xn32[:], mrs[:]).then_inc(s_ev, 1)
                v.wait_ge(s_ev, 9 + 3 * c + 2)
                v.tensor_scalar(xn2b[c][:], xn32[:], biases[:, 40 + c:41 + c],
                                biases[:, 48 + c:49 + c], ALU.mult, ALU.add).then_inc(s_ev, 1)
                if c >= 1:
                    v.wait_ge(s_le, c)
                    v.tensor_add(lacc[:], lacc[:], lscr[:]).then_inc(s_la, 1)
            v.wait_ge(s_le, 8)
            v.tensor_add(lacc[:], lacc[:], lscr[:]).then_inc(s_la, 1)

        @block.tensor
        def _(t):
            t.wait_ge(s_gg, 8)
            for c in range(NCH):
                t.wait_ge(s_ev, 9 + 3 * c + 2)
                if c >= 1:
                    t.wait_ge(s_le, c)
                for tt in range(16):
                    mm = t.matmul(ps_b[:, 16 * tt:16 * tt + 6],
                                  xn32[:, tt * P:(tt + 1) * P],
                                  gwg[:, c * 6:(c + 1) * 6],
                                  start=True, stop=True)
                mm.then_inc(s_lg, 1)

        @block.scalar
        def _(s):
            for c in range(NCH):
                s.wait_ge(s_lg, c + 1)
                if c >= 1:
                    s.wait_ge(s_la, c)
                s.activation(lscr[:], ps_b[:, 0:256], AF.Copy).then_inc(s_le, 1)

        @block.tensor
        def _(t):
            t.wait_ge(s_lw, 224)
            t.wait_ge(s_ev, 2)
            for c in range(NCH):
                mm = t.matmul(ps_a[0:1, 1040:1046], biases[:, 48 + c:49 + c],
                              gwf[:, c * 6:(c + 1) * 6], start=(c == 0), stop=(c == NCH - 1))
            mm.then_inc(s_gt, 1)

        @block.scalar
        def _(s):
            s.wait_ge(s_gt, 5)
            s.activation(brow[0:1, 0:6], ps_a[0:1, 1040:1046], AF.Copy).then_inc(s_gt, 1)

        @block.vector
        def _(v):
            v.wait_ge(s_gt, 6)
            v.tensor_add(gbr[:], gbr[:], brow[:]).then_inc(s_gt, 1)

        @block.tensor
        def _(t):
            t.wait_ge(s_gt, 7)
            mm = t.matmul(ps_a[:, 1024:1032], ones[0:1, :], gbr[:], start=True, stop=True)
            mm.then_inc(s_gt, 1)

        @block.scalar
        def _(s):
            s.wait_ge(s_gt, 8)
            s.activation(gbb[:], ps_a[:, 1024:1032], AF.Copy).then_inc(s_gt, 1)

        @block.vector
        def _(v):
            for tt in range(16):
                b = 10 * tt
                v.wait_ge(s_la, 8)
                v.wait_ge(s_gt, 9)
                v.tensor_tensor(lgt[:, 0:6], lacc[:, 16 * tt:16 * tt + 6], gbb[:, 0:6], ALU.add).then_inc(s_gv, 1)
                v.wait_ge(s_gv, b + 1)
                v.max(out=m8[:], in_=lgt[:]).then_inc(s_gv, 1)
                v.wait_ge(s_gv, b + 2)
                v.tensor_sub(w1[:], m8[:, 0:1], m8[:, 1:2]).then_inc(s_gv, 1)
                v.wait_ge(s_sig, tt + 1)
                v.tensor_scalar(w2[:], w1s[:], -1.0, 1.0, ALU.mult, ALU.add).then_inc(s_gv, 1)
                v.wait_ge(s_gv, b + 4)
                v.tensor_sub(dd[:], w1s[:], w2[:]).then_inc(s_gv, 1)
                v.tensor_scalar(mA[:], lgt[:, 0:6], m8[:, 0:1], None, ALU.is_ge).then_inc(s_gv, 1)
                v.tensor_scalar(mB[:], lgt[:, 0:6], m8[:, 1:2], None, ALU.is_ge).then_inc(s_gv, 1)
                v.wait_ge(s_gv, b + 7)
                v.tensor_scalar(mA[:], mA[:], dd[:, 0:1], None, ALU.mult).then_inc(s_gv, 1)
                v.tensor_scalar(mB[:], mB[:], w2[:, 0:1], None, ALU.mult).then_inc(s_gv, 1)
                v.wait_ge(s_gv, b + 9)
                if tt >= 2:
                    v.wait_ge(s_tp, tt - 1)
                v.tensor_add(wfull2[tt % 2][:], mA[:], mB[:]).then_inc(s_gv, 1)

        @block.scalar
        def _(s):
            for tt in range(18):
                if tt < 16:
                    s.wait_ge(s_gv, 10 * tt + 3)
                    s.activation(w1s[:], w1[:], AF.Sigmoid).then_inc(s_sig, 1)
                j = tt - 2
                if 0 <= j < 16:
                    s.wait_ge(s_tp, j + 1)
                    s.activation(wt6sb[0:6, j * P:(j + 1) * P],
                                 ps_a[0:8, (j % 2) * 512:(j % 2) * 512 + 128][0:6, :],
                                 AF.Copy).then_inc(s_wc, 1)

        @block.tensor
        def _(t):
            for j in range(16):
                t.wait_ge(s_gv, 10 * j + 10)
                if j >= 2:
                    t.wait_ge(s_wc, j - 1)
                tr = t.transpose(ps_a[0:8, (j % 2) * 512:(j % 2) * 512 + 128][0:6, :],
                                 wfull2[j % 2][:, 0:6], identt[:])
                tr.then_inc(s_tp, 1)

        @block.sync
        def _(sync):
            for c in range(NCH):
                sync.wait_ge(s_ev, 9 + 3 * (c + 1))
                sync.dma_start(out=o_xn2[c * P:(c + 1) * P, :], in_=xn2b[c][:]).then_inc(s_st, 16)
            sync.wait_ge(s_wc, 16)
            sync.dma_start(out=o_wt6, in_=wt6sb[:]).then_inc(s_st, 16)
    ln_es.close()

    h1 = sb("h1", [P, 32 * 512], BF16)
    aw2b = sb("aw2b", [P, 32 * 1024], BF16)
    outa = [sb(f"outa{i}", [P, 512], BF16) for i in range(2)]

    with nc.Block() as block:
        @block.gpsimd
        def _(g):
            g.wait_ge(s_ev, 33)
            g.wait_ge(s_lg, 8)
            for sl in range(32):
                g.dma_start(out=aw2b[:, sl * 1024:(sl + 1) * 1024],
                            in_=aw2[sl * P:(sl + 1) * P, :]).then_inc(s_lw2, 16)

        @block.tensor
        def _(t):
            t.wait_ge(s_ev, 33)
            t.wait_ge(s_wc, 16)
            for tch in range(4):
                tseg = slice(tch * 512, (tch + 1) * 512)
                for hh in range(32):
                    gh = 32 * tch + hh
                    slot = ps_a[:, (hh % 4) * 512:(hh % 4 + 1) * 512]
                    if gh >= 4:
                        t.wait_ge(s_ae, gh - 3)
                    for kc in range(NCH):
                        mm = t.matmul(slot, aw1b[:, kc * 4096 + hh * P: kc * 4096 + (hh + 1) * P],
                                      xn2b[kc][:, tseg], start=(kc == 0), stop=(kc == NCH - 1))
                    mm.then_inc(s_am, 1)
                t.wait_ge(s_lw2, 512)
                t.wait_ge(s_ae, 32 * tch + 32)
                for mc in range(NCH):
                    ga = 8 * tch + mc
                    slot = ps_b[:, (mc % 4) * 512:(mc % 4 + 1) * 512]
                    if ga >= 4:
                        t.wait_ge(s_ao, ga - 3)
                    for kc2 in range(32):
                        mm = t.matmul(slot, aw2b[:, kc2 * 1024 + mc * P: kc2 * 1024 + (mc + 1) * P],
                                      h1[:, kc2 * 512:(kc2 + 1) * 512],
                                      start=(kc2 == 0), stop=(kc2 == 31))
                    mm.then_inc(s_am, 1)

        @block.scalar
        def _(s):
            s.wait_ge(s_ev, 33)
            for tch in range(4):
                for hh in range(32):
                    nm = 40 * tch + hh
                    s.wait_ge(s_am, nm + 1)
                    slot = ps_a[:, (hh % 4) * 512:(hh % 4 + 1) * 512]
                    s.activation(h1[:, hh * 512:(hh + 1) * 512], slot, AF.Gelu,
                                 bias=biases[:, hh:hh + 1]).then_inc(s_ae, 1)

        @block.vector
        def _(v):
            for tch in range(4):
                for mc in range(NCH):
                    nm = 40 * tch + 32 + mc
                    ga = 8 * tch + mc
                    v.wait_ge(s_am, nm + 1)
                    if ga >= 2:
                        v.wait_ge(s_st, 144 + 16 * ga)
                    slot = ps_b[:, (mc % 4) * 512:(mc % 4 + 1) * 512]
                    v.tensor_scalar(outa[ga % 2][:], slot, biases[:, 32 + mc:33 + mc],
                                    None, ALU.add).then_inc(s_ao, 1)

        @block.sync
        def _(sync):
            for tch in range(4):
                for mc in range(NCH):
                    ga = 8 * tch + mc
                    sync.wait_ge(s_ao, ga + 1)
                    sync.dma_start(out=o_abel[mc * P:(mc + 1) * P, tch * 512:(tch + 1) * 512],
                                   in_=outa[ga % 2][:]).then_inc(s_st, 16)
            sync.wait_ge(s_st, 144 + 512)
    es.close()
    return nc



def build_s4():
    nc = bass.Bass("TRN2", target_bir_lowering=False)
    xn2 = nc.dram_tensor("xn2_b", [D, T], BF16, kind="ExternalInput").ap()
    ew1 = nc.dram_tensor("ew1", [6 * D, 2 * D], F32, kind="ExternalInput").ap()
    eb1t = nc.dram_tensor("eb1t", [P, 96], F32, kind="ExternalInput").ap()
    ew2 = nc.dram_tensor("ew2", [6 * 2 * D, D], F32, kind="ExternalInput").ap()
    eb2 = nc.dram_tensor("eb2", [6, D], F32, kind="ExternalInput").ap()
    wt6 = nc.dram_tensor("wT6", [8, T], F32, kind="ExternalInput").ap()
    o_moe = nc.dram_tensor("moe", [D, T], F32, kind="ExternalOutput").ap()

    es = ExitStack()
    sb = lambda n, s, d: es.enter_context(nc.sbuf_tensor(n, s, d))
    xnb = [sb(f"xnb{c}", [P, T], BF16) for c in range(NCH)]
    macc = [sb(f"macc{c}", [P, T], F32) for c in range(NCH)]
    ew1b = sb("ew1b", [P, NCH * 2 * D], BF16)
    ew2b = sb("ew2b", [P, 16 * D], BF16)
    hprime = sb("hprime", [P, 16 * 512], BF16)
    hg = [sb(f"hg{i}", [P, 512], BF16) for i in range(2)]
    wb = sb("wb", [P, T], BF16)
    wt6sb = sb("wt6sb", [8, T], F32)
    wrow = sb("wrow", [1, T], F32)
    eb1sb = sb("eb1sb", [P, 96], F32)
    eb2sb = sb("eb2sb", [8, D], F32)
    ones1 = sb("ones1", [1, P], F32)
    ps_a = es.enter_context(nc.psum_tensor("ps_a", [P, T], F32))
    ps_b = es.enter_context(nc.psum_tensor("ps_b", [P, T], F32))
    names = ("s_xn", "s_lw", "s_w1", "s_w2", "s_mm", "s_ae", "s_av", "s_st", "s_ms")
    sems = {n: es.enter_context(nc.semaphore(n)) for n in names}
    (s_xn, s_lw, s_w1, s_w2, s_mm, s_ae, s_av, s_st, s_ms) = [sems[n] for n in names]

    def mm_wb(e):
        return 9 + 97 * e

    def mm_h(e, tch, hch):
        return 9 + 97 * e + 1 + tch * 24 + hch

    def mm_out(e, tch, dch):
        return 9 + 97 * e + 1 + tch * 24 + 16 + dch

    def ae_wb(e):
        return 9 + 65 * e

    def ae_gelu(e, tch, hch):
        return 9 + 65 * e + 1 + tch * 16 + hch

    def av_mult(e, tch, hch):
        return 96 * e + 24 * tch + hch + 1

    def av_add(e, tch, dch):
        return 96 * e + 24 * tch + 16 + dch + 1

    with nc.Block() as block:
        @block.sync
        def _(sync):
            for c in range(NCH):
                sync.dma_start(out=xnb[c][:], in_=xn2[c * P:(c + 1) * P, :]).then_inc(s_xn, 16)

        @block.gpsimd
        def _(g):
            g.dma_start(out=wt6sb[:], in_=wt6).then_inc(s_lw, 16)
            g.dma_start(out=eb1sb[:], in_=eb1t).then_inc(s_lw, 16)
            g.dma_start(out=eb2sb[0:6, :], in_=eb2).then_inc(s_lw, 16)
            g.wait_ge(s_lw, 48)
            for e in range(6):
                if e >= 1:
                    g.wait_ge(s_mm, mm_wb(e - 1))
                g.dma_start(out=wrow[:], in_=wt6sb[e:e + 1, :]).then_inc(s_lw, 16)
                if e >= 1:
                    g.wait_ge(s_mm, mm_h(e - 1, 3, 15))
                for kc in range(NCH):
                    g.dma_start(out=ew1b[:, kc * 2048:(kc + 1) * 2048],
                                in_=ew1[(e * NCH + kc) * P:(e * NCH + kc + 1) * P, :]).then_inc(s_w1, 16)
                if e >= 1:
                    g.wait_ge(s_mm, mm_out(e - 1, 3, 7))
                for kc2 in range(16):
                    g.dma_start(out=ew2b[:, kc2 * 1024:(kc2 + 1) * 1024],
                                in_=ew2[(e * 16 + kc2) * P:(e * 16 + kc2 + 1) * P, :]).then_inc(s_w2, 16)

        @block.vector
        def _(v):
            v.memset(ones1[:], 1.0).then_inc(s_ms, 1)

        @block.tensor
        def _(t):
            t.wait_ge(s_lw, 48)
            for c in range(NCH):
                if c >= 1:
                    t.wait_ge(s_ae, c)
                for j in range(4):
                    mm = t.matmul(ps_a[:, SEGS[j]], eb2sb[0:6, c * P:(c + 1) * P],
                                  wt6sb[0:6, SEGS[j]], start=True, stop=True)
                mm.then_inc(s_mm, 1)
            t.wait_ge(s_xn, 128)
            for e in range(6):
                t.wait_ge(s_lw, 48 + 16 * (e + 1))
                t.wait_ge(s_ms, 1)
                if e >= 1:
                    t.wait_ge(s_av, av_add(e - 1, 3, 7))
                else:
                    t.wait_ge(s_ae, 8)
                for j in range(4):
                    mm = t.matmul(ps_b[:, SEGS[j]], ones1[0:1, :], wrow[0:1, SEGS[j]],
                                  start=True, stop=True)
                mm.then_inc(s_mm, 1)
                for tch in range(4):
                    tseg = slice(tch * 512, (tch + 1) * 512)
                    for hch in range(16):
                        t.wait_ge(s_w1, 128 * (e + 1))
                        k = tch * 16 + hch
                        if k >= 4:
                            pg = mm_h(e, (k - 4) // 16, (k - 4) % 16)
                            t.wait_ge(s_ae, ae_gelu(e, (k - 4) // 16, (k - 4) % 16))
                        slot = ps_a[:, (k % 4) * 512:(k % 4 + 1) * 512]
                        for kc in range(NCH):
                            mm = t.matmul(slot, ew1b[:, kc * 2048 + hch * P: kc * 2048 + (hch + 1) * P],
                                          xnb[kc][:, tseg], start=(kc == 0), stop=(kc == NCH - 1))
                        mm.then_inc(s_mm, 1)
                    t.wait_ge(s_w2, 256 * (e + 1))
                    t.wait_ge(s_av, av_mult(e, tch, 15))
                    for dch in range(8):
                        jj = tch * 8 + dch
                        if jj >= 4:
                            t.wait_ge(s_av, av_add(e, (jj - 4) // 8, (jj - 4) % 8))
                        elif e == 0:
                            t.wait_ge(s_ae, ae_wb(0))
                        else:
                            t.wait_ge(s_ae, ae_wb(e))
                        slot = ps_b[:, (jj % 4) * 512:(jj % 4 + 1) * 512]
                        for kc2 in range(16):
                            mm = t.matmul(slot, ew2b[:, kc2 * 1024 + dch * P: kc2 * 1024 + (dch + 1) * P],
                                          hprime[:, kc2 * 512:(kc2 + 1) * 512],
                                          start=(kc2 == 0), stop=(kc2 == 15))
                        mm.then_inc(s_mm, 1)

        @block.scalar
        def _(s):
            for c in range(NCH):
                s.wait_ge(s_mm, c + 1)
                s.activation(macc[c][:], ps_a[:], AF.Copy).then_inc(s_ae, 1)
            for e in range(6):
                s.wait_ge(s_mm, mm_wb(e))
                if e >= 1:
                    s.wait_ge(s_av, av_mult(e - 1, 3, 15))
                s.activation(wb[:], ps_b[:], AF.Copy).then_inc(s_ae, 1)
                for tch in range(4):
                    for hch in range(16):
                        s.wait_ge(s_mm, mm_h(e, tch, hch))
                        ig = e * 64 + tch * 16 + hch
                        if ig >= 2:
                            pig = ig - 2
                            pe_, ptch, phch = pig // 64, (pig % 64) // 16, pig % 16
                            s.wait_ge(s_av, av_mult(pe_, ptch, phch))
                        k = tch * 16 + hch
                        slot = ps_a[:, (k % 4) * 512:(k % 4 + 1) * 512]
                        s.activation(hg[ig % 2][:], slot, AF.Gelu,
                                     bias=eb1sb[:, e * 16 + hch:e * 16 + hch + 1]).then_inc(s_ae, 1)

        @block.vector
        def _(v):
            for e in range(6):
                for tch in range(4):
                    tseg = slice(tch * 512, (tch + 1) * 512)
                    for hch in range(16):
                        v.wait_ge(s_ae, ae_gelu(e, tch, hch))
                        ig = e * 64 + tch * 16 + hch
                        if tch >= 1:
                            v.wait_ge(s_mm, mm_out(e, tch - 1, 7))
                        elif e >= 1:
                            v.wait_ge(s_mm, mm_out(e - 1, 3, 7))
                        v.tensor_tensor(hprime[:, hch * 512:(hch + 1) * 512], hg[ig % 2][:],
                                        wb[:, tseg], ALU.mult).then_inc(s_av, 1)
                    for dch in range(8):
                        v.wait_ge(s_mm, mm_out(e, tch, dch))
                        if e == 0 and tch == 0:
                            v.wait_ge(s_ae, 8)
                        v.tensor_tensor(macc[dch][:, tseg], ps_b[:, (tch * 8 + dch) % 4 * 512:((tch * 8 + dch) % 4 + 1) * 512],
                                        macc[dch][:, tseg], ALU.add).then_inc(s_av, 1)

        @block.sync
        def _(sync):
            sync.wait_ge(s_av, 576)
            for c in range(NCH):
                sync.dma_start(out=o_moe[c * P:(c + 1) * P, :], in_=macc[c][:]).then_inc(s_st, 16)
            sync.wait_ge(s_st, 128)
    es.close()
    return nc



import math as _math


def build_s5():
    nc = bass.Bass("TRN2", target_bir_lowering=False)
    G = nc.dram_tensor("G", [D, D], F32, kind="ExternalInput").ap()
    o_E = nc.dram_tensor("E", [D, D], F32, kind="ExternalOutput").ap()
    o_ET = nc.dram_tensor("ET", [D, D], F32, kind="ExternalOutput").ap()

    es = ExitStack()
    sb = lambda n, s, d: es.enter_context(nc.sbuf_tensor(n, s, d))
    NMAT = 11
    pool = [sb(f"m{i}", [P, NCH * D], BF16) for i in range(NMAT)]
    cid = sb("cid", [P, 19 * P], BF16)
    identf = sb("identf", [P, P], F32)
    identb = sb("identb", [P, P], BF16)
    gld = sb("gld", [P, D], F32)
    stage = sb("stage", [P, D], F32)
    ps_a = es.enter_context(nc.psum_tensor("ps_a", [P, T], F32))
    names = ("s_g", "s_mm", "s_ae", "s_av", "s_id", "s_st")
    sems = {n: es.enter_context(nc.semaphore(n)) for n in names}
    (s_g, s_mm, s_ae, s_av, s_id, s_st) = [sems[n] for n in names]

    A, ID, A2, A3, A4 = 0, 1, 2, 3, 4
    B0, B1, B2, B0p, B1p, B2p = 5, 6, 7, 8, 9, 10
    p_, pp_ = 0, 1
    E1, E1T = 2, 3

    C = [1.0 / _math.factorial(k) for k in range(13)]
    coefs = []
    for k in range(13):
        coefs.append(C[k])
    for k in (1, 3, 5, 7, 9, 11):
        coefs.append(-C[k])
    NEG1 = len(coefs)

    def cslot(val):
        return cid[:, coefs.index(val) * P:(coefs.index(val) + 1) * P]

    cnt = {"mm": 0, "ae": 0, "av": 0}

    with nc.Block() as block:
        @block.sync
        def _(sync):
            for c in range(NCH):
                if c >= 1:
                    sync.wait_ge(s_ae, c)
                sync.dma_start(out=gld[:], in_=G[c * P:(c + 1) * P, :]).then_inc(s_g, 16)

        @block.gpsimd
        def _(g):
            g.iota(identf[:], pattern=[[-1, P]], base=0, channel_multiplier=1,
                   allow_small_or_imprecise_dtypes=True).then_inc(s_id, 1)

        @block.scalar
        def _(s):
            s.wait_ge(s_id, 1)
            s.activation(identf[:], identf[:], AF.Square).then_inc(s_id, 1)

        @block.vector
        def _(v):
            v.wait_ge(s_id, 2)
            v.tensor_scalar(identf[:], identf[:], -1.0, 1.0, ALU.mult, ALU.add).then_inc(s_id, 1)
            v.wait_ge(s_id, 3)
            v.tensor_relu(identf[:], identf[:]).then_inc(s_id, 1)

        @block.scalar
        def _(s):
            s.wait_ge(s_id, 4)
            s.activation(identb[:], identf[:], AF.Copy).then_inc(s_id, 1)
            for i, cv in enumerate(coefs):
                s.wait_ge(s_id, 5 + i)
                s.activation(cid[:, i * P:(i + 1) * P], identf[:], AF.Copy,
                             scale=float(cv)).then_inc(s_id, 1)

        @block.scalar
        def _(s):
            for c in range(NCH):
                s.wait_ge(s_g, 16 * (c + 1))
                s.activation(pool[A][:, c * D:(c + 1) * D], gld[:], AF.Copy,
                             scale=0.25).then_inc(s_ae, 1)
        cnt["ae"] = 8

        @block.vector
        def _(v):
            v.wait_ge(s_id, 5)
            v.memset(pool[ID][:], 0.0).then_inc(s_av, 1)
            v.wait_ge(s_av, 1)
            for c in range(NCH):
                v.tensor_copy(pool[ID][:, c * D + c * P: c * D + (c + 1) * P], identb[:]).then_inc(s_av, 1)
        cnt["av"] = 9

        def product(t, Zi, Yi, OUTi, scale, extra_wait=None):
            pass

    prods = [
        ("A2", A, A, A2, -1.0, "copy"),
        ("A3", A2, A, A3, 1.0, "copy"),
        ("A4", A3, A, A4, -1.0, "copy"),
    ]

    with nc.Block() as block:
        @block.tensor
        def _(t):
            t.wait_ge(s_id, 24)
            t.wait_ge(s_ae, 8)
            t.wait_ge(s_av, 9)
            for name, Zi, Yi, OUTi, scale, mode in prods:
                t.wait_ge(s_ae, 8 + cnt["mm"])
                for mc in range(NCH):
                    for j2 in range(2):
                        for kc in range(NCH):
                            mm = t.matmul(ps_a[:, (mc % 2) * 1024 + j2 * 512:(mc % 2) * 1024 + j2 * 512 + 512],
                                          pool[Zi][:, kc * D + mc * P: kc * D + (mc + 1) * P],
                                          pool[Yi][:, kc * D + j2 * 512: kc * D + j2 * 512 + 512],
                                          start=(kc == 0), stop=(kc == NCH - 1))
                    cnt["mm"] += 1
                    mm.then_inc(s_mm, 1)
                    t.wait_ge(s_ae, 8 + cnt["mm"] - 1)
            bdefs = [
                (B0, [(C[0], ID), (C[1], A), (C[2], A2), (C[3], A3)]),
                (B1, [(C[4], ID), (C[5], A), (C[6], A2), (C[7], A3)]),
                (B2, [(C[8], ID), (C[9], A), (C[10], A2), (C[11], A3), (C[12], A4)]),
                (B0p, [(C[0], ID), (-C[1], A), (C[2], A2), (-C[3], A3)]),
                (B1p, [(C[4], ID), (-C[5], A), (C[6], A2), (-C[7], A3)]),
                (B2p, [(C[8], ID), (-C[9], A), (C[10], A2), (-C[11], A3), (C[12], A4)]),
            ]
            for Bi, terms in bdefs:
                t.wait_ge(s_ae, 8 + cnt["mm"])
                for mc in range(NCH):
                    for j2 in range(2):
                        for ti, (cv, Xi) in enumerate(terms):
                            mm = t.matmul(ps_a[:, (mc % 2) * 1024 + j2 * 512:(mc % 2) * 1024 + j2 * 512 + 512],
                                          cslot(cv),
                                          pool[Xi][:, mc * D + j2 * 512: mc * D + j2 * 512 + 512],
                                          start=(ti == 0), stop=(ti == len(terms) - 1))
                    cnt["mm"] += 1
                    mm.then_inc(s_mm, 1)
                    t.wait_ge(s_ae, 8 + cnt["mm"] - 1)
            for name, Zi, Yi in (("T1", A4, B2), ("T2", A4, B1),
                                 ("T1p", A4, B2p), ("T2p", A4, B1p),
                                 ("E1", ID, A), ("E1T", A, ID),
                                 ("E", A3, A2), ("ET", A2, A3)):
                t.wait_ge(s_ae, 8 + cnt["mm"])
                for mc in range(NCH):
                    for j2 in range(2):
                        for kc in range(NCH):
                            mm = t.matmul(ps_a[:, (mc % 2) * 1024 + j2 * 512:(mc % 2) * 1024 + j2 * 512 + 512],
                                          pool[Zi][:, kc * D + mc * P: kc * D + (mc + 1) * P],
                                          pool[Yi][:, kc * D + j2 * 512: kc * D + j2 * 512 + 512],
                                          start=(kc == 0), stop=(kc == NCH - 1))
                    cnt["mm"] += 1
                    mm.then_inc(s_mm, 1)
                    t.wait_ge(s_ae, 8 + cnt["mm"] - 1)

        @block.scalar
        def _(s):
            nev = 8
            ngrp = 0
            for name, Zi, Yi, OUTi, scale, mode in prods:
                for mc in range(NCH):
                    ngrp += 1
                    s.wait_ge(s_mm, ngrp)
                    s.activation(pool[OUTi][:, mc * D:(mc + 1) * D],
                                 ps_a[:, (mc % 2) * 1024:(mc % 2) * 1024 + 1024],
                                 AF.Copy, scale=scale).then_inc(s_ae, 1)
                    nev += 1
            for Bi in (B0, B1, B2, B0p, B1p, B2p):
                for mc in range(NCH):
                    ngrp += 1
                    s.wait_ge(s_mm, ngrp)
                    s.activation(pool[Bi][:, mc * D:(mc + 1) * D],
                                 ps_a[:, (mc % 2) * 1024:(mc % 2) * 1024 + 1024],
                                 AF.Copy).then_inc(s_ae, 1)
                    nev += 1

        @block.vector
        def _(v):
            ngrp = 72
            plan = [
                (B1, B1, "add"),
                (A, B0, "add"),
                (B1p, B1p, "add"),
                (ID, B0p, "add"),
                (A2, None, "copy"),
                (A3, None, "copy"),
                (None, None, "E"),
                (None, None, "ET"),
            ]
            for pi, (OUTi, ADDi, mode) in enumerate(plan):
                for mc in range(NCH):
                    ngrp += 1
                    v.wait_ge(s_mm, ngrp)
                    src_ps = ps_a[:, (mc % 2) * 1024:(mc % 2) * 1024 + 1024]
                    if mode == "add":
                        v.tensor_tensor(pool[OUTi][:, mc * D:(mc + 1) * D], src_ps,
                                        pool[ADDi][:, mc * D:(mc + 1) * D], ALU.add).then_inc(s_ae, 1)
                    elif mode == "copy":
                        v.tensor_copy(pool[OUTi][:, mc * D:(mc + 1) * D], src_ps).then_inc(s_ae, 1)
                    else:
                        v.tensor_copy(stage[:], src_ps).then_inc(s_ae, 1)
                        v.wait_ge(s_st, 16 * ((1 if mode == "ET" else 0) * 8 + mc + 1))

        @block.sync
        def _(sync):
            for i, dst in enumerate((o_E, o_ET)):
                for mc in range(NCH):
                    sync.wait_ge(s_ae, 8 + 72 + 48 + i * 8 + mc + 1)
                    sync.dma_start(out=dst[mc * P:(mc + 1) * P, :], in_=stage[:]).then_inc(s_st, 16)
            sync.wait_ge(s_st, 256)
    es.close()
    return nc



def build_s6():
    nc = bass.Bass("TRN2", target_bir_lowering=False)
    E0T = nc.dram_tensor("E0T", [D, D], F32, kind="ExternalInput").ap()
    E1_ = nc.dram_tensor("E1", [D, D], F32, kind="ExternalInput").ap()
    E2_ = nc.dram_tensor("E2", [D, D], F32, kind="ExternalInput").ap()
    E3_ = nc.dram_tensor("E3", [D, D], F32, kind="ExternalInput").ap()
    moe = nc.dram_tensor("moe", [D, T], F32, kind="ExternalInput").ap()
    abel = nc.dram_tensor("abel_b", [D, T], BF16, kind="ExternalInput").ap()
    xn = nc.dram_tensor("xn_f", [D, T], F32, kind="ExternalInput").ap()
    aw = nc.dram_tensor("attnwo", [D, T], F32, kind="ExternalInput").ap()
    xT = nc.dram_tensor("xT", [D, T], F32, kind="ExternalInput").ap()
    gcwt = nc.dram_tensor("gcwt", [P, 16], F32, kind="ExternalInput").ap()
    gcb = nc.dram_tensor("gcb", [1, 1], F32, kind="ExternalInput").ap()
    o_x1 = nc.dram_tensor("x1", [D, T], F32, kind="ExternalOutput").ap()

    es = ExitStack()
    sb = lambda n, s, d: es.enter_context(nc.sbuf_tensor(n, s, d))
    ta = sb("ta", [P, NCH * D], BF16)
    tb = sb("tb", [P, NCH * D], BF16)
    tc = sb("tc", [P, NCH * D], BF16)
    abl = [sb(f"abl{c}", [P, T], BF16) for c in range(NCH)]
    nab = [sb(f"nab{c}", [P, T], BF16) for c in range(NCH)]
    gbf = sb("gbf", [P, T], F32)
    gcw = sb("gcw", [P, 16], BF16)
    gcwf = sb("gcwf", [P, 16], F32)
    gcbs = sb("gcbs", [1, 1], F32)
    grow = sb("grow", [1, T], F32)
    ones1 = sb("ones1", [1, P], F32)
    ps_a = es.enter_context(nc.psum_tensor("ps_a", [P, T], F32))
    ps_b = es.enter_context(nc.psum_tensor("ps_b", [P, T], F32))
    names = ("s_la", "s_lb", "s_lc", "s_mo", "s_ab", "s_gc", "s_gc2", "s_mm", "s_ae",
             "s_av", "s_ms", "s_st", "s_st1") + tuple(f"s_s{i}" for i in range(6))
    sems = {n: es.enter_context(nc.semaphore(n)) for n in names}
    (s_la, s_lb, s_lc, s_mo, s_ab, s_gc, s_gc2, s_mm, s_ae, s_av, s_ms, s_st, s_st1) = \
        [sems[n] for n in names[:13]]
    s_str = [sems[f"s_s{i}"] for i in range(6)]

    mo_es = ExitStack()
    mo = [mo_es.enter_context(nc.sbuf_tensor(f"mo{c}", [P, T], BF16)) for c in range(NCH)]

    with nc.Block() as block:
        @block.gpsimd
        def _(g):
            for c in range(NCH):
                g.dma_start(out=ta[:, c * D:(c + 1) * D], in_=E0T[c * P:(c + 1) * P, :]).then_inc(s_la, 16)
                g.dma_start(out=tb[:, c * D:(c + 1) * D], in_=E1_[c * P:(c + 1) * P, :]).then_inc(s_lb, 16)
            g.wait_ge(s_mm, 8)
            for c in range(NCH):
                g.dma_start(out=ta[:, c * D:(c + 1) * D], in_=E2_[c * P:(c + 1) * P, :]).then_inc(s_la, 16)
            g.wait_ge(s_mm, 16)
            for c in range(NCH):
                g.dma_start(out=tc[:, c * D:(c + 1) * D], in_=E3_[c * P:(c + 1) * P, :]).then_inc(s_lc, 16)
            for c in range(NCH):
                g.dma_start(out=mo[c][:], in_=moe[c * P:(c + 1) * P, :]).then_inc(s_mo, 16)
            g.dma_start(out=gcw[:], in_=gcwt).then_inc(s_gc2, 16)

        @block.sync
        def _(sync):
            for c in range(NCH):
                sync.dma_start(out=abl[c][:], in_=abel[c * P:(c + 1) * P, :]).then_inc(s_ab, 16)
            sync.dma_start(out=gcwf[:], in_=gcwt).then_inc(s_gc, 16)
            sync.dma_start(out=gcbs[:], in_=gcb).then_inc(s_gc, 16)

        @block.vector
        def _(v):
            v.memset(ones1[:], 1.0).then_inc(s_ms, 1)

        @block.tensor
        def _(t):
            plist = [
                (tb, ta, s_lb, 128, s_la, 128),
                (ta, tc, s_la, 256, None, 0),
                (tb, tc, s_lc, 128, None, 0),
            ]
            ng = 0
            for pi, (lhs, rhs, w1, v1, w2, v2) in enumerate(plist):
                t.wait_ge(w1, v1)
                if w2 is not None:
                    t.wait_ge(w2, v2)
                t.wait_ge(s_ae, ng)
                for mc in range(NCH):
                    if ng >= 2:
                        t.wait_ge(s_ae, ng - 1)
                    for j2 in range(2):
                        for kc in range(NCH):
                            mm = t.matmul(ps_a[:, (mc % 2) * 1024 + j2 * 512:(mc % 2) * 1024 + j2 * 512 + 512],
                                          lhs[:, kc * D + mc * P: kc * D + (mc + 1) * P],
                                          rhs[:, kc * D + j2 * 512: kc * D + j2 * 512 + 512],
                                          start=(kc == 0), stop=(kc == NCH - 1))
                    ng += 1
                    mm.then_inc(s_mm, 1)
            t.wait_ge(s_mo, 128)
            t.wait_ge(s_ae, 24)
            for mc in range(NCH):
                if mc >= 1:
                    t.wait_ge(s_ae, 24 + mc)
                for j in range(4):
                    for kc in range(NCH):
                        mm = t.matmul(ps_a[:, (mc % 2) * 1024 + (j % 2) * 512:(mc % 2) * 1024 + (j % 2) * 512 + 512]
                                      if False else ps_a[:, SEGS[j]],
                                      ta[:, kc * D + mc * P: kc * D + (mc + 1) * P],
                                      mo[kc][:, SEGS[j]], start=(kc == 0), stop=(kc == NCH - 1))
                mm.then_inc(s_mm, 1)
            t.wait_ge(s_ae, 32)
            t.wait_ge(s_ab, 128)
            t.wait_ge(s_gc, 32)
            t.wait_ge(s_gc2, 16)
            for j in range(4):
                for kc in range(NCH):
                    t.matmul(ps_b[0:1, SEGS[j]], gcw[:, kc:kc + 1], abl[kc][:, SEGS[j]],
                             start=(kc == 0), stop=False)
                for kc in range(NCH):
                    mm = t.matmul(ps_b[0:1, SEGS[j]], gcw[:, 8 + kc:9 + kc], nab[kc][:, SEGS[j]],
                                  start=False, stop=(kc == NCH - 1))
                mm.then_inc(s_mm, 1)
            t.wait_ge(s_ae, 36)
            t.wait_ge(s_ms, 1)
            for j in range(4):
                mm = t.matmul(ps_b[:, SEGS[j]], ones1[0:1, :], grow[0:1, SEGS[j]],
                              start=True, stop=True)
            mm.then_inc(s_mm, 1)

        @block.scalar
        def _(s):
            targets = [tc, tb, ta]
            ng = 0
            for pi in range(3):
                for mc in range(NCH):
                    ng += 1
                    s.wait_ge(s_mm, ng)
                    s.activation(targets[pi][:, mc * D:(mc + 1) * D],
                                 ps_a[:, (mc % 2) * 1024:(mc % 2) * 1024 + 1024],
                                 AF.Copy).then_inc(s_ae, 1)
            for mc in range(NCH):
                s.wait_ge(s_mm, 24 + mc + 1)
                s.activation(nab[mc][:], ps_a[:], AF.Copy).then_inc(s_ae, 1)
            for j in range(4):
                s.wait_ge(s_mm, 32 + j + 1)
                s.activation(grow[0:1, SEGS[j]], ps_b[0:1, SEGS[j]], AF.Sigmoid,
                             bias=gcbs[0:1, 0:1]).then_inc(s_ae, 1)
            s.wait_ge(s_mm, 37)
            s.activation(gbf[:], ps_b[:], AF.Copy).then_inc(s_ae, 1)
    mo_es.close()

    xnp = [sb(f"xnp{i}", [P, T], F32) for i in range(2)]
    awp = [sb(f"awp{i}", [P, T], F32) for i in range(2)]
    xtp = [sb(f"xtp{i}", [P, T], F32) for i in range(2)]
    x1c = [sb(f"x1c{i}", [P, T], F32) for i in range(2)]

    with nc.Block() as block:
        @block.sync
        def _(sync):
            sync.wait_ge(s_mm, 32)
            for mc in range(NCH):
                if mc >= 2:
                    sync.wait_ge(s_av, 6 * (mc - 1))
                sync.dma_start(out=xnp[mc % 2][:], in_=xn[mc * P:(mc + 1) * P, :]).then_inc(s_str[mc % 2], 16)
                sync.dma_start(out=awp[mc % 2][:], in_=aw[mc * P:(mc + 1) * P, :]).then_inc(s_str[2 + mc % 2], 16)
                sync.dma_start(out=xtp[mc % 2][:], in_=xT[mc * P:(mc + 1) * P, :]).then_inc(s_str[4 + mc % 2], 16)

        @block.vector
        def _(v):
            for mc in range(NCH):
                b = 6 * mc
                for i in range(3):
                    v.wait_ge(s_str[2 * i + mc % 2], 16 * (mc // 2 + 1))
                v.wait_ge(s_ae, 37)
                v.wait_ge(s_ae, 24 + mc + 1)
                if mc >= 2:
                    v.wait_ge(s_st if mc % 2 == 0 else s_st1, 16 * ((mc - 2) // 2 + 1))
                v.tensor_tensor(x1c[mc % 2][:], nab[mc][:], abl[mc][:], ALU.subtract).then_inc(s_av, 1)
                v.wait_ge(s_av, b + 1)
                v.tensor_tensor(x1c[mc % 2][:], x1c[mc % 2][:], gbf[:], ALU.mult).then_inc(s_av, 1)
                v.wait_ge(s_av, b + 2)
                v.tensor_tensor(x1c[mc % 2][:], x1c[mc % 2][:], abl[mc][:], ALU.add).then_inc(s_av, 1)
                v.wait_ge(s_av, b + 3)
                v.tensor_tensor(x1c[mc % 2][:], x1c[mc % 2][:], xnp[mc % 2][:], ALU.add).then_inc(s_av, 1)
                v.wait_ge(s_av, b + 4)
                v.tensor_tensor(x1c[mc % 2][:], x1c[mc % 2][:], awp[mc % 2][:], ALU.add).then_inc(s_av, 1)
                v.wait_ge(s_av, b + 5)
                v.tensor_tensor(x1c[mc % 2][:], x1c[mc % 2][:], xtp[mc % 2][:], ALU.add).then_inc(s_av, 1)

        @block.scalar
        def _(s):
            for mc in range(NCH):
                s.wait_ge(s_av, 6 * (mc + 1))
                s.dma_start(out=o_x1[mc * P:(mc + 1) * P, :], in_=x1c[mc % 2][:]).then_inc(s_st if mc % 2 == 0 else s_st1, 16)
            s.wait_ge(s_st, 64)
            s.wait_ge(s_st1, 64)
    es.close()
    return nc



def build_s7a():
    nc = bass.Bass("TRN2", target_bir_lowering=False)
    x1 = nc.dram_tensor("x1", [D, T], F32, kind="ExternalInput").ap()
    wqkvT = nc.dram_tensor("wqkvT", [D, 3 * D], F32, kind="ExternalInput").ap()
    bqkv = nc.dram_tensor("bqkv", [P, 24], F32, kind="ExternalInput").ap()
    o_xn1 = nc.dram_tensor("xn1_b", [D, T], BF16, kind="ExternalOutput").ap()
    o_q = nc.dram_tensor("q1_b", [D, T], BF16, kind="ExternalOutput").ap()
    o_k = nc.dram_tensor("k1_b", [D, T], BF16, kind="ExternalOutput").ap()
    o_v = nc.dram_tensor("v1_b", [D, T], BF16, kind="ExternalOutput").ap()

    es = ExitStack()
    sb = lambda n, s, d: es.enter_context(nc.sbuf_tensor(n, s, d))
    xn1b = [sb(f"xn1b{c}", [P, T], BF16) for c in range(NCH)]
    wbuf = sb("wbuf", [P, NCH * 3 * D], BF16)
    biases = sb("biases", [P, 32], F32)
    onesP = sb("onesP", [P, P], F32)
    outq = [sb(f"outq{i}", [P, T], BF16) for i in range(2)]
    x2 = [sb(f"xp{i}", [P, T], F32) for i in range(2)]
    scr = sb("scr", [P, T], F32)
    xn32 = sb("xn32", [P, T], F32)
    mu = sb("mu", [P, T], F32)
    aux = sb("aux", [P, T], F32)
    mrs = sb("mrs", [P, T], F32)
    ps_x = es.enter_context(nc.psum_tensor("ps_x", [P, T], F32))
    ps_y = es.enter_context(nc.psum_tensor("ps_y", [P, T], F32))
    names = ("s_x0", "s_x1", "s_lw", "s_mm", "s_ev", "s_sq", "s_st", "s_so", "s_qv")
    sems = {n: es.enter_context(nc.semaphore(n)) for n in names}
    (s_x0, s_x1, s_lw, s_mm, s_ev, s_sq, s_st, s_so, s_qv) = [sems[n] for n in names]

    with nc.Block() as block:
        @block.sync
        def _(sync):
            for c in range(NCH):
                if c >= 2:
                    sync.wait_ge(s_mm, 2 * (c - 1))
                    sync.wait_ge(s_sq, c - 1)
                sync.dma_start(out=x2[c % 2][:], in_=x1[c * P:(c + 1) * P, :]).then_inc(s_x0 if c % 2 == 0 else s_x1, 16)

        @block.gpsimd
        def _(g):
            for kc in range(NCH):
                g.dma_start(out=wbuf[:, kc * 3 * D:(kc + 1) * 3 * D], in_=wqkvT[kc * P:(kc + 1) * P, :]).then_inc(s_lw, 16)
            g.dma_start(out=biases[:, 0:24], in_=bqkv).then_inc(s_lw, 16)

        @block.vector
        def _(v):
            v.memset(onesP[:], 1.0).then_inc(s_ev, 1)

        @block.scalar
        def _(s):
            for c in range(NCH):
                s.wait_ge(s_x0 if c % 2 == 0 else s_x1, 16 * (c // 2 + 1))
                if c >= 1:
                    s.wait_ge(s_mm, 2 * c)
                s.activation(scr[:], x2[c % 2][:], AF.Square).then_inc(s_sq, 1)

        @block.tensor
        def _(t):
            t.wait_ge(s_ev, 1)
            for c in range(NCH):
                t.wait_ge(s_x0 if c % 2 == 0 else s_x1, 16 * (c // 2 + 1))
                for j in range(4):
                    mm = t.matmul(ps_x[:, SEGS[j]], onesP[:], x2[c % 2][:, SEGS[j]],
                                  start=(c == 0), stop=(c == NCH - 1))
                mm.then_inc(s_mm, 1)
                t.wait_ge(s_sq, c + 1)
                for j in range(4):
                    mm = t.matmul(ps_y[:, SEGS[j]], onesP[:], scr[:, SEGS[j]],
                                  start=(c == 0), stop=(c == NCH - 1))
                mm.then_inc(s_mm, 1)

        @block.scalar
        def _(s):
            s.wait_ge(s_mm, 16)
            s.activation(mu[:], ps_x[:], AF.Copy, scale=1.0 / D).then_inc(s_ev, 1)
            s.activation(aux[:], ps_y[:], AF.Copy, scale=1.0 / D).then_inc(s_ev, 1)
            s.wait_ge(s_ev, 2)
            s.activation(scr[:], mu[:], AF.Square).then_inc(s_ev, 1)

        @block.vector
        def _(v):
            v.wait_ge(s_ev, 4)
            v.tensor_sub(aux[:], aux[:], scr[:]).then_inc(s_ev, 1)
            v.wait_ge(s_ev, 5)
            v.tensor_scalar_add(aux[:], aux[:], float(LN_EPS)).then_inc(s_ev, 1)

        @block.scalar
        def _(s):
            s.wait_ge(s_ev, 6)
            s.activation(aux[:], aux[:], AF.Ln).then_inc(s_ev, 1)
            s.wait_ge(s_ev, 7)
            s.activation(aux[:], aux[:], AF.Exp, scale=-0.5).then_inc(s_ev, 1)

        @block.vector
        def _(v):
            v.wait_ge(s_ev, 8)
            v.tensor_mul(mrs[:], mu[:], aux[:]).then_inc(s_ev, 1)

        @block.sync
        def _(sync):
            for c in range(NCH):
                if c < 2:
                    sync.wait_ge(s_sq, 7 + c)
                    sync.wait_ge(s_mm, 14 + 2 * c)
                else:
                    sync.wait_ge(s_ev, 9 + 2 * (c - 1))
                sync.dma_start(out=x2[c % 2][:], in_=x1[c * P:(c + 1) * P, :]).then_inc(s_x0 if c % 2 == 0 else s_x1, 16)

        @block.vector
        def _(v):
            for c in range(NCH):
                v.wait_ge(s_x0 if c % 2 == 0 else s_x1, 64 + 16 * (c // 2 + 1))
                if c >= 1:
                    v.wait_ge(s_ev, 9 + 2 * c)
                    v.wait_ge(s_sq, 8 + c)
                v.tensor_mul(xn32[:], x2[c % 2][:], aux[:]).then_inc(s_ev, 1)
                v.wait_ge(s_ev, 9 + 2 * c + 1)
                v.tensor_sub(xn32[:], xn32[:], mrs[:]).then_inc(s_ev, 1)

        @block.scalar
        def _(s):
            for c in range(NCH):
                s.wait_ge(s_ev, 9 + 2 * (c + 1))
                s.activation(xn1b[c][:], xn32[:], AF.Copy).then_inc(s_sq, 1)

        @block.gpsimd
        def _(g):
            for c in range(NCH):
                g.wait_ge(s_sq, 8 + (c + 1))
                g.dma_start(out=o_xn1[c * P:(c + 1) * P, :], in_=xn1b[c][:]).then_inc(s_st, 16)

        @block.tensor
        def _(t):
            t.wait_ge(s_lw, 144)
            t.wait_ge(s_sq, 16)
            for mc in range(24):
                ps = ps_x if mc % 2 == 0 else ps_y
                if mc >= 2:
                    t.wait_ge(s_qv, mc - 1)
                for j in range(4):
                    for kc in range(NCH):
                        mm = t.matmul(ps[:, SEGS[j]],
                                      wbuf[:, kc * 3 * D + mc * P: kc * 3 * D + (mc + 1) * P],
                                      xn1b[kc][:, SEGS[j]], start=(kc == 0), stop=(kc == NCH - 1))
                mm.then_inc(s_mm, 1)

        @block.vector
        def _(v):
            for mc in range(24):
                ps = ps_x if mc % 2 == 0 else ps_y
                v.wait_ge(s_mm, 16 + mc + 1)
                if mc >= 2:
                    v.wait_ge(s_so, 16 * mc)
                if mc < 8:
                    v.tensor_scalar(outq[mc % 2][:], ps[:], biases[:, mc:mc + 1], 0.125,
                                    ALU.add, ALU.mult).then_inc(s_qv, 1)
                else:
                    v.tensor_scalar(outq[mc % 2][:], ps[:], biases[:, mc:mc + 1], None,
                                    ALU.add).then_inc(s_qv, 1)

        @block.sync
        def _(sync):
            outs = [o_q, o_k, o_v]
            for mc in range(24):
                sync.wait_ge(s_qv, mc + 1)
                dst = outs[mc // 8]
                cc = mc % 8
                sync.dma_start(out=dst[cc * P:(cc + 1) * P, :], in_=outq[mc % 2][:]).then_inc(s_so, 16)
            sync.wait_ge(s_so, 384)
            sync.wait_ge(s_st, 128)
    es.close()
    return nc



def build_s7b():
    nc = bass.Bass("TRN2", target_bir_lowering=False)
    q_b = nc.dram_tensor("q1_b", [D, T], BF16, kind="ExternalInput").ap()
    k_b = nc.dram_tensor("k1_b", [D, T], BF16, kind="ExternalInput").ap()
    v_b = nc.dram_tensor("v1_b", [D, T], BF16, kind="ExternalInput").ap()
    woutT = nc.dram_tensor("woutT", [D, D], F32, kind="ExternalInput").ap()
    bout = nc.dram_tensor("bout", [P, 8], F32, kind="ExternalInput").ap()
    o_a1 = nc.dram_tensor("attn1", [D, T], F32, kind="ExternalOutput").ap()

    NH, HD, VW = 16, 64, 65
    es = ExitStack()
    sb = lambda n, s, d: es.enter_context(nc.sbuf_tensor(n, s, d))
    qf = [sb(f"qf{c}", [P, T], BF16) for c in range(NCH)]
    kf = [sb(f"kf{c}", [P, T], BF16) for c in range(NCH)]
    vT = sb("vT", [P, NH * 16 * VW], BF16)
    vsc = sb("vsc", [P, T], BF16)
    apre = [sb(f"apre{c}", [P, T], BF16) for c in range(NCH)]
    wbuf2 = sb("wbuf2", [P, NCH * D], BF16)
    biases = sb("biases", [P, 8], F32)
    ident = sb("ident", [P, P], F32)
    identb = sb("identb", [P, P], BF16)
    ones1 = sb("ones1", [1, 64], F32)
    PTs = sb("PTs", [P, 1920], BF16)
    PTg = sb("PTg", [P, 1920], BF16)
    PT0 = sb("PT0", [P, T], BF16)
    zrow = sb("zrow", [1, T], F32)
    rz = sb("rz", [1, T], F32)
    rzb = sb("rzb", [64, T], F32)
    outw = sb("outw", [P, T], F32)
    ps_x = es.enter_context(nc.psum_tensor("ps_x", [P, T], F32))
    ps_y = es.enter_context(nc.psum_tensor("ps_y", [P, T], F32))
    names = ("s_qk", "s_vs", "s_lw", "s_id", "s_ms", "s_tr", "s_trev", "s_sg",
             "s_pt", "s_ag", "s_zc", "s_rr", "s_zb", "s_hf", "s_mm", "s_ow", "s_st")
    sems = {n: es.enter_context(nc.semaphore(n)) for n in names}
    (s_qk, s_vs, s_lw, s_id, s_ms, s_tr, s_trev, s_sg, s_pt, s_ag, s_zc,
     s_rr, s_zb, s_hf, s_mm, s_ow, s_st) = [sems[n] for n in names]

    def hrows(buflist, h):
        return buflist[h // 2][(h % 2) * HD:(h % 2) * HD + HD, :]

    def vtile(h, kc):
        return vT[:, (h * 16 + kc) * VW: (h * 16 + kc) * VW + VW]

    with nc.Block() as block:
        @block.sync
        def _(sync):
            for c in range(NCH):
                sync.dma_start(out=qf[c][:], in_=q_b[c * P:(c + 1) * P, :]).then_inc(s_qk, 16)
                sync.dma_start(out=kf[c][:], in_=k_b[c * P:(c + 1) * P, :]).then_inc(s_qk, 16)

        @block.gpsimd
        def _(g):
            for kc in range(NCH):
                g.dma_start(out=wbuf2[:, kc * D:(kc + 1) * D], in_=woutT[kc * P:(kc + 1) * P, :]).then_inc(s_lw, 16)
            g.dma_start(out=biases[:], in_=bout).then_inc(s_lw, 16)
            g.iota(ident[:], pattern=[[-1, P]], base=0, channel_multiplier=1,
                   allow_small_or_imprecise_dtypes=True).then_inc(s_id, 1)
            for c in range(NCH):
                if c >= 1:
                    g.wait_ge(s_tr, 32 * c)
                g.dma_start(out=vsc[:], in_=v_b[c * P:(c + 1) * P, :]).then_inc(s_vs, 16)

        @block.scalar
        def _(s):
            s.wait_ge(s_id, 1)
            s.activation(ident[:], ident[:], AF.Square).then_inc(s_id, 1)

        @block.vector
        def _(v):
            v.memset(ones1[:], 1.0)
            v.memset(vT[:].rearrange("p (t c) -> p t c", c=VW)[:, :, 64:65], 1.0).then_inc(s_ms, 1)
            v.wait_ge(s_id, 2)
            v.tensor_scalar(ident[:], ident[:], -1.0, 1.0, ALU.mult, ALU.add).then_inc(s_id, 1)
            v.wait_ge(s_id, 3)
            v.tensor_relu(ident[:], ident[:]).then_inc(s_id, 1)

        @block.scalar
        def _(s):
            s.wait_ge(s_id, 4)
            s.activation(identb[:], ident[:], AF.Copy).then_inc(s_id, 1)

        @block.tensor
        def _(t):
            t.wait_ge(s_id, 5)
            t.wait_ge(s_ms, 1)
            n_tr = 0
            for c in range(NCH):
                t.wait_ge(s_vs, 16 * (c + 1))
                for half in range(2):
                    for kcg in range(2):
                        for k8 in range(8):
                            kc = kcg * 8 + k8
                            tr = t.transpose(ps_x[:, 0:256].bitcast(BF16)[:, k8 * 64:(k8 + 1) * 64],
                                             vsc[half * HD:(half + 1) * HD, kc * P:(kc + 1) * P],
                                             identb[half * HD:(half + 1) * HD, half * HD:(half + 1) * HD])
                            n_tr += 1
                        tr.then_inc(s_tr, 8)
                        t.wait_ge(s_trev, (n_tr // 8))

        @block.vector
        def _(v):
            for ng in range(32):
                v.wait_ge(s_tr, (ng + 1) * 8)
                c8, rem = divmod(ng, 4)
                half, kcg = divmod(rem, 2)
                h = 2 * c8 + half
                srcv = ps_x[:, 0:256].bitcast(BF16).rearrange("p (k e) -> p k e", e=64)
                dstv = vT[:, (h * 16 + kcg * 8) * VW: (h * 16 + kcg * 8 + 8) * VW] \
                    .rearrange("p (k e) -> p k e", e=VW)[:, :, 0:64]
                v.tensor_copy(dstv, srcv).then_inc(s_trev, 1)

        @block.tensor
        def _(t):
            t.wait_ge(s_qk, 256)
            t.wait_ge(s_trev, 32)
            for h in range(NH):
                if h >= 1:
                    t.wait_ge(s_hf, 2 * h)
                    t.wait_ge(s_zc, 4 * h)
                for qb in range(1, 16):
                    mm = t.matmul(ps_x[:, (qb - 1) * P: qb * P],
                                  hrows(kf, h)[:, qb * P:(qb + 1) * P],
                                  hrows(qf, h)[:, qb * P:(qb + 1) * P],
                                  start=True, stop=True)
                mm.then_inc(s_sg, 1)
                for lo, width in ((0, 512), (512, 512), (1024, 512), (1536, 384)):
                    mm = t.matmul(ps_y[:, lo:lo + width],
                                  hrows(kf, h)[:, 0:P],
                                  hrows(qf, h)[:, P + lo: P + lo + width],
                                  start=True, stop=True)
                mm.then_inc(s_sg, 1)
                t.wait_ge(s_pt, 3 * h + 2)
                for qb in range(1, 16):
                    t.matmul(ps_x[0:VW, (qb - 1) * P: qb * P], vtile(h, qb),
                             PTs[:, (qb - 1) * P: qb * P], start=True, stop=False)
                    mm = t.matmul(ps_x[0:VW, (qb - 1) * P: qb * P], vtile(h, 0),
                                  PTg[:, (qb - 1) * P: qb * P], start=False, stop=True)
                mm.then_inc(s_ag, 1)
                for kb in range(16):
                    mm = t.matmul(ps_y[:, kb * P:(kb + 1) * P],
                                  hrows(kf, h)[:, kb * P:(kb + 1) * P],
                                  hrows(qf, h)[:, 0:P], start=True, stop=True)
                mm.then_inc(s_sg, 1)
                t.wait_ge(s_pt, 3 * h + 3)
                for kb in range(16):
                    mm = t.matmul(ps_x[0:VW, 1920:2048], vtile(h, kb),
                                  PT0[:, kb * P:(kb + 1) * P], start=(kb == 0), stop=(kb == 15))
                mm.then_inc(s_ag, 1)
                t.wait_ge(s_rr, h + 1)
                for j in range(4):
                    mm = t.matmul(ps_y[0:64, SEGS[j]], ones1[:], rz[0:1, SEGS[j]],
                                  start=True, stop=True)
                mm.then_inc(s_zb, 1)

        @block.scalar
        def _(s):
            for h in range(NH):
                s.wait_ge(s_sg, 3 * h + 1)
                if h >= 1:
                    s.wait_ge(s_ag, 2 * h)
                s.activation(PTs[:], ps_x[:, 0:1920], AF.Exp).then_inc(s_pt, 1)
                s.wait_ge(s_sg, 3 * h + 2)
                s.activation(PTg[:], ps_y[:, 0:1920], AF.Exp).then_inc(s_pt, 1)
                s.wait_ge(s_sg, 3 * h + 3)
                s.activation(PT0[:], ps_y[:], AF.Exp).then_inc(s_pt, 1)
                s.wait_ge(s_ag, 2 * h + 2)
                if h >= 1:
                    s.wait_ge(s_rr, h)
                s.activation(zrow[0:1, P:T], ps_x[64:65, 0:1920], AF.Copy).then_inc(s_zc, 1)
                s.activation(zrow[0:1, 0:P], ps_x[64:65, 1920:2048], AF.Copy).then_inc(s_zc, 1)
                s.wait_ge(s_zb, h + 1)
                if h >= 1:
                    s.wait_ge(s_hf, 2 * h)
                s.activation(rzb[:, 0:1024], ps_y[0:64, 0:1024], AF.Copy).then_inc(s_zc, 1)
                s.activation(rzb[:, 1024:2048], ps_y[0:64, 1024:2048], AF.Copy).then_inc(s_zc, 1)

        @block.vector
        def _(v):
            for h in range(NH):
                v.wait_ge(s_zc, 4 * h + 2)
                v.reciprocal(rz[:], zrow[:]).then_inc(s_rr, 1)
                v.wait_ge(s_zc, 4 * h + 4)
                v.tensor_tensor(hrows(apre, h)[:, P:T], ps_x[0:64, 0:1920], rzb[:, P:T], ALU.mult).then_inc(s_hf, 1)
                v.wait_ge(s_hf, 2 * h + 1)
                v.tensor_tensor(hrows(apre, h)[:, 0:P], ps_x[0:64, 1920:2048], rzb[:, 0:P], ALU.mult).then_inc(s_hf, 1)

        @block.tensor
        def _(t):
            t.wait_ge(s_hf, 32)
            t.wait_ge(s_lw, 144)
            for mc in range(NCH):
                ps = ps_x if mc % 2 == 0 else ps_y
                if mc >= 2:
                    t.wait_ge(s_ow, mc - 1)
                for j in range(4):
                    for kc in range(NCH):
                        mm = t.matmul(ps[:, SEGS[j]],
                                      wbuf2[:, kc * D + mc * P: kc * D + (mc + 1) * P],
                                      apre[kc][:, SEGS[j]], start=(kc == 0), stop=(kc == NCH - 1))
                mm.then_inc(s_mm, 1)

        @block.vector
        def _(v):
            for mc in range(NCH):
                ps = ps_x if mc % 2 == 0 else ps_y
                v.wait_ge(s_mm, mc + 1)
                if mc >= 1:
                    v.wait_ge(s_st, 16 * mc)
                v.tensor_scalar(outw[:], ps[:], biases[:, mc:mc + 1], None,
                                ALU.add).then_inc(s_ow, 1)

        @block.sync
        def _(sync):
            for mc in range(NCH):
                sync.wait_ge(s_ow, mc + 1)
                sync.dma_start(out=o_a1[mc * P:(mc + 1) * P, :], in_=outw[:]).then_inc(s_st, 16)
            sync.wait_ge(s_st, 128)
    es.close()
    return nc



def build_s8():
    nc = bass.Bass("TRN2", target_bir_lowering=False)
    xn1 = nc.dram_tensor("xn1_b", [D, T], BF16, kind="ExternalInput").ap()
    Wq = nc.dram_tensor("Wq", [D, 2 * D], F32, kind="ExternalInput").ap()
    bq = nc.dram_tensor("bq", [P, 16], F32, kind="ExternalInput").ap()
    edT = nc.dram_tensor("edT", [D, D], F32, kind="ExternalInput").ap()
    o_q = nc.dram_tensor("q1f_b", [2 * D, T], BF16, kind="ExternalOutput").ap()
    o_G = nc.dram_tensor("G_b", [T, D], BF16, kind="ExternalOutput").ap()
    o_stats = nc.dram_tensor("stats", [P, 4], F32, kind="ExternalOutput").ap()

    es = ExitStack()
    sb = lambda n, s, d: es.enter_context(nc.sbuf_tensor(n, s, d))
    xnb = [sb(f"xnb{c}", [P, T], BF16) for c in range(NCH)]
    wqb = sb("wqb", [P, NCH * 2 * D], BF16)
    edb = sb("edb", [P, NCH * D], BF16)
    bqs = sb("bqs", [P, 16], F32)
    outq = [sb(f"outq{i}", [P, T], BF16) for i in range(2)]
    scrq = sb("scrq", [P, T], BF16)
    s1 = sb("s1", [P, 16], F32)
    s2 = sb("s2", [P, 16], F32)
    par = sb("par", [P, 4], F32)
    outg = [sb(f"outg{i}", [P, D], BF16) for i in range(2)]
    ps_x = es.enter_context(nc.psum_tensor("ps_x", [P, T], F32))
    ps_y = es.enter_context(nc.psum_tensor("ps_y", [P, T], F32))
    names = ("s_xn", "s_lw", "s_mm", "s_qv", "s_sq", "s_st", "s_gm", "s_ge", "s_gs")
    sems = {n: es.enter_context(nc.semaphore(n)) for n in names}
    (s_xn, s_lw, s_mm, s_qv, s_sq, s_st, s_gm, s_ge, s_gs) = [sems[n] for n in names]

    with nc.Block() as block:
        @block.sync
        def _(sync):
            for c in range(NCH):
                sync.dma_start(out=xnb[c][:], in_=xn1[c * P:(c + 1) * P, :]).then_inc(s_xn, 16)

        @block.gpsimd
        def _(g):
            for kc in range(NCH):
                g.dma_start(out=wqb[:, kc * 2 * D:(kc + 1) * 2 * D], in_=Wq[kc * P:(kc + 1) * P, :]).then_inc(s_lw, 16)
            for kc in range(NCH):
                g.dma_start(out=edb[:, kc * D:(kc + 1) * D], in_=edT[kc * P:(kc + 1) * P, :]).then_inc(s_lw, 16)
            g.dma_start(out=bqs[:], in_=bq).then_inc(s_lw, 16)

        @block.tensor
        def _(t):
            t.wait_ge(s_lw, 272)
            t.wait_ge(s_xn, 128)
            for mc in range(16):
                ps = ps_x if mc % 2 == 0 else ps_y
                if mc >= 2:
                    t.wait_ge(s_sq, mc - 1)
                for j in range(4):
                    for kc in range(NCH):
                        mm = t.matmul(ps[:, SEGS[j]],
                                      wqb[:, kc * 2 * D + mc * P: kc * 2 * D + (mc + 1) * P],
                                      xnb[kc][:, SEGS[j]], start=(kc == 0), stop=(kc == NCH - 1))
                mm.then_inc(s_mm, 1)
            for tt in range(16):
                ps = ps_x if tt % 2 == 0 else ps_y
                if tt == 0:
                    t.wait_ge(s_sq, 16)
                if tt >= 2:
                    t.wait_ge(s_ge, tt - 1)
                for half in range(2):
                    for kc in range(NCH):
                        mm = t.matmul(ps[:, half * 512:(half + 1) * 512],
                                      xnb[kc][:, tt * P:(tt + 1) * P],
                                      edb[:, kc * D + half * 512: kc * D + half * 512 + 512],
                                      start=(kc == 0), stop=(kc == NCH - 1))
                mm.then_inc(s_gm, 1)

        @block.vector
        def _(v):
            for mc in range(16):
                ps = ps_x if mc % 2 == 0 else ps_y
                v.wait_ge(s_mm, mc + 1)
                if mc >= 2:
                    v.wait_ge(s_st, 16 * mc)
                v.tensor_scalar(outq[mc % 2][:], ps[:], bqs[:, mc:mc + 1], 0.0,
                                ALU.add, ALU.add, accum_out=s1[:, mc:mc + 1]).then_inc(s_qv, 1)
            v.wait_ge(s_sq, 16)
            v.wait_ge(s_qv, 16)
            v.tensor_reduce(par[:, 0:1], s1[:].rearrange("p (c two) -> p two c", two=2)[:, 0:1, :].squeeze(1), AX, ALU.add)
            v.tensor_reduce(par[:, 1:2], s1[:].rearrange("p (c two) -> p two c", two=2)[:, 1:2, :].squeeze(1), AX, ALU.add)
            v.tensor_reduce(par[:, 2:3], s2[:].rearrange("p (c two) -> p two c", two=2)[:, 0:1, :].squeeze(1), AX, ALU.add)
            v.tensor_reduce(par[:, 3:4], s2[:].rearrange("p (c two) -> p two c", two=2)[:, 1:2, :].squeeze(1), AX, ALU.add).then_inc(s_qv, 1)

        @block.scalar
        def _(s):
            for mc in range(16):
                ps = ps_x if mc % 2 == 0 else ps_y
                s.wait_ge(s_qv, mc + 1)
                if mc >= 1:
                    s.wait_ge(s_sq, mc)
                s.activation(scrq[:], ps[:], AF.Square, bias=bqs[:, mc:mc + 1],
                             accum_out=s2[:, mc:mc + 1]).then_inc(s_sq, 1)
            for tt in range(16):
                ps = ps_x if tt % 2 == 0 else ps_y
                s.wait_ge(s_gm, tt + 1)
                if tt >= 2:
                    s.wait_ge(s_gs, 16 * tt)
                s.activation(outg[tt % 2][:], ps[:, 0:D], AF.Gelu).then_inc(s_ge, 1)

        @block.sync
        def _(sync):
            for mc in range(16):
                sync.wait_ge(s_qv, mc + 1)
                sync.dma_start(out=o_q[mc * P:(mc + 1) * P, :], in_=outq[mc % 2][:]).then_inc(s_st, 16)
            sync.wait_ge(s_qv, 17)
            sync.dma_start(out=o_stats, in_=par[:]).then_inc(s_st, 16)

        @block.gpsimd
        def _(g):
            for tt in range(16):
                g.wait_ge(s_ge, tt + 1)
                g.dma_start(out=o_G[tt * P:(tt + 1) * P, :], in_=outg[tt % 2][:]).then_inc(s_gs, 16)
            g.wait_ge(s_gs, 256)

        @block.sync
        def _(sync):
            sync.wait_ge(s_st, 272)
    es.close()
    return nc



def build_s9():
    nc = bass.Bass("TRN2", target_bir_lowering=False)
    q_in = nc.dram_tensor("q1f_b", [2 * D, T], BF16, kind="ExternalInput").ap()
    G_in = nc.dram_tensor("G_b", [T, D], BF16, kind="ExternalInput").ap()
    statsall = nc.dram_tensor("statsall", [P, 32], F32, kind="ExternalInput").ap()
    bng = nc.dram_tensor("bng", [P, 2], F32, kind="ExternalInput").ap()
    bnb = nc.dram_tensor("bnb", [P, 2], F32, kind="ExternalInput").ap()
    sk0 = nc.dram_tensor("sk0", [32, P], F32, kind="ExternalInput").ap()
    sk1 = nc.dram_tensor("sk1", [32, P], F32, kind="ExternalInput").ap()
    eu = nc.dram_tensor("eu", [D, D], F32, kind="ExternalInput").ap()
    x1 = nc.dram_tensor("x1", [D, T], F32, kind="ExternalInput").ap()
    a1 = nc.dram_tensor("attn1", [D, T], F32, kind="ExternalInput").ap()
    o_x2 = nc.dram_tensor("x2", [D, T], F32, kind="ExternalOutput").ap()

    NBN = 8 * T * 8
    es = ExitStack()
    sb = lambda n, s, d: es.enter_context(nc.sbuf_tensor(n, s, d))
    q = [sb(f"q{c}", [P, T], BF16) for c in range(16)]
    gT = sb("gT", [P, NCH * T], BF16)
    eub = sb("eub", [P, NCH * D], BF16)
    Gt = [sb(f"Gt{i}", [P, D], BF16) for i in range(2)]
    gt = [sb(f"gt{i}", [P, D], BF16) for i in range(2)]
    Pacc = sb("Pacc", [P, D], F32)
    obuf = sb("obuf", [P, D], F32)
    s01 = sb("s01", [P, 512], F32)
    m8t = sb("m8t", [P, P], F32)
    thr = sb("thr", [P, 512], F32)
    msk = sb("msk", [P, 512], F32)
    ee = sb("ee", [P, 512], F32)
    aa = sb("aa", [P, 512], F32)
    zz = sb("zz", [P, 16], F32)
    zp = sb("zp", [P, 8], F32)
    rzp = sb("rzp", [P, 8], F32)
    scr = sb("scr", [P, T], F32)
    rb = sb("rb", [P, T], F32)
    fin = sb("fin", [P, 4], F32)
    sts = sb("sts", [P, 32], F32)
    msc = sb("msc", [P, 2], F32)
    esc = sb("esc", [P, 2], F32)
    m2c = sb("m2c", [P, 2], F32)
    vrc = sb("vrc", [P, 2], F32)
    bngs = sb("bngs", [P, 2], F32)
    bnbs = sb("bnbs", [P, 2], F32)
    scl = sb("scl", [P, 2], F32)
    shf = sb("shf", [P, 2], F32)
    tmp2 = sb("tmp2", [P, 2], F32)
    sk0s = sb("sk0s", [32, P], F32)
    sk1s = sb("sk1s", [32, P], F32)
    kscr = sb("kscr", [32, P], F32)
    kn2 = sb("kn2", [32, 2], F32)
    k0b = sb("k0b", [32, P], BF16)
    k1b = sb("k1b", [32, P], BF16)
    k0nT = sb("k0nT", [P, 32], BF16)
    k1nT = sb("k1nT", [P, 32], BF16)
    ident = sb("ident", [P, P], F32)
    identb = sb("identb", [P, P], BF16)
    onesP = sb("onesP", [P, P], F32)
    x1p = [sb(f"x1p{i}", [P, T], F32) for i in range(2)]
    a1p = [sb(f"a1p{i}", [P, T], F32) for i in range(2)]
    x2c = [sb(f"x2c{i}", [P, T], F32) for i in range(2)]
    ps_x = es.enter_context(nc.psum_tensor("ps_x", [P, T], F32))
    ps_y = es.enter_context(nc.psum_tensor("ps_y", [P, T], F32))
    names = ("s_in", "s_q", "s_eu", "s_g0", "s_g1", "s_fs", "s_kk", "s_id",
             "s_cd", "s_ca", "s_cm", "s_td", "s_ta", "s_tm", "s_om", "s_ov",
             "s_so", "s_so1", "s_sx0", "s_sx1", "s_sa0", "s_sa1", "s_ms")
    sems = {n: es.enter_context(nc.semaphore(n)) for n in names}
    (s_in, s_q, s_eu, s_g0, s_g1, s_fs, s_kk, s_id, s_cd, s_ca, s_cm, s_td,
     s_ta, s_tm, s_om, s_ov, s_so, s_so1, s_sx0, s_sx1, s_sa0, s_sa1, s_ms) = \
        [sems[n] for n in names]

    with nc.Block() as block:
        @block.sync
        def _(sync):
            sync.dma_start(out=sts[:], in_=statsall).then_inc(s_in, 16)
            sync.dma_start(out=bngs[:], in_=bng).then_inc(s_in, 16)
            sync.dma_start(out=bnbs[:], in_=bnb).then_inc(s_in, 16)
            sync.dma_start(out=sk0s[:], in_=sk0).then_inc(s_in, 16)
            sync.dma_start(out=sk1s[:], in_=sk1).then_inc(s_in, 16)
            for c in range(16):
                sync.dma_start(out=q[c][:], in_=q_in[c * P:(c + 1) * P, :]).then_inc(s_q, 16)

        @block.gpsimd
        def _(g):
            for kc in range(NCH):
                g.dma_start(out=eub[:, kc * D:(kc + 1) * D], in_=eu[kc * P:(kc + 1) * P, :]).then_inc(s_eu, 16)
            g.iota(ident[:], pattern=[[-1, P]], base=0, channel_multiplier=1,
                   allow_small_or_imprecise_dtypes=True).then_inc(s_id, 1)
            for tt in range(16):
                if tt >= 2:
                    g.wait_ge(s_td, 48 * (tt - 2) + 47)
                g.dma_start(out=Gt[tt % 2][:], in_=G_in[tt * P:(tt + 1) * P, :]).then_inc(s_g0 if tt % 2 == 0 else s_g1, 16)

        @block.vector
        def _(v):
            v.memset(onesP[:], 1.0).then_inc(s_ms, 1)
            v.wait_ge(s_in, 80)
            for j in range(4):
                v.tensor_reduce(fin[:, j:j + 1],
                                sts[:].rearrange("p (c f) -> p f c", f=4)[:, j:j + 1, :].squeeze(1),
                                AX, ALU.add).then_inc(s_fs, 1)
            v.wait_ge(s_fs, 4)
            v.tensor_scalar(msc[:], fin[:, 0:2], 1.0 / NBN, None, ALU.mult).then_inc(s_fs, 1)
            v.tensor_scalar(esc[:], fin[:, 2:4], 1.0 / NBN, None, ALU.mult).then_inc(s_fs, 1)

        @block.scalar
        def _(s):
            s.wait_ge(s_fs, 6)
            s.activation(m2c[:], msc[:], AF.Square).then_inc(s_fs, 1)

        @block.vector
        def _(v):
            v.wait_ge(s_fs, 7)
            v.tensor_sub(vrc[:], esc[:], m2c[:]).then_inc(s_fs, 1)
            v.wait_ge(s_fs, 8)
            v.tensor_scalar_add(vrc[:], vrc[:], 1e-5).then_inc(s_fs, 1)

        @block.scalar
        def _(s):
            s.wait_ge(s_fs, 9)
            s.activation(vrc[:], vrc[:], AF.Ln).then_inc(s_fs, 1)
            s.wait_ge(s_fs, 10)
            s.activation(vrc[:], vrc[:], AF.Exp, scale=-0.5).then_inc(s_fs, 1)

        @block.vector
        def _(v):
            v.wait_ge(s_fs, 11)
            v.tensor_mul(scl[:], bngs[:], vrc[:]).then_inc(s_fs, 1)
            v.wait_ge(s_fs, 12)
            v.tensor_mul(tmp2[:], msc[:], scl[:]).then_inc(s_fs, 1)
            v.wait_ge(s_fs, 13)
            v.tensor_sub(shf[:], bnbs[:], tmp2[:]).then_inc(s_fs, 1)

        @block.scalar
        def _(s):
            s.wait_ge(s_id, 1)
            s.activation(ident[:], ident[:], AF.Square).then_inc(s_id, 1)

        @block.vector
        def _(v):
            v.wait_ge(s_id, 2)
            v.tensor_scalar(ident[:], ident[:], -1.0, 1.0, ALU.mult, ALU.add).then_inc(s_id, 1)
            v.wait_ge(s_id, 3)
            v.tensor_relu(ident[:], ident[:]).then_inc(s_id, 1)

        @block.scalar
        def _(s):
            s.wait_ge(s_id, 4)
            s.activation(identb[:], ident[:], AF.Copy).then_inc(s_id, 1)
            s.wait_ge(s_in, 80)
            s.activation(kscr[:], sk0s[:], AF.Square).then_inc(s_kk, 1)

        @block.vector
        def _(v):
            v.wait_ge(s_kk, 1)
            v.tensor_reduce(kn2[:, 0:1], kscr[:], AX, ALU.add).then_inc(s_kk, 1)

        @block.scalar
        def _(s):
            s.wait_ge(s_kk, 2)
            s.activation(kscr[:], sk1s[:], AF.Square).then_inc(s_kk, 1)

        @block.vector
        def _(v):
            v.wait_ge(s_kk, 3)
            v.tensor_reduce(kn2[:, 1:2], kscr[:], AX, ALU.add).then_inc(s_kk, 1)

        @block.scalar
        def _(s):
            s.wait_ge(s_kk, 4)
            s.activation(kn2[:], kn2[:], AF.Ln).then_inc(s_kk, 1)
            s.wait_ge(s_kk, 5)
            s.activation(kn2[:], kn2[:], AF.Exp, scale=-0.5).then_inc(s_kk, 1)

        @block.vector
        def _(v):
            v.wait_ge(s_kk, 6)
            v.tensor_scalar(kscr[:], sk0s[:], kn2[:, 0:1], None, ALU.mult).then_inc(s_kk, 1)

        @block.scalar
        def _(s):
            s.wait_ge(s_kk, 7)
            s.activation(k0b[:], kscr[:], AF.Copy).then_inc(s_kk, 1)

        @block.vector
        def _(v):
            v.wait_ge(s_kk, 8)
            v.tensor_scalar(kscr[:], sk1s[:], kn2[:, 1:2], None, ALU.mult).then_inc(s_kk, 1)

        @block.scalar
        def _(s):
            s.wait_ge(s_kk, 9)
            s.activation(k1b[:], kscr[:], AF.Copy).then_inc(s_kk, 1)

        @block.tensor
        def _(t):
            t.wait_ge(s_kk, 10)
            t.wait_ge(s_id, 5)
            tr = t.transpose(ps_x[:, 0:16].bitcast(BF16)[:, 0:32], k0b[0:32, :], identb[0:32, 0:32])
            tr.then_inc(s_kk, 1)

        @block.vector
        def _(v):
            v.wait_ge(s_kk, 11)
            v.tensor_copy(k0nT[:], ps_x[:, 0:16].bitcast(BF16)[:, 0:32]).then_inc(s_kk, 1)

        @block.tensor
        def _(t):
            t.wait_ge(s_kk, 12)
            tr = t.transpose(ps_x[:, 0:16].bitcast(BF16)[:, 0:32], k1b[0:32, :], identb[0:32, 0:32])
            tr.then_inc(s_kk, 1)

        @block.vector
        def _(v):
            v.wait_ge(s_kk, 13)
            v.tensor_copy(k1nT[:], ps_x[:, 0:16].bitcast(BF16)[:, 0:32]).then_inc(s_kk, 1)

        @block.vector
        def _(v):
            v.wait_ge(s_q, 256)
            v.wait_ge(s_fs, 14)
            for mc in range(16):
                pr = mc % 2
                if mc >= 1:
                    v.wait_ge(s_cd, 2 * mc)
                v.tensor_scalar(q[mc][:], q[mc][:], scl[:, pr:pr + 1],
                                shf[:, pr:pr + 1], ALU.mult, ALU.add).then_inc(s_cd, 1)
                v.wait_ge(s_ca, 3 * mc + 3)
                v.tensor_mul(q[mc][:], q[mc][:], rb[:]).then_inc(s_cd, 1)

        @block.scalar
        def _(s):
            for mc in range(16):
                s.wait_ge(s_cd, 2 * mc + 1)
                if mc >= 1:
                    s.wait_ge(s_cm, mc)
                s.activation(scr[:], q[mc][:], AF.Square).then_inc(s_ca, 1)
                s.wait_ge(s_cm, mc + 1)
                if mc >= 1:
                    s.wait_ge(s_cd, 2 * mc)
                s.activation(rb[:], ps_y[:], AF.Ln).then_inc(s_ca, 1)
                s.wait_ge(s_ca, 3 * mc + 2)
                s.activation(rb[:], rb[:], AF.Exp, scale=-0.5).then_inc(s_ca, 1)

        @block.tensor
        def _(t):
            t.wait_ge(s_ms, 1)
            for mc in range(16):
                t.wait_ge(s_ca, 3 * mc + 1)
                if mc >= 1:
                    t.wait_ge(s_ca, 3 * mc)
                for j in range(4):
                    mm = t.matmul(ps_y[:, SEGS[j]], onesP[:], scr[:, SEGS[j]],
                                  start=True, stop=True)
                mm.then_inc(s_cm, 1)

        @block.tensor
        def _(t):
            t.wait_ge(s_cd, 32)
            t.wait_ge(s_kk, 14)
            for tt in range(16):
                pp = (tt % 2) * 512
                if tt >= 2:
                    t.wait_ge(s_td, 48 * (tt - 2) + 1)
                for h in range(8):
                    t.matmul(ps_x[:, pp + h * 64: pp + h * 64 + 32],
                             q[2 * h][:, tt * P:(tt + 1) * P], k0nT[:],
                             start=True, stop=True)
                    mm = t.matmul(ps_x[:, pp + h * 64 + 32: pp + h * 64 + 64],
                                  q[2 * h + 1][:, tt * P:(tt + 1) * P], k1nT[:],
                                  start=True, stop=True)
                mm.then_inc(s_tm, 1)
                t.wait_ge(s_td, 48 * tt + 47)
                for ec in range(NCH):
                    tr = t.transpose(ps_x[:, 1024:1536].bitcast(BF16)[:, ec * P:(ec + 1) * P],
                                     gt[tt % 2][:, ec * P:(ec + 1) * P], identb[:])
                tr.then_inc(s_tm, 1)

        @block.scalar
        def _(s):
            for tt in range(16):
                s.wait_ge(s_td, 48 * tt + 1)
                if tt >= 1:
                    s.wait_ge(s_td, 48 * (tt - 1) + 20)
                s.activation(ee[:], s01[:], AF.Exp).then_inc(s_ta, 1)

        @block.vector
        def _(v):
            ntd = 0

            def inc(op):
                nonlocal ntd
                op.then_inc(s_td, 1)
                ntd += 1
            for tt in range(16):
                pp = (tt % 2) * 512
                base = 48 * tt
                assert ntd == base, (tt, ntd, base)
                v.wait_ge(s_tm, 2 * tt + 1)
                if tt >= 1:
                    v.wait_ge(s_ta, tt)
                inc(v.tensor_copy(s01[:], ps_x[:, pp:pp + 512]))
                v.wait_ge(s_td, base + 1)
                for sseg in range(16):
                    inc(v.max(out=m8t[:, sseg * 8:(sseg + 1) * 8], in_=s01[:, sseg * 32:(sseg + 1) * 32]))
                v.wait_ge(s_td, base + 17)
                inc(v.tensor_copy(thr[:].rearrange("p (s k) -> p s k", k=32),
                                  m8t[:].rearrange("p (s e) -> p s e", e=8)[:, :, 3:4].broadcast_to([P, 16, 32])))
                v.wait_ge(s_td, base + 18)
                inc(v.tensor_tensor(msk[:], s01[:], thr[:], ALU.is_ge))
                v.wait_ge(s_ta, tt + 1)
                v.wait_ge(s_td, base + 19)
                inc(v.tensor_tensor(aa[:], ee[:], msk[:], ALU.mult))
                v.wait_ge(s_td, base + 20)
                inc(v.tensor_reduce(zz[:], aa[:].rearrange("p (s k) -> p s k", k=32), AX, ALU.add))
                v.wait_ge(s_td, base + 21)
                inc(v.tensor_tensor(zp[:], zz[:].rearrange("p (h two) -> p h two", two=2)[:, :, 0:1].squeeze(2),
                                    zz[:].rearrange("p (h two) -> p h two", two=2)[:, :, 1:2].squeeze(2),
                                    ALU.mult))
                v.wait_ge(s_td, base + 22)
                inc(v.reciprocal(rzp[:], zp[:]))
                v.wait_ge(s_td, base + 23)
                for h in range(8):
                    inc(v.tensor_scalar(aa[:, h * 64:h * 64 + 32], aa[:, h * 64:h * 64 + 32],
                                        rzp[:, h:h + 1], None, ALU.mult))
                v.wait_ge(s_td, base + 31)
                for h in range(8):
                    A0 = aa[:, h * 64:h * 64 + 32].unsqueeze(2).broadcast_to([P, 32, 32])
                    A1 = aa[:, h * 64 + 32:h * 64 + 64].unsqueeze(1).broadcast_to([P, 32, 32])
                    if h == 0:
                        inc(v.tensor_tensor(Pacc[:].rearrange("p (i j) -> p i j", j=32), A0, A1, ALU.mult))
                    else:
                        v.wait_ge(s_td, ntd)
                        inc(v.tensor_tensor(obuf[:].rearrange("p (i j) -> p i j", j=32), A0, A1, ALU.mult))
                        v.wait_ge(s_td, ntd)
                        inc(v.tensor_add(Pacc[:], Pacc[:], obuf[:]))
                v.wait_ge(s_td, base + 46)
                v.wait_ge(s_g0 if tt % 2 == 0 else s_g1, 16 * (tt // 2 + 1))
                inc(v.tensor_tensor(gt[tt % 2][:], Pacc[:], Gt[tt % 2][:], ALU.mult))
                v.wait_ge(s_tm, 2 * tt + 2)
                inc(v.tensor_copy(gT[:].rearrange("p (e t) -> p e t", t=T)[:, :, tt * P:(tt + 1) * P],
                                  ps_x[:, 1024:1536].bitcast(BF16).rearrange("p (e k) -> p e k", k=P)))

        @block.sync
        def _(sync):
            for mc in range(NCH):
                if mc >= 2:
                    sync.wait_ge(s_ov, 2 * (mc - 1))
                sync.dma_start(out=x1p[mc % 2][:], in_=x1[mc * P:(mc + 1) * P, :]).then_inc(s_sx0 if mc % 2 == 0 else s_sx1, 16)
                sync.dma_start(out=a1p[mc % 2][:], in_=a1[mc * P:(mc + 1) * P, :]).then_inc(s_sa0 if mc % 2 == 0 else s_sa1, 16)

        @block.tensor
        def _(t):
            t.wait_ge(s_td, 768)
            t.wait_ge(s_eu, 128)
            for mc in range(NCH):
                ps = ps_y if mc % 2 == 0 else ps_x
                if mc >= 2:
                    t.wait_ge(s_ov, 2 * (mc - 1))
                for j in range(4):
                    for ec in range(NCH):
                        mm = t.matmul(ps[:, SEGS[j]],
                                      eub[:, ec * D + mc * P: ec * D + (mc + 1) * P],
                                      gT[:, ec * T + j * 512: ec * T + j * 512 + 512],
                                      start=(ec == 0), stop=(ec == NCH - 1))
                mm.then_inc(s_om, 1)

        @block.vector
        def _(v):
            for mc in range(NCH):
                ps = ps_y if mc % 2 == 0 else ps_x
                v.wait_ge(s_om, mc + 1)
                v.wait_ge(s_sx0 if mc % 2 == 0 else s_sx1, 16 * (mc // 2 + 1))
                v.wait_ge(s_sa0 if mc % 2 == 0 else s_sa1, 16 * (mc // 2 + 1))
                if mc >= 2:
                    v.wait_ge(s_so if mc % 2 == 0 else s_so1, 16 * ((mc - 2) // 2 + 1))
                v.tensor_tensor(x2c[mc % 2][:], ps[:], x1p[mc % 2][:], ALU.add).then_inc(s_ov, 1)
                v.wait_ge(s_ov, 2 * mc + 1)
                v.tensor_tensor(x2c[mc % 2][:], x2c[mc % 2][:], a1p[mc % 2][:], ALU.add).then_inc(s_ov, 1)

        @block.scalar
        def _(s):
            for mc in range(NCH):
                s.wait_ge(s_ov, 2 * (mc + 1))
                s.dma_start(out=o_x2[mc * P:(mc + 1) * P, :], in_=x2c[mc % 2][:]).then_inc(s_so if mc % 2 == 0 else s_so1, 16)
            s.wait_ge(s_so, 64)
            s.wait_ge(s_so1, 64)
    es.close()
    return nc



_NC_CACHE = {}


def _get(builder):
    if builder.__name__ not in _NC_CACHE:
        _NC_CACHE[builder.__name__] = builder()
    return _NC_CACHE[builder.__name__]


def _run(builder, in_maps):
    nc = _get(builder)
    return run_bass_kernel_spmd(nc, in_maps, core_ids=list(range(NB))).results


def _ct(a):
    return np.ascontiguousarray(a)


def kernel(**inp):
    f32 = np.float32
    xT = [_ct(inp['L0_attn_x'][b].T) for b in range(NB)]

    r1 = _run(build_s1, [dict(
        xT=xT[b], Wq=inp['L0_attn_Wq'], Wk=inp['L0_attn_Wk'], Wv=inp['L0_attn_Wv'],
        Wdown=inp['L0_attn_W_down_kv'], Wupk=inp['L0_attn_W_up_k'],
        Wupv=inp['L0_attn_W_up_v']) for b in range(NB)])

    r2 = _run(build_s2, [dict(
        q_b=r1[b]['q_b'], krec_b=r1[b]['krec_b'], vrec_b=r1[b]['vrec_b'],
        Wo=inp['L0_attn_Wo']) for b in range(NB)])

    lng = _ct(inp['L0_ffn_ln_g'].reshape(NCH, P).T)
    lnb = _ct(inp['L0_ffn_ln_b'].reshape(NCH, P).T)
    ab1 = _ct(inp['L0_ffn_ab1'].reshape(32, P).T)
    ab2 = _ct(inp['L0_ffn_ab2'].reshape(NCH, P).T)
    gwt = _ct(inp['L0_ffn_gate_w'].reshape(NCH, P, 6).transpose(1, 0, 2).reshape(P, 48))
    r3 = _run(build_s3, [dict(
        xn_f=r1[b]['xn_f'], lng=lng, lnb=lnb, aw1=inp['L0_ffn_aw1'], ab1=ab1,
        aw2=inp['L0_ffn_aw2'], ab2=ab2, gwt=gwt,
        gate_b=inp['L0_ffn_gate_b'].reshape(1, 6)) for b in range(NB)])

    ew1 = inp['L0_ffn_ew1'].reshape(6 * D, 2 * D)
    eb1t = _ct(inp['L0_ffn_eb1'].reshape(6, 16, P).transpose(2, 0, 1).reshape(P, 96))
    ew2 = inp['L0_ffn_ew2'].reshape(6 * 2 * D, D)
    r4 = _run(build_s4, [dict(
        xn2_b=r3[b]['xn2_b'], ew1=ew1, eb1t=eb1t, ew2=ew2, eb2=inp['L0_ffn_eb2'],
        wT6=r3[b]['wT6']) for b in range(NB)])

    r5 = _run(build_s5, [dict(G=_ct(inp['L0_ffn_gens'][b % 4])) for b in range(NB)])

    gcwt = _ct(inp['L0_ffn_gc_w'].reshape(16, P).T)
    gcb = inp['L0_ffn_gc_b'].reshape(1, 1)
    r6 = _run(build_s6, [dict(
        E0T=r5[0]['ET'], E1=r5[1]['E'], E2=r5[2]['E'], E3=r5[3]['E'],
        moe=r4[b]['moe'], abel_b=r3[b]['abel_b'], xn_f=r1[b]['xn_f'],
        attnwo=r2[b]['attnwo'], xT=xT[b], gcwt=gcwt, gcb=gcb) for b in range(NB)])

    wqkvT = _ct(inp['L1_attn_w_qkv'].T)
    bqkv = _ct(inp['L1_attn_b_qkv'].reshape(24, P).T)
    r7a = _run(build_s7a, [dict(x1=r6[b]['x1'], wqkvT=wqkvT, bqkv=bqkv)
                           for b in range(NB)])

    woutT = _ct(inp['L1_attn_w_out'].T)
    bout = _ct(inp['L1_attn_b_out'].reshape(NCH, P).T)
    r7b = _run(build_s7b, [dict(
        q1_b=r7a[b]['q1_b'], k1_b=r7a[b]['k1_b'], v1_b=r7a[b]['v1_b'],
        woutT=woutT, bout=bout) for b in range(NB)])

    bq = _ct(inp['L1_ffn_bq'].reshape(16, P).T)
    edT = _ct(inp['L1_ffn_expert_down'].T)
    r8 = _run(build_s8, [dict(
        xn1_b=r7a[b]['xn1_b'], Wq=inp['L1_ffn_Wq'], bq=bq, edT=edT)
        for b in range(NB)])

    statsall = _ct(np.concatenate([r8[b]['stats'] for b in range(NB)], axis=1))
    bng = _ct(inp['L1_ffn_bn_gamma'].reshape(2, P).T)
    bnb = _ct(inp['L1_ffn_bn_beta'].reshape(2, P).T)
    r9 = _run(build_s9, [dict(
        q1f_b=r8[b]['q1f_b'], G_b=r8[b]['G_b'], statsall=statsall, bng=bng,
        bnb=bnb, sk0=inp['L1_ffn_sub_key0'], sk1=inp['L1_ffn_sub_key1'],
        eu=inp['L1_ffn_expert_up'], x1=r6[b]['x1'], attn1=r7b[b]['attn1'])
        for b in range(NB)])

    return np.stack([r9[b]['x2'].T for b in range(NB)]).astype(np.float32)


# version 04
# speedup vs baseline: 1.2020x; 1.2020x over previous
import numpy as np
import ml_dtypes
import concourse.bass as bass
import concourse.mybir as mybir
from concourse.bass_utils import run_bass_kernel_spmd
from contextlib import ExitStack

F32 = mybir.dt.float32
BF16 = mybir.dt.bfloat16
U32 = mybir.dt.uint32
AF = mybir.ActivationFunctionType
ALU = mybir.AluOpType
AX = mybir.AxisListType.X

P = 128
T = 2048
D = 1024
NCH = D // P
NB = 8
LN_EPS = 1e-5

bf16 = ml_dtypes.bfloat16
SEGS = [slice(j * 512, (j + 1) * 512) for j in range(4)]



def build_s1(nc=None, pre="", io=None):
    standalone = nc is None
    if standalone:
        nc = bass.Bass("TRN2", target_bir_lowering=False)

    def dram(name, shape, dt, kind):
        return io[name] if io is not None else nc.dram_tensor(name, shape, dt, kind=kind).ap()
    xT = dram("xT", [D, T], F32, "ExternalInput")
    Wq = dram("Wq", [D, D], F32, "ExternalInput")
    Wk = dram("Wk", [D, D], F32, "ExternalInput")
    Wv = dram("Wv", [D, D], F32, "ExternalInput")
    Wdown = dram("Wdown", [D, 256], F32, "ExternalInput")
    Wupk = dram("Wupk", [256, D], F32, "ExternalInput")
    Wupv = dram("Wupv", [256, D], F32, "ExternalInput")
    o_xn = dram("xn_f", [D, T], F32, "ExternalOutput")
    o_xnb = dram("xn_b", [D, T], BF16, "ExternalOutput")
    o_q = dram("q_b", [D, T], BF16, "ExternalOutput")
    o_krec = dram("krec_b", [D, T], BF16, "ExternalOutput")
    o_vrec = dram("vrec_b", [D, T], BF16, "ExternalOutput")

    es = ExitStack()
    sb = lambda n, s, d: es.enter_context(nc.sbuf_tensor(pre + n, s, d))
    x2 = [sb(f"x{i}", [P, T], F32) for i in range(2)]
    scr = sb("scr", [P, T], F32)
    xn32 = sb("xn32", [P, T], F32)
    ones = sb("ones", [P, P], F32)
    mu = sb("mu", [P, T], F32)
    aux = sb("aux", [P, T], F32)
    mrs = sb("mrs", [P, T], F32)
    xnb = [sb(f"xnb{c}", [P, T], BF16) for c in range(NCH)]
    kvb = [sb(f"kvb{c}", [P, T], BF16) for c in range(NCH)]
    wA = sb("wA", [P, NCH * D], BF16)
    wB = sb("wB", [P, NCH * D], BF16)
    lat = [sb(f"lat{i}", [P, T], BF16) for i in range(2)]
    outb = sb("outb", [P, T], BF16)
    ps_a = es.enter_context(nc.psum_tensor(pre + "ps_a", [P, T], F32))
    ps_b = es.enter_context(nc.psum_tensor(pre + "ps_b", [P, T], F32))
    s_x0 = es.enter_context(nc.semaphore(pre + "s_x0"))
    s_x1 = es.enter_context(nc.semaphore(pre + "s_x1"))
    s_lw = es.enter_context(nc.semaphore(pre + "s_lw"))
    s_lw2 = es.enter_context(nc.semaphore(pre + "s_lw2"))
    s_lw3 = es.enter_context(nc.semaphore(pre + "s_lw3"))
    s_mm = es.enter_context(nc.semaphore(pre + "s_mm"))
    s_ev = es.enter_context(nc.semaphore(pre + "s_ev"))
    s_sq = es.enter_context(nc.semaphore(pre + "s_sq"))
    s_st = es.enter_context(nc.semaphore(pre + "s_st"))
    s_so = es.enter_context(nc.semaphore(pre + "s_so"))

    def wt(buf, kc, mc):
        return buf[:, kc * D + mc * P: kc * D + (mc + 1) * P]

    with nc.Block() as block:
        @block.sync
        def _(sync):
            for c in range(NCH):
                if c >= 2:
                    sync.wait_ge(s_mm, 2 * (c - 1))
                    sync.wait_ge(s_sq, c - 1)
                sync.dma_start(out=x2[c % 2][:], in_=xT[c * P:(c + 1) * P, :]).then_inc(s_x0 if c % 2 == 0 else s_x1, 16)

        @block.gpsimd
        def _(g):
            for kc in range(NCH):
                g.dma_start(out=wA[:, kc * D:(kc + 1) * D], in_=Wk[kc * P:(kc + 1) * P, :]).then_inc(s_lw, 16)
                g.dma_start(out=wB[:, kc * D:(kc + 1) * D], in_=Wv[kc * P:(kc + 1) * P, :]).then_inc(s_lw, 16)

        @block.vector
        def _(v):
            v.memset(ones[:], 1.0).then_inc(s_ev, 1)

        @block.scalar
        def _(s):
            for c in range(NCH):
                s.wait_ge(s_x0 if c % 2 == 0 else s_x1, 16 * (c // 2 + 1))
                if c >= 1:
                    s.wait_ge(s_mm, 2 * c)
                s.activation(scr[:], x2[c % 2][:], AF.Square).then_inc(s_sq, 1)

        @block.tensor
        def _(t):
            t.wait_ge(s_ev, 1)
            for c in range(NCH):
                t.wait_ge(s_x0 if c % 2 == 0 else s_x1, 16 * (c // 2 + 1))
                for j in range(4):
                    mm = t.matmul(ps_a[:, SEGS[j]], ones[:], x2[c % 2][:, SEGS[j]],
                                  start=(c == 0), stop=(c == NCH - 1))
                mm.then_inc(s_mm, 1)
                t.wait_ge(s_sq, c + 1)
                for j in range(4):
                    mm = t.matmul(ps_b[:, SEGS[j]], ones[:], scr[:, SEGS[j]],
                                  start=(c == 0), stop=(c == NCH - 1))
                mm.then_inc(s_mm, 1)

        @block.scalar
        def _(s):
            s.wait_ge(s_mm, 16)
            s.activation(mu[:], ps_a[:], AF.Copy, scale=1.0 / D).then_inc(s_ev, 1)
            s.activation(aux[:], ps_b[:], AF.Copy, scale=1.0 / D).then_inc(s_ev, 1)
            s.wait_ge(s_ev, 2)
            s.activation(scr[:], mu[:], AF.Square).then_inc(s_ev, 1)

        @block.vector
        def _(v):
            v.wait_ge(s_ev, 4)
            v.tensor_sub(aux[:], aux[:], scr[:]).then_inc(s_ev, 1)
            v.wait_ge(s_ev, 5)
            v.tensor_scalar_add(aux[:], aux[:], float(LN_EPS)).then_inc(s_ev, 1)

        @block.scalar
        def _(s):
            s.wait_ge(s_ev, 6)
            s.activation(aux[:], aux[:], AF.Ln).then_inc(s_ev, 1)
            s.wait_ge(s_ev, 7)
            s.activation(aux[:], aux[:], AF.Exp, scale=-0.5).then_inc(s_ev, 1)

        @block.vector
        def _(v):
            v.wait_ge(s_ev, 8)
            v.tensor_mul(mrs[:], mu[:], aux[:]).then_inc(s_ev, 1)

        @block.sync
        def _(sync):
            for c in range(NCH):
                if c < 2:
                    sync.wait_ge(s_sq, 7 + c)
                    sync.wait_ge(s_mm, 14 + 2 * c)
                else:
                    sync.wait_ge(s_ev, 9 + 2 * (c - 1))
                sync.dma_start(out=x2[c % 2][:], in_=xT[c * P:(c + 1) * P, :]).then_inc(s_x0 if c % 2 == 0 else s_x1, 16)

        @block.vector
        def _(v):
            for c in range(NCH):
                v.wait_ge(s_x0 if c % 2 == 0 else s_x1, 64 + 16 * (c // 2 + 1))
                if c >= 1:
                    v.wait_ge(s_sq, 8 + c)
                    v.wait_ge(s_so, 32 * c)
                v.tensor_mul(xn32[:], x2[c % 2][:], aux[:]).then_inc(s_ev, 1)
                v.wait_ge(s_ev, 9 + 2 * c + 1)
                v.tensor_sub(xn32[:], xn32[:], mrs[:]).then_inc(s_ev, 1)

        @block.scalar
        def _(s):
            for c in range(NCH):
                s.wait_ge(s_ev, 9 + 2 * (c + 1))
                s.activation(xnb[c][:], xn32[:], AF.Copy).then_inc(s_sq, 1)

        @block.gpsimd
        def _(g):
            for c in range(NCH):
                g.wait_ge(s_sq, 8 + (c + 1))
                g.dma_start(out=o_xn[c * P:(c + 1) * P, :], in_=xn32[:]).then_inc(s_so, 16)
                g.dma_start(out=o_xnb[c * P:(c + 1) * P, :], in_=xnb[c][:]).then_inc(s_so, 16)

        @block.tensor
        def _(t):
            t.wait_ge(s_lw, 256)
            t.wait_ge(s_sq, 16)
            for mc in range(NCH):
                for j in range(4):
                    for kc in range(NCH):
                        t.matmul(ps_a[:, SEGS[j]], wt(wA, kc, mc), xnb[kc][:, SEGS[j]],
                                 start=(kc == 0), stop=False)
                        mm = t.matmul(ps_a[:, SEGS[j]], wt(wB, kc, mc), xnb[kc][:, SEGS[j]],
                                      start=False, stop=(kc == NCH - 1))
                mm.then_inc(s_mm, 1)
                t.wait_ge(s_ev, 25 + mc + 1)

        @block.scalar
        def _(s):
            for mc in range(NCH):
                s.wait_ge(s_mm, 16 + mc + 1)
                s.activation(kvb[mc][:], ps_a[:], AF.Copy).then_inc(s_ev, 1)

        @block.gpsimd
        def _(g):
            g.wait_ge(s_mm, 24)
            for kc in range(NCH):
                g.dma_start(out=wA[:, kc * 256:(kc + 1) * 256], in_=Wdown[kc * P:(kc + 1) * P, :]).then_inc(s_lw2, 16)
            for kc in range(2):
                g.dma_start(out=wA[:, 2048 + kc * D: 2048 + (kc + 1) * D], in_=Wupk[kc * P:(kc + 1) * P, :]).then_inc(s_lw2, 16)
                g.dma_start(out=wA[:, 4096 + kc * D: 4096 + (kc + 1) * D], in_=Wupv[kc * P:(kc + 1) * P, :]).then_inc(s_lw2, 16)
            for kc in range(NCH):
                g.dma_start(out=wB[:, kc * D:(kc + 1) * D], in_=Wq[kc * P:(kc + 1) * P, :]).then_inc(s_lw3, 16)

        @block.tensor
        def _(t):
            t.wait_ge(s_lw2, 192)
            t.wait_ge(s_ev, 33)
            for mc in range(2):
                for j in range(4):
                    for kc in range(NCH):
                        mm = t.matmul(ps_b[:, SEGS[j]],
                                      wA[:, kc * 256 + mc * P: kc * 256 + (mc + 1) * P],
                                      kvb[kc][:, SEGS[j]], start=(kc == 0), stop=(kc == NCH - 1))
                mm.then_inc(s_mm, 1)
                t.wait_ge(s_ev, 33 + mc + 1)

        @block.scalar
        def _(s):
            for mc in range(2):
                s.wait_ge(s_mm, 24 + mc + 1)
                s.activation(lat[mc][:], ps_b[:], AF.Copy).then_inc(s_ev, 1)

        @block.tensor
        def _(t):
            t.wait_ge(s_ev, 35)
            for mc in range(NCH):
                for j in range(4):
                    for kc in range(2):
                        mm = t.matmul(ps_a[:, SEGS[j]],
                                      wA[:, 2048 + kc * D + mc * P: 2048 + kc * D + (mc + 1) * P],
                                      lat[kc][:, SEGS[j]], start=(kc == 0), stop=(kc == 1))
                mm.then_inc(s_mm, 1)
                t.wait_ge(s_ev, 35 + mc + 1)
            for mc in range(NCH):
                for j in range(4):
                    for kc in range(2):
                        mm = t.matmul(ps_a[:, SEGS[j]],
                                      wA[:, 4096 + kc * D + mc * P: 4096 + kc * D + (mc + 1) * P],
                                      lat[kc][:, SEGS[j]], start=(kc == 0), stop=(kc == 1))
                mm.then_inc(s_mm, 1)
                t.wait_ge(s_ev, 35 + 8 + mc + 1)
            t.wait_ge(s_lw3, 128)
            for mc in range(NCH):
                for j in range(4):
                    for kc in range(NCH):
                        mm = t.matmul(ps_a[:, SEGS[j]], wt(wB, kc, mc), xnb[kc][:, SEGS[j]],
                                      start=(kc == 0), stop=(kc == NCH - 1))
                mm.then_inc(s_mm, 1)
                t.wait_ge(s_ev, 35 + 16 + mc + 1)

        @block.scalar
        def _(s):
            for i in range(24):
                s.wait_ge(s_mm, 26 + i + 1)
                if i >= 1:
                    s.wait_ge(s_st, 16 * i)
                s.activation(outb[:], ps_a[:], AF.Copy).then_inc(s_ev, 1)

        @block.sync
        def _(sync):
            outs = [o_krec, o_vrec, o_q]
            for i in range(24):
                sync.wait_ge(s_ev, 35 + i + 1)
                dst = outs[i // 8]
                mc = i % 8
                sync.dma_start(out=dst[mc * P:(mc + 1) * P, :], in_=outb[:]).then_inc(s_st, 16)
            sync.wait_ge(s_st, 16 * 24)
            sync.wait_ge(s_so, 256)
    es.close()
    if standalone:
        return nc
    return [(s_st, 384), (s_so, 256)]



def build_s2(nc=None, pre="", io=None):
    standalone = nc is None
    if standalone:
        nc = bass.Bass("TRN2", target_bir_lowering=False)

    def dram(name, shape, dt, kind):
        return io[name] if io is not None else nc.dram_tensor(name, shape, dt, kind=kind).ap()
    q_b = dram("q_b", [D, T], BF16, "ExternalInput")
    krec = dram("krec_b", [D, T], BF16, "ExternalInput")
    vrec = dram("vrec_b", [D, T], BF16, "ExternalInput")
    Wo = dram("Wo", [D, D], F32, "ExternalInput")
    o_aw = dram("attnwo", [D, T], F32, "ExternalOutput")

    NH = 16
    HD = 64
    VW = 65
    SCALE = float(1.0 / (32.0 + 1e-8))

    es = ExitStack()
    sb = lambda n, s, d: es.enter_context(nc.sbuf_tensor(pre + n, s, d))
    qf = [sb(f"qf{c}", [P, T], BF16) for c in range(NCH)]
    kf = [sb(f"kf{c}", [P, T], BF16) for c in range(NCH)]
    B = sb("B", [P, 2 * T], BF16)
    t1 = [sb(f"t1{i}", [P, 1024], F32) for i in range(4)]
    PT = [sb(f"PT{i}", [P, 1024], BF16) for i in range(4)]
    apre = [sb(f"apre{c}", [P, T], BF16) for c in range(NCH)]
    zrow = sb("zrow", [1, T], F32)
    rz = sb("rz", [1, T], F32)
    rzb = [sb(f"rzb{i}", [64, 1024], F32) for i in range(2)]
    ident = sb("ident", [P, P], F32)
    identb = sb("identb", [P, P], BF16)
    onesP = sb("onesP", [P, P], F32)
    ones1 = sb("ones1", [1, 64], F32)
    ves = ExitStack()
    vT = ves.enter_context(nc.sbuf_tensor(pre + "vT", [P, NH * 16 * VW], BF16))
    vst = ves.enter_context(nc.sbuf_tensor(pre + "vst", [P, T], BF16))
    ps_s = [es.enter_context(nc.psum_tensor(pre + f"ps_s{i}", [P, 1024], F32)) for i in range(2)]
    ps_av = es.enter_context(nc.psum_tensor(pre + "ps_av", [P, T], F32))
    sems = {}
    names = ("s_qk", "s_vs", "s_tr", "s_trev", "s_id", "s_dk", "s_vo", "s_sg",
             "s_t1", "s_pt", "s_ag", "s_zc", "s_rr", "s_zb", "s_hf", "s_wo",
             "s_ow", "s_st")
    for name in names:
        sems[name] = es.enter_context(nc.semaphore(pre + name))
    (s_qk, s_vs, s_tr, s_trev, s_id, s_dk, s_vo, s_sg, s_t1, s_pt, s_ag,
     s_zc, s_rr, s_zb, s_hf, s_wo, s_ow, s_st) = [sems[n] for n in names]

    def vtile(h, kc):
        return vT[:, (h * 16 + kc) * VW: (h * 16 + kc) * VW + VW]

    def hrows(buflist, h):
        return buflist[h // 2][(h % 2) * HD:(h % 2) * HD + HD, :]

    with nc.Block() as block:
        @block.sync
        def _(sync):
            for c in range(NCH):
                sync.dma_start(out=qf[c][:], in_=q_b[c * P:(c + 1) * P, :]).then_inc(s_qk, 16)
                sync.dma_start(out=kf[c][:], in_=krec[c * P:(c + 1) * P, :]).then_inc(s_qk, 16)

        @block.gpsimd
        def _(g):
            g.iota(ident[:], pattern=[[-1, P]], base=0, channel_multiplier=1,
                   allow_small_or_imprecise_dtypes=True).then_inc(s_id, 1)
            g.iota(t1[0][:], pattern=[[-1, 1024]], base=2048, channel_multiplier=1,
                   allow_small_or_imprecise_dtypes=True)
            g.iota(t1[1][:], pattern=[[-1, 1024]], base=1024, channel_multiplier=1,
                   allow_small_or_imprecise_dtypes=True).then_inc(s_dk, 1)

        @block.scalar
        def _(s):
            s.wait_ge(s_id, 1)
            s.activation(ident[:], ident[:], AF.Square).then_inc(s_id, 1)
            s.wait_ge(s_dk, 1)
            s.activation(t1[0][:], t1[0][:], AF.Abs).then_inc(s_dk, 1)
            s.wait_ge(s_dk, 2)
            s.activation(B[:, 0:1024], t1[0][:], AF.Exp, scale=-1e-4).then_inc(s_dk, 1)
            s.activation(t1[1][:], t1[1][:], AF.Abs).then_inc(s_dk, 1)
            s.wait_ge(s_dk, 4)
            s.activation(B[:, 1024:2048], t1[1][:], AF.Exp, scale=-1e-4).then_inc(s_dk, 1)

        @block.vector
        def _(v):
            v.memset(ones1[:], 1.0)
            v.memset(vT[:].rearrange("p (t c) -> p t c", c=VW)[:, :, 64:65], 1.0).then_inc(s_vo, 1)
            v.wait_ge(s_id, 2)
            v.tensor_scalar(ident[:], ident[:], -1.0, 1.0, ALU.mult, ALU.add).then_inc(s_id, 1)
            v.wait_ge(s_id, 3)
            v.tensor_relu(ident[:], ident[:]).then_inc(s_id, 1)

        @block.scalar
        def _(s):
            s.wait_ge(s_id, 4)
            s.activation(identb[:], ident[:], AF.Copy).then_inc(s_id, 1)

        @block.gpsimd
        def _(g):
            g.wait_ge(s_dk, 5)
            g.iota(t1[0][:], pattern=[[-1, 1024]], base=0, channel_multiplier=1,
                   allow_small_or_imprecise_dtypes=True)
            g.iota(t1[1][:], pattern=[[-1, 1024]], base=-1024, channel_multiplier=1,
                   allow_small_or_imprecise_dtypes=True).then_inc(s_dk, 1)
            for c in range(NCH):
                if c >= 1:
                    g.wait_ge(s_tr, 32 * c)
                g.dma_start(out=vst[:], in_=vrec[c * P:(c + 1) * P, :]).then_inc(s_vs, 16)

        @block.scalar
        def _(s):
            s.wait_ge(s_dk, 6)
            s.activation(t1[0][:], t1[0][:], AF.Abs).then_inc(s_dk, 1)
            s.wait_ge(s_dk, 7)
            s.activation(B[:, 2048:3072], t1[0][:], AF.Exp, scale=-1e-4).then_inc(s_dk, 1)
            s.activation(t1[1][:], t1[1][:], AF.Abs).then_inc(s_dk, 1)
            s.wait_ge(s_dk, 9)
            s.activation(B[:, 3072:4096], t1[1][:], AF.Exp, scale=-1e-4).then_inc(s_dk, 1)

        @block.tensor
        def _(t):
            t.wait_ge(s_id, 5)
            t.wait_ge(s_vo, 1)
            n_tr = 0
            for c in range(NCH):
                t.wait_ge(s_vs, 16 * (c + 1))
                for half in range(2):
                    h = 2 * c + half
                    for kcg in range(2):
                        for k8 in range(8):
                            kc = kcg * 8 + k8
                            tr = t.transpose(ps_av[:, 0:256].bitcast(BF16)[:, k8 * 64:(k8 + 1) * 64],
                                             vst[half * HD:(half + 1) * HD, kc * P:(kc + 1) * P],
                                             identb[half * HD:(half + 1) * HD, half * HD:(half + 1) * HD])
                            n_tr += 1
                        tr.then_inc(s_tr, 8)
                        t.wait_ge(s_trev, (n_tr // 8))

        @block.vector
        def _(v):
            ng = 0
            for c in range(NCH):
                for half in range(2):
                    h = 2 * c + half
                    for kcg in range(2):
                        ng += 1
                        v.wait_ge(s_tr, ng * 8)
                        src = ps_av[:, 0:256].bitcast(BF16).rearrange("p (k e) -> p k e", e=64)
                        dst = vT[:, (h * 16 + kcg * 8) * VW: (h * 16 + kcg * 8 + 8) * VW] \
                            .rearrange("p (k e) -> p k e", e=VW)[:, :, 0:64]
                        v.tensor_copy(dst, src).then_inc(s_trev, 1)

        @block.tensor
        def _(t):
            t.wait_ge(s_qk, 256)
            t.wait_ge(s_dk, 10)
            t.wait_ge(s_trev, 32)

            def emit_S(g):
                h, r = divmod(g, 32)
                qh, kc = divmod(r, 16)
                if g >= 2:
                    t.wait_ge(s_t1, g - 1)
                for jj in range(2):
                    mm = t.matmul(ps_s[g % 2][:, jj * 512:(jj + 1) * 512],
                                  hrows(kf, h)[:, kc * P:(kc + 1) * P],
                                  hrows(qf, h)[:, qh * 1024 + jj * 512: qh * 1024 + (jj + 1) * 512],
                                  start=True, stop=True)
                mm.then_inc(s_sg, 1)

            def emit_AV(g):
                h, r = divmod(g, 32)
                qh, kc = divmod(r, 16)
                if kc == 0 and qh == 0:
                    if h >= 1:
                        t.wait_ge(s_hf, 2 * h)
                        t.wait_ge(s_zc, 3 * h)
                t.wait_ge(s_pt, g + 1)
                for jj in range(2):
                    mm = t.matmul(ps_av[0:VW, qh * 1024 + jj * 512: qh * 1024 + (jj + 1) * 512],
                                  vtile(h, kc), PT[g % 4][:, jj * 512:(jj + 1) * 512],
                                  start=(kc == 0), stop=(kc == 15))
                mm.then_inc(s_ag, 1)

            def emit_bcast(h):
                t.wait_ge(s_rr, 2 * h + 2)
                for j in range(4):
                    mm = t.matmul(ps_av[64:128, SEGS[j]], ones1[:], rz[0:1, SEGS[j]],
                                  start=True, stop=True)
                mm.then_inc(s_zb, 1)

            for g in range(512):
                emit_S(g)
                if g >= 1:
                    emit_AV(g - 1)
                    if (g - 1) % 32 == 31:
                        emit_bcast((g - 1) // 32)
            emit_AV(511)
            emit_bcast(15)

        @block.vector
        def _(v):
            v.wait_ge(s_dk, 10)
            g = 0
            for h in range(NH):
                for qh in range(2):
                    for kc in range(16):
                        v.wait_ge(s_sg, g + 1)
                        if g >= 4:
                            v.wait_ge(s_pt, g - 3)
                        dslice = B[:, 2048 - kc * P + qh * 1024: 2048 - kc * P + qh * 1024 + 1024]
                        v.scalar_tensor_tensor(t1[g % 4][:], ps_s[g % 2][:], SCALE,
                                               dslice, ALU.mult, ALU.mult).then_inc(s_t1, 1)
                        g += 1
                v.wait_ge(s_zc, 3 * h + 3)
                v.tensor_tensor(hrows(apre, h)[:, 0:1024], ps_av[0:64, 0:1024], rzb[0][:], ALU.mult).then_inc(s_hf, 1)
                v.wait_ge(s_hf, 2 * h + 1)
                v.tensor_tensor(hrows(apre, h)[:, 1024:2048], ps_av[0:64, 1024:2048], rzb[1][:], ALU.mult).then_inc(s_hf, 1)

        @block.scalar
        def _(s):
            g = 0
            for h in range(NH):
                for qh in range(2):
                    for kc in range(16):
                        s.wait_ge(s_t1, g + 1)
                        if g >= 4:
                            s.wait_ge(s_ag, g - 3)
                        s.activation(PT[g % 4][:], t1[g % 4][:], AF.Exp).then_inc(s_pt, 1)
                        g += 1
                s.wait_ge(s_ag, 32 * (h + 1))
                if h >= 1:
                    s.wait_ge(s_rr, 2 * h - 1)
                s.activation(zrow[:], ps_av[64:65, :], AF.Copy).then_inc(s_zc, 1)
                s.wait_ge(s_zc, 3 * h + 1)
                if h >= 1:
                    s.wait_ge(s_zb, h)
                s.activation(rz[:], zrow[:], AF.Ln).then_inc(s_rr, 1)
                s.wait_ge(s_rr, 2 * h + 1)
                s.activation(rz[:], rz[:], AF.Exp, scale=-1.0).then_inc(s_rr, 1)
                s.wait_ge(s_zb, h + 1)
                if h >= 1:
                    s.wait_ge(s_hf, 2 * h)
                s.activation(rzb[0][:], ps_av[64:128, 0:1024][0:64, :], AF.Copy).then_inc(s_zc, 1)
                s.activation(rzb[1][:], ps_av[64:128, 1024:2048][0:64, :], AF.Copy).then_inc(s_zc, 1)

    ves.close()
    wo = sb("wo", [P, NCH * D], BF16)
    outw = sb("outw", [P, T], F32)

    with nc.Block() as block:
        @block.gpsimd
        def _(g):
            g.wait_ge(s_ag, 512)
            g.wait_ge(s_hf, 32)
            for c in range(NCH):
                g.dma_start(out=wo[:, c * D:(c + 1) * D], in_=Wo[c * P:(c + 1) * P, :]).then_inc(s_wo, 16)

        @block.tensor
        def _(t):
            t.wait_ge(s_hf, 32)
            t.wait_ge(s_wo, 128)
            for mc in range(NCH):
                for jj in range(4):
                    for kc in range(NCH):
                        mm = t.matmul(ps_av[:, SEGS[jj]], wo[:, kc * D + mc * P: kc * D + (mc + 1) * P],
                                      apre[kc][:, SEGS[jj]], start=(kc == 0), stop=(kc == NCH - 1))
                mm.then_inc(s_sg, 1)
                t.wait_ge(s_ow, mc + 1)

        @block.scalar
        def _(s):
            for mc in range(NCH):
                s.wait_ge(s_sg, 512 + mc + 1)
                if mc >= 1:
                    s.wait_ge(s_st, 16 * mc)
                s.activation(outw[:], ps_av[:], AF.Copy).then_inc(s_ow, 1)

        @block.sync
        def _(sync):
            for mc in range(NCH):
                sync.wait_ge(s_ow, mc + 1)
                sync.dma_start(out=o_aw[mc * P:(mc + 1) * P, :], in_=outw[:]).then_inc(s_st, 16)
            sync.wait_ge(s_st, 128)
    es.close()
    if standalone:
        return nc
    return [(s_st, 128)]



def build_s3(nc=None, pre="", io=None):
    standalone = nc is None
    if standalone:
        nc = bass.Bass("TRN2", target_bir_lowering=False)

    def dram(name, shape, dt, kind):
        return io[name] if io is not None else nc.dram_tensor(name, shape, dt, kind=kind).ap()
    xn = dram("xn_f", [D, T], F32, "ExternalInput")
    lng = dram("lng", [P, NCH], F32, "ExternalInput")
    lnb = dram("lnb", [P, NCH], F32, "ExternalInput")
    aw1 = dram("aw1", [D, 4 * D], F32, "ExternalInput")
    ab1 = dram("ab1", [P, 32], F32, "ExternalInput")
    aw2 = dram("aw2", [4 * D, D], F32, "ExternalInput")
    ab2 = dram("ab2", [P, NCH], F32, "ExternalInput")
    gwt = dram("gwt", [P, 48], F32, "ExternalInput")
    gb = dram("gate_b", [1, 6], F32, "ExternalInput")
    o_xn2 = dram("xn2_b", [D, T], BF16, "ExternalOutput")
    o_abel = dram("abel_b", [D, T], BF16, "ExternalOutput")
    o_wt6 = dram("wT6", [8, T], F32, "ExternalOutput")

    es = ExitStack()
    sb = lambda n, s, d: es.enter_context(nc.sbuf_tensor(pre + n, s, d))
    xn2b = [sb(f"xn2b{c}", [P, T], BF16) for c in range(NCH)]
    aw1b = sb("aw1b", [P, NCH * 4 * D], BF16)
    biases = sb("biases", [P, 64], F32)
    ones = sb("ones", [P, P], F32)
    identt = sb("identt", [P, P], F32)
    gwf = sb("gwf", [P, 48], F32)
    gwg = sb("gwg", [P, 48], F32)
    brow = sb("brow", [1, 8], F32)
    gbr = sb("gbr", [1, 8], F32)
    gbb = sb("gbb", [P, 8], F32)
    lgt = sb("lgt", [P, 8], F32)
    m8 = sb("m8", [P, 8], F32)
    w1 = sb("w1", [P, 1], F32)
    w1s = sb("w1s", [P, 1], F32)
    w2 = sb("w2", [P, 1], F32)
    dd = sb("dd", [P, 1], F32)
    mA = sb("mA", [P, 6], F32)
    mB = sb("mB", [P, 6], F32)
    wfull2 = [sb(f"wfull{i}", [P, 6], F32) for i in range(2)]
    wt6sb = sb("wt6sb", [8, T], F32)
    lacc = sb("lacc", [P, 256], F32)
    lscr = sb("lscr", [P, 256], F32)
    ps_a = es.enter_context(nc.psum_tensor(pre + "ps_a", [P, T], F32))
    ps_b = es.enter_context(nc.psum_tensor(pre + "ps_b", [P, T], F32))
    names = ("s_x0", "s_x1", "s_lw", "s_lw2", "s_lw3", "s_mm", "s_ev", "s_sq",
             "s_st", "s_am", "s_ae", "s_ao", "s_gt", "s_gv", "s_lg", "s_sig",
             "s_tp", "s_wc", "s_gg", "s_le", "s_la")
    sems = {n: es.enter_context(nc.semaphore(pre + n)) for n in names}
    (s_x0, s_x1, s_lw, s_lw2, s_lw3, s_mm, s_ev, s_sq, s_st, s_am, s_ae, s_ao,
     s_gt, s_gv, s_lg, s_sig, s_tp, s_wc, s_gg, s_le, s_la) = [sems[n] for n in names]

    ln_es = ExitStack()
    sbl = lambda n, s, d: ln_es.enter_context(nc.sbuf_tensor(pre + n, s, d))
    x2 = [sbl(f"x{i}", [P, T], F32) for i in range(2)]
    scr = sbl("scr", [P, T], F32)
    xn32 = sbl("xn32", [P, T], F32)
    mu = sbl("mu", [P, T], F32)
    aux = sbl("aux", [P, T], F32)
    mrs = sbl("mrs", [P, T], F32)

    with nc.Block() as block:
        @block.sync
        def _(sync):
            for c in range(NCH):
                if c >= 2:
                    sync.wait_ge(s_mm, 2 * (c - 1))
                    sync.wait_ge(s_sq, c - 1)
                sync.dma_start(out=x2[c % 2][:], in_=xn[c * P:(c + 1) * P, :]).then_inc(s_x0 if c % 2 == 0 else s_x1, 16)

        @block.gpsimd
        def _(g):
            for kc in range(NCH):
                g.dma_start(out=aw1b[:, kc * 4 * D:(kc + 1) * 4 * D], in_=aw1[kc * P:(kc + 1) * P, :]).then_inc(s_lw, 16)
            g.dma_start(out=biases[:, 0:32], in_=ab1).then_inc(s_lw, 16)
            g.dma_start(out=biases[:, 32:40], in_=ab2).then_inc(s_lw, 16)
            g.dma_start(out=biases[:, 40:48], in_=lng).then_inc(s_lw, 16)
            g.dma_start(out=biases[:, 48:56], in_=lnb).then_inc(s_lw, 16)
            g.dma_start(out=gwf[:], in_=gwt).then_inc(s_lw, 16)
            g.dma_start(out=gbr[0:1, 0:6], in_=gb).then_inc(s_lw, 16)
            g.iota(identt[:], pattern=[[-1, P]], base=0, channel_multiplier=1,
                   allow_small_or_imprecise_dtypes=True).then_inc(s_gt, 1)

        @block.vector
        def _(v):
            v.memset(ones[:], 1.0)
            v.memset(lgt[:], -1e30)
            v.memset(gbr[0:1, 6:8], 0.0)
            v.memset(brow[:], 0.0)
            v.memset(lacc[:], 0.0)
            v.memset(wt6sb[:], 0.0).then_inc(s_ev, 1)

        @block.scalar
        def _(s):
            s.wait_ge(s_gt, 1)
            s.activation(identt[:], identt[:], AF.Square).then_inc(s_gt, 1)
            for c in range(NCH):
                s.wait_ge(s_x0 if c % 2 == 0 else s_x1, 16 * (c // 2 + 1))
                if c >= 1:
                    s.wait_ge(s_mm, 2 * c)
                s.activation(scr[:], x2[c % 2][:], AF.Square).then_inc(s_sq, 1)

        @block.vector
        def _(v):
            v.wait_ge(s_gt, 2)
            v.tensor_scalar(identt[:], identt[:], -1.0, 1.0, ALU.mult, ALU.add).then_inc(s_gt, 1)
            v.wait_ge(s_gt, 3)
            v.tensor_relu(identt[:], identt[:]).then_inc(s_gt, 1)

        @block.tensor
        def _(t):
            t.wait_ge(s_ev, 1)
            for c in range(NCH):
                t.wait_ge(s_x0 if c % 2 == 0 else s_x1, 16 * (c // 2 + 1))
                for j in range(4):
                    mm = t.matmul(ps_a[:, SEGS[j]], ones[:], x2[c % 2][:, SEGS[j]],
                                  start=(c == 0), stop=(c == NCH - 1))
                mm.then_inc(s_mm, 1)
                t.wait_ge(s_sq, c + 1)
                for j in range(4):
                    mm = t.matmul(ps_b[:, SEGS[j]], ones[:], scr[:, SEGS[j]],
                                  start=(c == 0), stop=(c == NCH - 1))
                mm.then_inc(s_mm, 1)

        @block.scalar
        def _(s):
            s.wait_ge(s_mm, 16)
            s.activation(mu[:], ps_a[:], AF.Copy, scale=1.0 / D).then_inc(s_ev, 1)
            s.activation(aux[:], ps_b[:], AF.Copy, scale=1.0 / D).then_inc(s_ev, 1)
            s.wait_ge(s_ev, 2)
            s.activation(scr[:], mu[:], AF.Square).then_inc(s_ev, 1)

        @block.vector
        def _(v):
            v.wait_ge(s_ev, 4)
            v.tensor_sub(aux[:], aux[:], scr[:]).then_inc(s_ev, 1)
            v.wait_ge(s_ev, 5)
            v.tensor_scalar_add(aux[:], aux[:], float(LN_EPS)).then_inc(s_ev, 1)

        @block.scalar
        def _(s):
            s.wait_ge(s_ev, 6)
            s.activation(aux[:], aux[:], AF.Ln).then_inc(s_ev, 1)
            s.wait_ge(s_ev, 7)
            s.activation(aux[:], aux[:], AF.Exp, scale=-0.5).then_inc(s_ev, 1)

        @block.vector
        def _(v):
            v.wait_ge(s_ev, 8)
            v.tensor_mul(mrs[:], mu[:], aux[:]).then_inc(s_ev, 1)
            v.wait_ge(s_lw, 224)
            for c in range(NCH):
                v.tensor_scalar(gwg[:, 6 * c:6 * (c + 1)], gwf[:, 6 * c:6 * (c + 1)],
                                biases[:, 40 + c:41 + c], None, ALU.mult).then_inc(s_gg, 1)

        @block.sync
        def _(sync):
            for c in range(NCH):
                if c < 2:
                    sync.wait_ge(s_sq, 7 + c)
                    sync.wait_ge(s_mm, 14 + 2 * c)
                else:
                    sync.wait_ge(s_ev, 9 + 3 * (c - 1))
                sync.dma_start(out=x2[c % 2][:], in_=xn[c * P:(c + 1) * P, :]).then_inc(s_x0 if c % 2 == 0 else s_x1, 16)

        @block.vector
        def _(v):
            v.wait_ge(s_lw, 224)
            for c in range(NCH):
                v.wait_ge(s_x0 if c % 2 == 0 else s_x1, 64 + 16 * (c // 2 + 1))
                if c >= 1:
                    v.wait_ge(s_ev, 9 + 3 * c)
                    v.wait_ge(s_lg, c)
                v.tensor_mul(xn32[:], x2[c % 2][:], aux[:]).then_inc(s_ev, 1)
                v.wait_ge(s_ev, 9 + 3 * c + 1)
                v.tensor_sub(xn32[:], xn32[:], mrs[:]).then_inc(s_ev, 1)
                v.wait_ge(s_ev, 9 + 3 * c + 2)
                v.tensor_scalar(xn2b[c][:], xn32[:], biases[:, 40 + c:41 + c],
                                biases[:, 48 + c:49 + c], ALU.mult, ALU.add).then_inc(s_ev, 1)
                if c >= 1:
                    v.wait_ge(s_le, c)
                    v.tensor_add(lacc[:], lacc[:], lscr[:]).then_inc(s_la, 1)
            v.wait_ge(s_le, 8)
            v.tensor_add(lacc[:], lacc[:], lscr[:]).then_inc(s_la, 1)

        @block.tensor
        def _(t):
            t.wait_ge(s_gg, 8)
            for c in range(NCH):
                t.wait_ge(s_ev, 9 + 3 * c + 2)
                if c >= 1:
                    t.wait_ge(s_le, c)
                for tt in range(16):
                    mm = t.matmul(ps_b[:, 16 * tt:16 * tt + 6],
                                  xn32[:, tt * P:(tt + 1) * P],
                                  gwg[:, c * 6:(c + 1) * 6],
                                  start=True, stop=True)
                mm.then_inc(s_lg, 1)

        @block.scalar
        def _(s):
            for c in range(NCH):
                s.wait_ge(s_lg, c + 1)
                if c >= 1:
                    s.wait_ge(s_la, c)
                s.activation(lscr[:], ps_b[:, 0:256], AF.Copy).then_inc(s_le, 1)

        @block.tensor
        def _(t):
            t.wait_ge(s_lw, 224)
            t.wait_ge(s_ev, 2)
            for c in range(NCH):
                mm = t.matmul(ps_a[0:1, 1040:1046], biases[:, 48 + c:49 + c],
                              gwf[:, c * 6:(c + 1) * 6], start=(c == 0), stop=(c == NCH - 1))
            mm.then_inc(s_gt, 1)

        @block.scalar
        def _(s):
            s.wait_ge(s_gt, 5)
            s.activation(brow[0:1, 0:6], ps_a[0:1, 1040:1046], AF.Copy).then_inc(s_gt, 1)

        @block.vector
        def _(v):
            v.wait_ge(s_gt, 6)
            v.tensor_add(gbr[:], gbr[:], brow[:]).then_inc(s_gt, 1)

        @block.tensor
        def _(t):
            t.wait_ge(s_gt, 7)
            mm = t.matmul(ps_a[:, 1024:1032], ones[0:1, :], gbr[:], start=True, stop=True)
            mm.then_inc(s_gt, 1)

        @block.scalar
        def _(s):
            s.wait_ge(s_gt, 8)
            s.activation(gbb[:], ps_a[:, 1024:1032], AF.Copy).then_inc(s_gt, 1)

        @block.vector
        def _(v):
            for tt in range(16):
                b = 10 * tt
                v.wait_ge(s_la, 8)
                v.wait_ge(s_gt, 9)
                v.tensor_tensor(lgt[:, 0:6], lacc[:, 16 * tt:16 * tt + 6], gbb[:, 0:6], ALU.add).then_inc(s_gv, 1)
                v.wait_ge(s_gv, b + 1)
                v.max(out=m8[:], in_=lgt[:]).then_inc(s_gv, 1)
                v.wait_ge(s_gv, b + 2)
                v.tensor_sub(w1[:], m8[:, 0:1], m8[:, 1:2]).then_inc(s_gv, 1)
                v.wait_ge(s_sig, tt + 1)
                v.tensor_scalar(w2[:], w1s[:], -1.0, 1.0, ALU.mult, ALU.add).then_inc(s_gv, 1)
                v.wait_ge(s_gv, b + 4)
                v.tensor_sub(dd[:], w1s[:], w2[:]).then_inc(s_gv, 1)
                v.tensor_scalar(mA[:], lgt[:, 0:6], m8[:, 0:1], None, ALU.is_ge).then_inc(s_gv, 1)
                v.tensor_scalar(mB[:], lgt[:, 0:6], m8[:, 1:2], None, ALU.is_ge).then_inc(s_gv, 1)
                v.wait_ge(s_gv, b + 7)
                v.tensor_scalar(mA[:], mA[:], dd[:, 0:1], None, ALU.mult).then_inc(s_gv, 1)
                v.tensor_scalar(mB[:], mB[:], w2[:, 0:1], None, ALU.mult).then_inc(s_gv, 1)
                v.wait_ge(s_gv, b + 9)
                if tt >= 2:
                    v.wait_ge(s_tp, tt - 1)
                v.tensor_add(wfull2[tt % 2][:], mA[:], mB[:]).then_inc(s_gv, 1)

        @block.scalar
        def _(s):
            for tt in range(18):
                if tt < 16:
                    s.wait_ge(s_gv, 10 * tt + 3)
                    s.activation(w1s[:], w1[:], AF.Sigmoid).then_inc(s_sig, 1)
                j = tt - 2
                if 0 <= j < 16:
                    s.wait_ge(s_tp, j + 1)
                    s.activation(wt6sb[0:6, j * P:(j + 1) * P],
                                 ps_a[0:8, (j % 2) * 512:(j % 2) * 512 + 128][0:6, :],
                                 AF.Copy).then_inc(s_wc, 1)

        @block.tensor
        def _(t):
            for j in range(16):
                t.wait_ge(s_gv, 10 * j + 10)
                if j >= 2:
                    t.wait_ge(s_wc, j - 1)
                tr = t.transpose(ps_a[0:8, (j % 2) * 512:(j % 2) * 512 + 128][0:6, :],
                                 wfull2[j % 2][:, 0:6], identt[:])
                tr.then_inc(s_tp, 1)

        @block.sync
        def _(sync):
            for c in range(NCH):
                sync.wait_ge(s_ev, 9 + 3 * (c + 1))
                sync.dma_start(out=o_xn2[c * P:(c + 1) * P, :], in_=xn2b[c][:]).then_inc(s_st, 16)
            sync.wait_ge(s_wc, 16)
            sync.dma_start(out=o_wt6, in_=wt6sb[:]).then_inc(s_st, 16)
    ln_es.close()

    h1 = sb("h1", [P, 32 * 512], BF16)
    aw2b = sb("aw2b", [P, 32 * 1024], BF16)
    outa = [sb(f"outa{i}", [P, 512], BF16) for i in range(2)]

    with nc.Block() as block:
        @block.gpsimd
        def _(g):
            g.wait_ge(s_ev, 33)
            g.wait_ge(s_lg, 8)
            for sl in range(32):
                g.dma_start(out=aw2b[:, sl * 1024:(sl + 1) * 1024],
                            in_=aw2[sl * P:(sl + 1) * P, :]).then_inc(s_lw2, 16)

        @block.tensor
        def _(t):
            t.wait_ge(s_ev, 33)
            t.wait_ge(s_wc, 16)
            for tch in range(4):
                tseg = slice(tch * 512, (tch + 1) * 512)
                for hh in range(32):
                    gh = 32 * tch + hh
                    slot = ps_a[:, (hh % 4) * 512:(hh % 4 + 1) * 512]
                    if gh >= 4:
                        t.wait_ge(s_ae, gh - 3)
                    for kc in range(NCH):
                        mm = t.matmul(slot, aw1b[:, kc * 4096 + hh * P: kc * 4096 + (hh + 1) * P],
                                      xn2b[kc][:, tseg], start=(kc == 0), stop=(kc == NCH - 1))
                    mm.then_inc(s_am, 1)
                t.wait_ge(s_lw2, 512)
                t.wait_ge(s_ae, 32 * tch + 32)
                for mc in range(NCH):
                    ga = 8 * tch + mc
                    slot = ps_b[:, (mc % 4) * 512:(mc % 4 + 1) * 512]
                    if ga >= 4:
                        t.wait_ge(s_ao, ga - 3)
                    for kc2 in range(32):
                        mm = t.matmul(slot, aw2b[:, kc2 * 1024 + mc * P: kc2 * 1024 + (mc + 1) * P],
                                      h1[:, kc2 * 512:(kc2 + 1) * 512],
                                      start=(kc2 == 0), stop=(kc2 == 31))
                    mm.then_inc(s_am, 1)

        @block.scalar
        def _(s):
            s.wait_ge(s_ev, 33)
            for tch in range(4):
                for hh in range(32):
                    nm = 40 * tch + hh
                    s.wait_ge(s_am, nm + 1)
                    slot = ps_a[:, (hh % 4) * 512:(hh % 4 + 1) * 512]
                    s.activation(h1[:, hh * 512:(hh + 1) * 512], slot, AF.Gelu,
                                 bias=biases[:, hh:hh + 1]).then_inc(s_ae, 1)

        @block.vector
        def _(v):
            for tch in range(4):
                for mc in range(NCH):
                    nm = 40 * tch + 32 + mc
                    ga = 8 * tch + mc
                    v.wait_ge(s_am, nm + 1)
                    if ga >= 2:
                        v.wait_ge(s_st, 144 + 16 * ga)
                    slot = ps_b[:, (mc % 4) * 512:(mc % 4 + 1) * 512]
                    v.tensor_scalar(outa[ga % 2][:], slot, biases[:, 32 + mc:33 + mc],
                                    None, ALU.add).then_inc(s_ao, 1)

        @block.sync
        def _(sync):
            for tch in range(4):
                for mc in range(NCH):
                    ga = 8 * tch + mc
                    sync.wait_ge(s_ao, ga + 1)
                    sync.dma_start(out=o_abel[mc * P:(mc + 1) * P, tch * 512:(tch + 1) * 512],
                                   in_=outa[ga % 2][:]).then_inc(s_st, 16)
            sync.wait_ge(s_st, 144 + 512)
    es.close()
    if standalone:
        return nc
    return [(s_st, 656)]



def build_s4(nc=None, pre="", io=None):
    standalone = nc is None
    if standalone:
        nc = bass.Bass("TRN2", target_bir_lowering=False)

    def dram(name, shape, dt, kind):
        return io[name] if io is not None else nc.dram_tensor(name, shape, dt, kind=kind).ap()
    xn2 = dram("xn2_b", [D, T], BF16, "ExternalInput")
    ew1 = dram("ew1", [6 * D, 2 * D], F32, "ExternalInput")
    eb1t = dram("eb1t", [P, 96], F32, "ExternalInput")
    ew2 = dram("ew2", [6 * 2 * D, D], F32, "ExternalInput")
    eb2 = dram("eb2", [6, D], F32, "ExternalInput")
    wt6 = dram("wT6", [8, T], F32, "ExternalInput")
    o_moe = dram("moe", [D, T], F32, "ExternalOutput")

    es = ExitStack()
    sb = lambda n, s, d: es.enter_context(nc.sbuf_tensor(pre + n, s, d))
    xnb = [sb(f"xnb{c}", [P, T], BF16) for c in range(NCH)]
    macc = [sb(f"macc{c}", [P, T], F32) for c in range(NCH)]
    ew1b = sb("ew1b", [P, NCH * 2 * D], BF16)
    ew2b = sb("ew2b", [P, 16 * D], BF16)
    hprime = sb("hprime", [P, 16 * 512], BF16)
    hg = [sb(f"hg{i}", [P, 512], BF16) for i in range(2)]
    wb = sb("wb", [P, T], BF16)
    wt6sb = sb("wt6sb", [8, T], F32)
    wrow = sb("wrow", [1, T], F32)
    eb1sb = sb("eb1sb", [P, 96], F32)
    eb2sb = sb("eb2sb", [8, D], F32)
    ones1 = sb("ones1", [1, P], F32)
    ps_a = es.enter_context(nc.psum_tensor(pre + "ps_a", [P, T], F32))
    ps_b = es.enter_context(nc.psum_tensor(pre + "ps_b", [P, T], F32))
    names = ("s_xn", "s_lw", "s_w1", "s_w2", "s_mm", "s_ae", "s_av", "s_st", "s_ms")
    sems = {n: es.enter_context(nc.semaphore(pre + n)) for n in names}
    (s_xn, s_lw, s_w1, s_w2, s_mm, s_ae, s_av, s_st, s_ms) = [sems[n] for n in names]

    def mm_wb(e):
        return 9 + 97 * e

    def mm_h(e, tch, hch):
        return 9 + 97 * e + 1 + tch * 24 + hch

    def mm_out(e, tch, dch):
        return 9 + 97 * e + 1 + tch * 24 + 16 + dch

    def ae_wb(e):
        return 9 + 65 * e

    def ae_gelu(e, tch, hch):
        return 9 + 65 * e + 1 + tch * 16 + hch

    def av_mult(e, tch, hch):
        return 96 * e + 24 * tch + hch + 1

    def av_add(e, tch, dch):
        return 96 * e + 24 * tch + 16 + dch + 1

    with nc.Block() as block:
        @block.sync
        def _(sync):
            for c in range(NCH):
                sync.dma_start(out=xnb[c][:], in_=xn2[c * P:(c + 1) * P, :]).then_inc(s_xn, 16)

        @block.gpsimd
        def _(g):
            g.dma_start(out=wt6sb[:], in_=wt6).then_inc(s_lw, 16)
            g.dma_start(out=eb1sb[:], in_=eb1t).then_inc(s_lw, 16)
            g.dma_start(out=eb2sb[0:6, :], in_=eb2).then_inc(s_lw, 16)
            g.wait_ge(s_lw, 48)
            for e in range(6):
                if e >= 1:
                    g.wait_ge(s_mm, mm_wb(e - 1))
                g.dma_start(out=wrow[:], in_=wt6sb[e:e + 1, :]).then_inc(s_lw, 16)
                if e >= 1:
                    g.wait_ge(s_mm, mm_h(e - 1, 3, 15))
                for kc in range(NCH):
                    g.dma_start(out=ew1b[:, kc * 2048:(kc + 1) * 2048],
                                in_=ew1[(e * NCH + kc) * P:(e * NCH + kc + 1) * P, :]).then_inc(s_w1, 16)
                if e >= 1:
                    g.wait_ge(s_mm, mm_out(e - 1, 3, 7))
                for kc2 in range(16):
                    g.dma_start(out=ew2b[:, kc2 * 1024:(kc2 + 1) * 1024],
                                in_=ew2[(e * 16 + kc2) * P:(e * 16 + kc2 + 1) * P, :]).then_inc(s_w2, 16)

        @block.vector
        def _(v):
            v.memset(ones1[:], 1.0).then_inc(s_ms, 1)

        @block.tensor
        def _(t):
            t.wait_ge(s_lw, 48)
            for c in range(NCH):
                if c >= 1:
                    t.wait_ge(s_ae, c)
                for j in range(4):
                    mm = t.matmul(ps_a[:, SEGS[j]], eb2sb[0:6, c * P:(c + 1) * P],
                                  wt6sb[0:6, SEGS[j]], start=True, stop=True)
                mm.then_inc(s_mm, 1)
            t.wait_ge(s_xn, 128)
            for e in range(6):
                t.wait_ge(s_lw, 48 + 16 * (e + 1))
                t.wait_ge(s_ms, 1)
                if e >= 1:
                    t.wait_ge(s_av, av_add(e - 1, 3, 7))
                else:
                    t.wait_ge(s_ae, 8)
                for j in range(4):
                    mm = t.matmul(ps_b[:, SEGS[j]], ones1[0:1, :], wrow[0:1, SEGS[j]],
                                  start=True, stop=True)
                mm.then_inc(s_mm, 1)
                for tch in range(4):
                    tseg = slice(tch * 512, (tch + 1) * 512)
                    for hch in range(16):
                        t.wait_ge(s_w1, 128 * (e + 1))
                        k = tch * 16 + hch
                        if k >= 4:
                            pg = mm_h(e, (k - 4) // 16, (k - 4) % 16)
                            t.wait_ge(s_ae, ae_gelu(e, (k - 4) // 16, (k - 4) % 16))
                        slot = ps_a[:, (k % 4) * 512:(k % 4 + 1) * 512]
                        for kc in range(NCH):
                            mm = t.matmul(slot, ew1b[:, kc * 2048 + hch * P: kc * 2048 + (hch + 1) * P],
                                          xnb[kc][:, tseg], start=(kc == 0), stop=(kc == NCH - 1))
                        mm.then_inc(s_mm, 1)
                    t.wait_ge(s_w2, 256 * (e + 1))
                    t.wait_ge(s_av, av_mult(e, tch, 15))
                    for dch in range(8):
                        jj = tch * 8 + dch
                        if jj >= 4:
                            t.wait_ge(s_av, av_add(e, (jj - 4) // 8, (jj - 4) % 8))
                        elif e == 0:
                            t.wait_ge(s_ae, ae_wb(0))
                        else:
                            t.wait_ge(s_ae, ae_wb(e))
                        slot = ps_b[:, (jj % 4) * 512:(jj % 4 + 1) * 512]
                        for kc2 in range(16):
                            mm = t.matmul(slot, ew2b[:, kc2 * 1024 + dch * P: kc2 * 1024 + (dch + 1) * P],
                                          hprime[:, kc2 * 512:(kc2 + 1) * 512],
                                          start=(kc2 == 0), stop=(kc2 == 15))
                        mm.then_inc(s_mm, 1)

        @block.scalar
        def _(s):
            for c in range(NCH):
                s.wait_ge(s_mm, c + 1)
                s.activation(macc[c][:], ps_a[:], AF.Copy).then_inc(s_ae, 1)
            for e in range(6):
                s.wait_ge(s_mm, mm_wb(e))
                if e >= 1:
                    s.wait_ge(s_av, av_mult(e - 1, 3, 15))
                s.activation(wb[:], ps_b[:], AF.Copy).then_inc(s_ae, 1)
                for tch in range(4):
                    for hch in range(16):
                        s.wait_ge(s_mm, mm_h(e, tch, hch))
                        ig = e * 64 + tch * 16 + hch
                        if ig >= 2:
                            pig = ig - 2
                            pe_, ptch, phch = pig // 64, (pig % 64) // 16, pig % 16
                            s.wait_ge(s_av, av_mult(pe_, ptch, phch))
                        k = tch * 16 + hch
                        slot = ps_a[:, (k % 4) * 512:(k % 4 + 1) * 512]
                        s.activation(hg[ig % 2][:], slot, AF.Gelu,
                                     bias=eb1sb[:, e * 16 + hch:e * 16 + hch + 1]).then_inc(s_ae, 1)

        @block.vector
        def _(v):
            for e in range(6):
                for tch in range(4):
                    tseg = slice(tch * 512, (tch + 1) * 512)
                    for hch in range(16):
                        v.wait_ge(s_ae, ae_gelu(e, tch, hch))
                        ig = e * 64 + tch * 16 + hch
                        if tch >= 1:
                            v.wait_ge(s_mm, mm_out(e, tch - 1, 7))
                        elif e >= 1:
                            v.wait_ge(s_mm, mm_out(e - 1, 3, 7))
                        v.tensor_tensor(hprime[:, hch * 512:(hch + 1) * 512], hg[ig % 2][:],
                                        wb[:, tseg], ALU.mult).then_inc(s_av, 1)
                    for dch in range(8):
                        v.wait_ge(s_mm, mm_out(e, tch, dch))
                        if e == 0 and tch == 0:
                            v.wait_ge(s_ae, 8)
                        v.tensor_tensor(macc[dch][:, tseg], ps_b[:, (tch * 8 + dch) % 4 * 512:((tch * 8 + dch) % 4 + 1) * 512],
                                        macc[dch][:, tseg], ALU.add).then_inc(s_av, 1)

        @block.sync
        def _(sync):
            sync.wait_ge(s_av, 576)
            for c in range(NCH):
                sync.dma_start(out=o_moe[c * P:(c + 1) * P, :], in_=macc[c][:]).then_inc(s_st, 16)
            sync.wait_ge(s_st, 128)
    es.close()
    if standalone:
        return nc
    return [(s_st, 128)]



import math as _math


def build_s5(nc=None, pre="", io=None):
    standalone = nc is None
    if standalone:
        nc = bass.Bass("TRN2", target_bir_lowering=False)

    def dram(name, shape, dt, kind):
        return io[name] if io is not None else nc.dram_tensor(name, shape, dt, kind=kind).ap()
    G = dram("G", [D, D], F32, "ExternalInput")
    o_E = dram("E", [D, D], F32, "ExternalOutput")
    o_ET = dram("ET", [D, D], F32, "ExternalOutput")

    es = ExitStack()
    sb = lambda n, s, d: es.enter_context(nc.sbuf_tensor(pre + n, s, d))
    NMAT = 11
    pool = [sb(f"m{i}", [P, NCH * D], BF16) for i in range(NMAT)]
    cid = sb("cid", [P, 19 * P], BF16)
    identf = sb("identf", [P, P], F32)
    identb = sb("identb", [P, P], BF16)
    gld = sb("gld", [P, D], F32)
    stage = sb("stage", [P, D], F32)
    ps_a = es.enter_context(nc.psum_tensor(pre + "ps_a", [P, T], F32))
    names = ("s_g", "s_mm", "s_ae", "s_av", "s_id", "s_st")
    sems = {n: es.enter_context(nc.semaphore(pre + n)) for n in names}
    (s_g, s_mm, s_ae, s_av, s_id, s_st) = [sems[n] for n in names]

    A, ID, A2, A3, A4 = 0, 1, 2, 3, 4
    B0, B1, B2, B0p, B1p, B2p = 5, 6, 7, 8, 9, 10
    p_, pp_ = 0, 1
    E1, E1T = 2, 3

    C = [1.0 / _math.factorial(k) for k in range(13)]
    coefs = []
    for k in range(13):
        coefs.append(C[k])
    for k in (1, 3, 5, 7, 9, 11):
        coefs.append(-C[k])
    NEG1 = len(coefs)

    def cslot(val):
        return cid[:, coefs.index(val) * P:(coefs.index(val) + 1) * P]

    cnt = {"mm": 0, "ae": 0, "av": 0}

    with nc.Block() as block:
        @block.sync
        def _(sync):
            for c in range(NCH):
                if c >= 1:
                    sync.wait_ge(s_ae, c)
                sync.dma_start(out=gld[:], in_=G[c * P:(c + 1) * P, :]).then_inc(s_g, 16)

        @block.gpsimd
        def _(g):
            g.iota(identf[:], pattern=[[-1, P]], base=0, channel_multiplier=1,
                   allow_small_or_imprecise_dtypes=True).then_inc(s_id, 1)

        @block.scalar
        def _(s):
            s.wait_ge(s_id, 1)
            s.activation(identf[:], identf[:], AF.Square).then_inc(s_id, 1)

        @block.vector
        def _(v):
            v.wait_ge(s_id, 2)
            v.tensor_scalar(identf[:], identf[:], -1.0, 1.0, ALU.mult, ALU.add).then_inc(s_id, 1)
            v.wait_ge(s_id, 3)
            v.tensor_relu(identf[:], identf[:]).then_inc(s_id, 1)

        @block.scalar
        def _(s):
            s.wait_ge(s_id, 4)
            s.activation(identb[:], identf[:], AF.Copy).then_inc(s_id, 1)
            for i, cv in enumerate(coefs):
                s.wait_ge(s_id, 5 + i)
                s.activation(cid[:, i * P:(i + 1) * P], identf[:], AF.Copy,
                             scale=float(cv)).then_inc(s_id, 1)

        @block.scalar
        def _(s):
            for c in range(NCH):
                s.wait_ge(s_g, 16 * (c + 1))
                s.activation(pool[A][:, c * D:(c + 1) * D], gld[:], AF.Copy,
                             scale=0.25).then_inc(s_ae, 1)
        cnt["ae"] = 8

        @block.vector
        def _(v):
            v.wait_ge(s_id, 5)
            v.memset(pool[ID][:], 0.0).then_inc(s_av, 1)
            v.wait_ge(s_av, 1)
            for c in range(NCH):
                v.tensor_copy(pool[ID][:, c * D + c * P: c * D + (c + 1) * P], identb[:]).then_inc(s_av, 1)
        cnt["av"] = 9

        def product(t, Zi, Yi, OUTi, scale, extra_wait=None):
            pass

    prods = [
        ("A2", A, A, A2, -1.0, "copy"),
        ("A3", A2, A, A3, 1.0, "copy"),
        ("A4", A3, A, A4, -1.0, "copy"),
    ]

    with nc.Block() as block:
        @block.tensor
        def _(t):
            t.wait_ge(s_id, 24)
            t.wait_ge(s_ae, 8)
            t.wait_ge(s_av, 9)
            for name, Zi, Yi, OUTi, scale, mode in prods:
                t.wait_ge(s_ae, 8 + cnt["mm"])
                for mc in range(NCH):
                    for j2 in range(2):
                        for kc in range(NCH):
                            mm = t.matmul(ps_a[:, (mc % 2) * 1024 + j2 * 512:(mc % 2) * 1024 + j2 * 512 + 512],
                                          pool[Zi][:, kc * D + mc * P: kc * D + (mc + 1) * P],
                                          pool[Yi][:, kc * D + j2 * 512: kc * D + j2 * 512 + 512],
                                          start=(kc == 0), stop=(kc == NCH - 1))
                    cnt["mm"] += 1
                    mm.then_inc(s_mm, 1)
                    t.wait_ge(s_ae, 8 + cnt["mm"] - 1)
            bdefs = [
                (B0, [(C[0], ID), (C[1], A), (C[2], A2), (C[3], A3)]),
                (B1, [(C[4], ID), (C[5], A), (C[6], A2), (C[7], A3)]),
                (B2, [(C[8], ID), (C[9], A), (C[10], A2), (C[11], A3), (C[12], A4)]),
                (B0p, [(C[0], ID), (-C[1], A), (C[2], A2), (-C[3], A3)]),
                (B1p, [(C[4], ID), (-C[5], A), (C[6], A2), (-C[7], A3)]),
                (B2p, [(C[8], ID), (-C[9], A), (C[10], A2), (-C[11], A3), (C[12], A4)]),
            ]
            for Bi, terms in bdefs:
                t.wait_ge(s_ae, 8 + cnt["mm"])
                for mc in range(NCH):
                    for j2 in range(2):
                        for ti, (cv, Xi) in enumerate(terms):
                            mm = t.matmul(ps_a[:, (mc % 2) * 1024 + j2 * 512:(mc % 2) * 1024 + j2 * 512 + 512],
                                          cslot(cv),
                                          pool[Xi][:, mc * D + j2 * 512: mc * D + j2 * 512 + 512],
                                          start=(ti == 0), stop=(ti == len(terms) - 1))
                    cnt["mm"] += 1
                    mm.then_inc(s_mm, 1)
                    t.wait_ge(s_ae, 8 + cnt["mm"] - 1)
            for name, Zi, Yi in (("T1", A4, B2), ("T2", A4, B1),
                                 ("T1p", A4, B2p), ("T2p", A4, B1p),
                                 ("E1", ID, A), ("E1T", A, ID),
                                 ("E", A3, A2), ("ET", A2, A3)):
                t.wait_ge(s_ae, 8 + cnt["mm"])
                for mc in range(NCH):
                    for j2 in range(2):
                        for kc in range(NCH):
                            mm = t.matmul(ps_a[:, (mc % 2) * 1024 + j2 * 512:(mc % 2) * 1024 + j2 * 512 + 512],
                                          pool[Zi][:, kc * D + mc * P: kc * D + (mc + 1) * P],
                                          pool[Yi][:, kc * D + j2 * 512: kc * D + j2 * 512 + 512],
                                          start=(kc == 0), stop=(kc == NCH - 1))
                    cnt["mm"] += 1
                    mm.then_inc(s_mm, 1)
                    t.wait_ge(s_ae, 8 + cnt["mm"] - 1)

        @block.scalar
        def _(s):
            nev = 8
            ngrp = 0
            for name, Zi, Yi, OUTi, scale, mode in prods:
                for mc in range(NCH):
                    ngrp += 1
                    s.wait_ge(s_mm, ngrp)
                    s.activation(pool[OUTi][:, mc * D:(mc + 1) * D],
                                 ps_a[:, (mc % 2) * 1024:(mc % 2) * 1024 + 1024],
                                 AF.Copy, scale=scale).then_inc(s_ae, 1)
                    nev += 1
            for Bi in (B0, B1, B2, B0p, B1p, B2p):
                for mc in range(NCH):
                    ngrp += 1
                    s.wait_ge(s_mm, ngrp)
                    s.activation(pool[Bi][:, mc * D:(mc + 1) * D],
                                 ps_a[:, (mc % 2) * 1024:(mc % 2) * 1024 + 1024],
                                 AF.Copy).then_inc(s_ae, 1)
                    nev += 1

        @block.vector
        def _(v):
            ngrp = 72
            plan = [
                (B1, B1, "add"),
                (A, B0, "add"),
                (B1p, B1p, "add"),
                (ID, B0p, "add"),
                (A2, None, "copy"),
                (A3, None, "copy"),
                (None, None, "E"),
                (None, None, "ET"),
            ]
            for pi, (OUTi, ADDi, mode) in enumerate(plan):
                for mc in range(NCH):
                    ngrp += 1
                    v.wait_ge(s_mm, ngrp)
                    src_ps = ps_a[:, (mc % 2) * 1024:(mc % 2) * 1024 + 1024]
                    if mode == "add":
                        v.tensor_tensor(pool[OUTi][:, mc * D:(mc + 1) * D], src_ps,
                                        pool[ADDi][:, mc * D:(mc + 1) * D], ALU.add).then_inc(s_ae, 1)
                    elif mode == "copy":
                        v.tensor_copy(pool[OUTi][:, mc * D:(mc + 1) * D], src_ps).then_inc(s_ae, 1)
                    else:
                        v.tensor_copy(stage[:], src_ps).then_inc(s_ae, 1)
                        v.wait_ge(s_st, 16 * ((1 if mode == "ET" else 0) * 8 + mc + 1))

        @block.sync
        def _(sync):
            for i, dst in enumerate((o_E, o_ET)):
                for mc in range(NCH):
                    sync.wait_ge(s_ae, 8 + 72 + 48 + i * 8 + mc + 1)
                    sync.dma_start(out=dst[mc * P:(mc + 1) * P, :], in_=stage[:]).then_inc(s_st, 16)
            sync.wait_ge(s_st, 256)
    es.close()
    if standalone:
        return nc
    return [(s_st, 256)]



def build_s6(nc=None, pre="", io=None):
    standalone = nc is None
    if standalone:
        nc = bass.Bass("TRN2", target_bir_lowering=False)

    def dram(name, shape, dt, kind):
        return io[name] if io is not None else nc.dram_tensor(name, shape, dt, kind=kind).ap()
    E0T = dram("E0T", [D, D], F32, "ExternalInput")
    E1_ = dram("E1", [D, D], F32, "ExternalInput")
    E2_ = dram("E2", [D, D], F32, "ExternalInput")
    E3_ = dram("E3", [D, D], F32, "ExternalInput")
    moe = dram("moe", [D, T], F32, "ExternalInput")
    abel = dram("abel_b", [D, T], BF16, "ExternalInput")
    xn = dram("xn_f", [D, T], F32, "ExternalInput")
    aw = dram("attnwo", [D, T], F32, "ExternalInput")
    xT = dram("xT", [D, T], F32, "ExternalInput")
    gcwt = dram("gcwt", [P, 16], F32, "ExternalInput")
    gcb = dram("gcb", [1, 1], F32, "ExternalInput")
    o_x1 = dram("x1", [D, T], F32, "ExternalOutput")

    es = ExitStack()
    sb = lambda n, s, d: es.enter_context(nc.sbuf_tensor(pre + n, s, d))
    ta = sb("ta", [P, NCH * D], BF16)
    tb = sb("tb", [P, NCH * D], BF16)
    tc = sb("tc", [P, NCH * D], BF16)
    abl = [sb(f"abl{c}", [P, T], BF16) for c in range(NCH)]
    nab = [sb(f"nab{c}", [P, T], BF16) for c in range(NCH)]
    gbf = sb("gbf", [P, T], F32)
    gcw = sb("gcw", [P, 16], BF16)
    gcwf = sb("gcwf", [P, 16], F32)
    gcbs = sb("gcbs", [1, 1], F32)
    grow = sb("grow", [1, T], F32)
    ones1 = sb("ones1", [1, P], F32)
    ps_a = es.enter_context(nc.psum_tensor(pre + "ps_a", [P, T], F32))
    ps_b = es.enter_context(nc.psum_tensor(pre + "ps_b", [P, T], F32))
    names = ("s_la", "s_lb", "s_lc", "s_mo", "s_ab", "s_gc", "s_gc2", "s_mm", "s_ae",
             "s_av", "s_ms", "s_st", "s_st1") + tuple(f"s_s{i}" for i in range(6))
    sems = {n: es.enter_context(nc.semaphore(pre + n)) for n in names}
    (s_la, s_lb, s_lc, s_mo, s_ab, s_gc, s_gc2, s_mm, s_ae, s_av, s_ms, s_st, s_st1) = \
        [sems[n] for n in names[:13]]
    s_str = [sems[f"s_s{i}"] for i in range(6)]

    mo_es = ExitStack()
    mo = [mo_es.enter_context(nc.sbuf_tensor(pre + f"mo{c}", [P, T], BF16)) for c in range(NCH)]

    with nc.Block() as block:
        @block.gpsimd
        def _(g):
            for c in range(NCH):
                g.dma_start(out=ta[:, c * D:(c + 1) * D], in_=E0T[c * P:(c + 1) * P, :]).then_inc(s_la, 16)
                g.dma_start(out=tb[:, c * D:(c + 1) * D], in_=E1_[c * P:(c + 1) * P, :]).then_inc(s_lb, 16)
            g.wait_ge(s_mm, 8)
            for c in range(NCH):
                g.dma_start(out=ta[:, c * D:(c + 1) * D], in_=E2_[c * P:(c + 1) * P, :]).then_inc(s_la, 16)
            g.wait_ge(s_mm, 16)
            for c in range(NCH):
                g.dma_start(out=tc[:, c * D:(c + 1) * D], in_=E3_[c * P:(c + 1) * P, :]).then_inc(s_lc, 16)
            for c in range(NCH):
                g.dma_start(out=mo[c][:], in_=moe[c * P:(c + 1) * P, :]).then_inc(s_mo, 16)
            g.dma_start(out=gcw[:], in_=gcwt).then_inc(s_gc2, 16)

        @block.sync
        def _(sync):
            for c in range(NCH):
                sync.dma_start(out=abl[c][:], in_=abel[c * P:(c + 1) * P, :]).then_inc(s_ab, 16)
            sync.dma_start(out=gcwf[:], in_=gcwt).then_inc(s_gc, 16)
            sync.dma_start(out=gcbs[:], in_=gcb).then_inc(s_gc, 16)

        @block.vector
        def _(v):
            v.memset(ones1[:], 1.0).then_inc(s_ms, 1)

        @block.tensor
        def _(t):
            plist = [
                (tb, ta, s_lb, 128, s_la, 128),
                (ta, tc, s_la, 256, None, 0),
                (tb, tc, s_lc, 128, None, 0),
            ]
            ng = 0
            for pi, (lhs, rhs, w1, v1, w2, v2) in enumerate(plist):
                t.wait_ge(w1, v1)
                if w2 is not None:
                    t.wait_ge(w2, v2)
                t.wait_ge(s_ae, ng)
                for mc in range(NCH):
                    if ng >= 2:
                        t.wait_ge(s_ae, ng - 1)
                    for j2 in range(2):
                        for kc in range(NCH):
                            mm = t.matmul(ps_a[:, (mc % 2) * 1024 + j2 * 512:(mc % 2) * 1024 + j2 * 512 + 512],
                                          lhs[:, kc * D + mc * P: kc * D + (mc + 1) * P],
                                          rhs[:, kc * D + j2 * 512: kc * D + j2 * 512 + 512],
                                          start=(kc == 0), stop=(kc == NCH - 1))
                    ng += 1
                    mm.then_inc(s_mm, 1)
            t.wait_ge(s_mo, 128)
            t.wait_ge(s_ae, 24)
            for mc in range(NCH):
                if mc >= 1:
                    t.wait_ge(s_ae, 24 + mc)
                for j in range(4):
                    for kc in range(NCH):
                        mm = t.matmul(ps_a[:, (mc % 2) * 1024 + (j % 2) * 512:(mc % 2) * 1024 + (j % 2) * 512 + 512]
                                      if False else ps_a[:, SEGS[j]],
                                      ta[:, kc * D + mc * P: kc * D + (mc + 1) * P],
                                      mo[kc][:, SEGS[j]], start=(kc == 0), stop=(kc == NCH - 1))
                mm.then_inc(s_mm, 1)
            t.wait_ge(s_ae, 32)
            t.wait_ge(s_ab, 128)
            t.wait_ge(s_gc, 32)
            t.wait_ge(s_gc2, 16)
            for j in range(4):
                for kc in range(NCH):
                    t.matmul(ps_b[0:1, SEGS[j]], gcw[:, kc:kc + 1], abl[kc][:, SEGS[j]],
                             start=(kc == 0), stop=False)
                for kc in range(NCH):
                    mm = t.matmul(ps_b[0:1, SEGS[j]], gcw[:, 8 + kc:9 + kc], nab[kc][:, SEGS[j]],
                                  start=False, stop=(kc == NCH - 1))
                mm.then_inc(s_mm, 1)
            t.wait_ge(s_ae, 36)
            t.wait_ge(s_ms, 1)
            for j in range(4):
                mm = t.matmul(ps_b[:, SEGS[j]], ones1[0:1, :], grow[0:1, SEGS[j]],
                              start=True, stop=True)
            mm.then_inc(s_mm, 1)

        @block.scalar
        def _(s):
            targets = [tc, tb, ta]
            ng = 0
            for pi in range(3):
                for mc in range(NCH):
                    ng += 1
                    s.wait_ge(s_mm, ng)
                    s.activation(targets[pi][:, mc * D:(mc + 1) * D],
                                 ps_a[:, (mc % 2) * 1024:(mc % 2) * 1024 + 1024],
                                 AF.Copy).then_inc(s_ae, 1)
            for mc in range(NCH):
                s.wait_ge(s_mm, 24 + mc + 1)
                s.activation(nab[mc][:], ps_a[:], AF.Copy).then_inc(s_ae, 1)
            for j in range(4):
                s.wait_ge(s_mm, 32 + j + 1)
                s.activation(grow[0:1, SEGS[j]], ps_b[0:1, SEGS[j]], AF.Sigmoid,
                             bias=gcbs[0:1, 0:1]).then_inc(s_ae, 1)
            s.wait_ge(s_mm, 37)
            s.activation(gbf[:], ps_b[:], AF.Copy).then_inc(s_ae, 1)
    mo_es.close()

    xnp = [sb(f"xnp{i}", [P, T], F32) for i in range(2)]
    awp = [sb(f"awp{i}", [P, T], F32) for i in range(2)]
    xtp = [sb(f"xtp{i}", [P, T], F32) for i in range(2)]
    x1c = [sb(f"x1c{i}", [P, T], F32) for i in range(2)]

    with nc.Block() as block:
        @block.sync
        def _(sync):
            sync.wait_ge(s_mm, 32)
            for mc in range(NCH):
                if mc >= 2:
                    sync.wait_ge(s_av, 6 * (mc - 1))
                sync.dma_start(out=xnp[mc % 2][:], in_=xn[mc * P:(mc + 1) * P, :]).then_inc(s_str[mc % 2], 16)
                sync.dma_start(out=awp[mc % 2][:], in_=aw[mc * P:(mc + 1) * P, :]).then_inc(s_str[2 + mc % 2], 16)
                sync.dma_start(out=xtp[mc % 2][:], in_=xT[mc * P:(mc + 1) * P, :]).then_inc(s_str[4 + mc % 2], 16)

        @block.vector
        def _(v):
            for mc in range(NCH):
                b = 6 * mc
                for i in range(3):
                    v.wait_ge(s_str[2 * i + mc % 2], 16 * (mc // 2 + 1))
                v.wait_ge(s_ae, 37)
                v.wait_ge(s_ae, 24 + mc + 1)
                if mc >= 2:
                    v.wait_ge(s_st if mc % 2 == 0 else s_st1, 16 * ((mc - 2) // 2 + 1))
                v.tensor_tensor(x1c[mc % 2][:], nab[mc][:], abl[mc][:], ALU.subtract).then_inc(s_av, 1)
                v.wait_ge(s_av, b + 1)
                v.tensor_tensor(x1c[mc % 2][:], x1c[mc % 2][:], gbf[:], ALU.mult).then_inc(s_av, 1)
                v.wait_ge(s_av, b + 2)
                v.tensor_tensor(x1c[mc % 2][:], x1c[mc % 2][:], abl[mc][:], ALU.add).then_inc(s_av, 1)
                v.wait_ge(s_av, b + 3)
                v.tensor_tensor(x1c[mc % 2][:], x1c[mc % 2][:], xnp[mc % 2][:], ALU.add).then_inc(s_av, 1)
                v.wait_ge(s_av, b + 4)
                v.tensor_tensor(x1c[mc % 2][:], x1c[mc % 2][:], awp[mc % 2][:], ALU.add).then_inc(s_av, 1)
                v.wait_ge(s_av, b + 5)
                v.tensor_tensor(x1c[mc % 2][:], x1c[mc % 2][:], xtp[mc % 2][:], ALU.add).then_inc(s_av, 1)

        @block.scalar
        def _(s):
            for mc in range(NCH):
                s.wait_ge(s_av, 6 * (mc + 1))
                s.dma_start(out=o_x1[mc * P:(mc + 1) * P, :], in_=x1c[mc % 2][:]).then_inc(s_st if mc % 2 == 0 else s_st1, 16)
            s.wait_ge(s_st, 64)
            s.wait_ge(s_st1, 64)
    es.close()
    if standalone:
        return nc
    return [(s_st, 64), (s_st1, 64)]



def build_s7a(nc=None, pre="", io=None):
    standalone = nc is None
    if standalone:
        nc = bass.Bass("TRN2", target_bir_lowering=False)

    def dram(name, shape, dt, kind):
        return io[name] if io is not None else nc.dram_tensor(name, shape, dt, kind=kind).ap()
    x1 = dram("x1", [D, T], F32, "ExternalInput")
    wqkvT = dram("wqkvT", [D, 3 * D], F32, "ExternalInput")
    bqkv = dram("bqkv", [P, 24], F32, "ExternalInput")
    o_xn1 = dram("xn1_b", [D, T], BF16, "ExternalOutput")
    o_q = dram("q1_b", [D, T], BF16, "ExternalOutput")
    o_k = dram("k1_b", [D, T], BF16, "ExternalOutput")
    o_v = dram("v1_b", [D, T], BF16, "ExternalOutput")

    es = ExitStack()
    sb = lambda n, s, d: es.enter_context(nc.sbuf_tensor(pre + n, s, d))
    xn1b = [sb(f"xn1b{c}", [P, T], BF16) for c in range(NCH)]
    wbuf = sb("wbuf", [P, NCH * 3 * D], BF16)
    biases = sb("biases", [P, 32], F32)
    onesP = sb("onesP", [P, P], F32)
    outq = [sb(f"outq{i}", [P, T], BF16) for i in range(2)]
    x2 = [sb(f"xp{i}", [P, T], F32) for i in range(2)]
    scr = sb("scr", [P, T], F32)
    xn32 = sb("xn32", [P, T], F32)
    mu = sb("mu", [P, T], F32)
    aux = sb("aux", [P, T], F32)
    mrs = sb("mrs", [P, T], F32)
    ps_x = es.enter_context(nc.psum_tensor(pre + "ps_x", [P, T], F32))
    ps_y = es.enter_context(nc.psum_tensor(pre + "ps_y", [P, T], F32))
    names = ("s_x0", "s_x1", "s_lw", "s_mm", "s_ev", "s_sq", "s_st", "s_so", "s_qv")
    sems = {n: es.enter_context(nc.semaphore(pre + n)) for n in names}
    (s_x0, s_x1, s_lw, s_mm, s_ev, s_sq, s_st, s_so, s_qv) = [sems[n] for n in names]

    with nc.Block() as block:
        @block.sync
        def _(sync):
            for c in range(NCH):
                if c >= 2:
                    sync.wait_ge(s_mm, 2 * (c - 1))
                    sync.wait_ge(s_sq, c - 1)
                sync.dma_start(out=x2[c % 2][:], in_=x1[c * P:(c + 1) * P, :]).then_inc(s_x0 if c % 2 == 0 else s_x1, 16)

        @block.gpsimd
        def _(g):
            for kc in range(NCH):
                g.dma_start(out=wbuf[:, kc * 3 * D:(kc + 1) * 3 * D], in_=wqkvT[kc * P:(kc + 1) * P, :]).then_inc(s_lw, 16)
            g.dma_start(out=biases[:, 0:24], in_=bqkv).then_inc(s_lw, 16)

        @block.vector
        def _(v):
            v.memset(onesP[:], 1.0).then_inc(s_ev, 1)

        @block.scalar
        def _(s):
            for c in range(NCH):
                s.wait_ge(s_x0 if c % 2 == 0 else s_x1, 16 * (c // 2 + 1))
                if c >= 1:
                    s.wait_ge(s_mm, 2 * c)
                s.activation(scr[:], x2[c % 2][:], AF.Square).then_inc(s_sq, 1)

        @block.tensor
        def _(t):
            t.wait_ge(s_ev, 1)
            for c in range(NCH):
                t.wait_ge(s_x0 if c % 2 == 0 else s_x1, 16 * (c // 2 + 1))
                for j in range(4):
                    mm = t.matmul(ps_x[:, SEGS[j]], onesP[:], x2[c % 2][:, SEGS[j]],
                                  start=(c == 0), stop=(c == NCH - 1))
                mm.then_inc(s_mm, 1)
                t.wait_ge(s_sq, c + 1)
                for j in range(4):
                    mm = t.matmul(ps_y[:, SEGS[j]], onesP[:], scr[:, SEGS[j]],
                                  start=(c == 0), stop=(c == NCH - 1))
                mm.then_inc(s_mm, 1)

        @block.scalar
        def _(s):
            s.wait_ge(s_mm, 16)
            s.activation(mu[:], ps_x[:], AF.Copy, scale=1.0 / D).then_inc(s_ev, 1)
            s.activation(aux[:], ps_y[:], AF.Copy, scale=1.0 / D).then_inc(s_ev, 1)
            s.wait_ge(s_ev, 2)
            s.activation(scr[:], mu[:], AF.Square).then_inc(s_ev, 1)

        @block.vector
        def _(v):
            v.wait_ge(s_ev, 4)
            v.tensor_sub(aux[:], aux[:], scr[:]).then_inc(s_ev, 1)
            v.wait_ge(s_ev, 5)
            v.tensor_scalar_add(aux[:], aux[:], float(LN_EPS)).then_inc(s_ev, 1)

        @block.scalar
        def _(s):
            s.wait_ge(s_ev, 6)
            s.activation(aux[:], aux[:], AF.Ln).then_inc(s_ev, 1)
            s.wait_ge(s_ev, 7)
            s.activation(aux[:], aux[:], AF.Exp, scale=-0.5).then_inc(s_ev, 1)

        @block.vector
        def _(v):
            v.wait_ge(s_ev, 8)
            v.tensor_mul(mrs[:], mu[:], aux[:]).then_inc(s_ev, 1)

        @block.sync
        def _(sync):
            for c in range(NCH):
                if c < 2:
                    sync.wait_ge(s_sq, 7 + c)
                    sync.wait_ge(s_mm, 14 + 2 * c)
                else:
                    sync.wait_ge(s_ev, 9 + 2 * (c - 1))
                sync.dma_start(out=x2[c % 2][:], in_=x1[c * P:(c + 1) * P, :]).then_inc(s_x0 if c % 2 == 0 else s_x1, 16)

        @block.vector
        def _(v):
            for c in range(NCH):
                v.wait_ge(s_x0 if c % 2 == 0 else s_x1, 64 + 16 * (c // 2 + 1))
                if c >= 1:
                    v.wait_ge(s_ev, 9 + 2 * c)
                    v.wait_ge(s_sq, 8 + c)
                v.tensor_mul(xn32[:], x2[c % 2][:], aux[:]).then_inc(s_ev, 1)
                v.wait_ge(s_ev, 9 + 2 * c + 1)
                v.tensor_sub(xn32[:], xn32[:], mrs[:]).then_inc(s_ev, 1)

        @block.scalar
        def _(s):
            for c in range(NCH):
                s.wait_ge(s_ev, 9 + 2 * (c + 1))
                s.activation(xn1b[c][:], xn32[:], AF.Copy).then_inc(s_sq, 1)

        @block.gpsimd
        def _(g):
            for c in range(NCH):
                g.wait_ge(s_sq, 8 + (c + 1))
                g.dma_start(out=o_xn1[c * P:(c + 1) * P, :], in_=xn1b[c][:]).then_inc(s_st, 16)

        @block.tensor
        def _(t):
            t.wait_ge(s_lw, 144)
            t.wait_ge(s_sq, 16)
            for mc in range(24):
                ps = ps_x if mc % 2 == 0 else ps_y
                if mc >= 2:
                    t.wait_ge(s_qv, mc - 1)
                for j in range(4):
                    for kc in range(NCH):
                        mm = t.matmul(ps[:, SEGS[j]],
                                      wbuf[:, kc * 3 * D + mc * P: kc * 3 * D + (mc + 1) * P],
                                      xn1b[kc][:, SEGS[j]], start=(kc == 0), stop=(kc == NCH - 1))
                mm.then_inc(s_mm, 1)

        @block.vector
        def _(v):
            for mc in range(24):
                ps = ps_x if mc % 2 == 0 else ps_y
                v.wait_ge(s_mm, 16 + mc + 1)
                if mc >= 2:
                    v.wait_ge(s_so, 16 * mc)
                if mc < 8:
                    v.tensor_scalar(outq[mc % 2][:], ps[:], biases[:, mc:mc + 1], 0.125,
                                    ALU.add, ALU.mult).then_inc(s_qv, 1)
                else:
                    v.tensor_scalar(outq[mc % 2][:], ps[:], biases[:, mc:mc + 1], None,
                                    ALU.add).then_inc(s_qv, 1)

        @block.sync
        def _(sync):
            outs = [o_q, o_k, o_v]
            for mc in range(24):
                sync.wait_ge(s_qv, mc + 1)
                dst = outs[mc // 8]
                cc = mc % 8
                sync.dma_start(out=dst[cc * P:(cc + 1) * P, :], in_=outq[mc % 2][:]).then_inc(s_so, 16)
            sync.wait_ge(s_so, 384)
            sync.wait_ge(s_st, 128)
    es.close()
    if standalone:
        return nc
    return [(s_so, 384), (s_st, 128)]



def build_s7b(nc=None, pre="", io=None):
    standalone = nc is None
    if standalone:
        nc = bass.Bass("TRN2", target_bir_lowering=False)

    def dram(name, shape, dt, kind):
        return io[name] if io is not None else nc.dram_tensor(name, shape, dt, kind=kind).ap()
    q_b = dram("q1_b", [D, T], BF16, "ExternalInput")
    k_b = dram("k1_b", [D, T], BF16, "ExternalInput")
    v_b = dram("v1_b", [D, T], BF16, "ExternalInput")
    woutT = dram("woutT", [D, D], F32, "ExternalInput")
    bout = dram("bout", [P, 8], F32, "ExternalInput")
    o_a1 = dram("attn1", [D, T], F32, "ExternalOutput")

    NH, HD, VW = 16, 64, 65
    es = ExitStack()
    sb = lambda n, s, d: es.enter_context(nc.sbuf_tensor(pre + n, s, d))
    qf = [sb(f"qf{c}", [P, T], BF16) for c in range(NCH)]
    kf = [sb(f"kf{c}", [P, T], BF16) for c in range(NCH)]
    vT = sb("vT", [P, NH * 16 * VW], BF16)
    vsc = sb("vsc", [P, T], BF16)
    apre = [sb(f"apre{c}", [P, T], BF16) for c in range(NCH)]
    wbuf2 = sb("wbuf2", [P, NCH * D], BF16)
    biases = sb("biases", [P, 8], F32)
    ident = sb("ident", [P, P], F32)
    identb = sb("identb", [P, P], BF16)
    ones1 = sb("ones1", [1, 64], F32)
    PTs = sb("PTs", [P, 1920], BF16)
    PTg = sb("PTg", [P, 1920], BF16)
    PT0 = sb("PT0", [P, T], BF16)
    zrow = sb("zrow", [1, T], F32)
    rz = sb("rz", [1, T], F32)
    rzb = sb("rzb", [64, T], F32)
    outw = sb("outw", [P, T], F32)
    ps_x = es.enter_context(nc.psum_tensor(pre + "ps_x", [P, T], F32))
    ps_y = es.enter_context(nc.psum_tensor(pre + "ps_y", [P, T], F32))
    names = ("s_qk", "s_vs", "s_lw", "s_id", "s_ms", "s_tr", "s_trev", "s_sg",
             "s_pt", "s_ag", "s_zc", "s_rr", "s_zb", "s_hf", "s_mm", "s_ow", "s_st")
    sems = {n: es.enter_context(nc.semaphore(pre + n)) for n in names}
    (s_qk, s_vs, s_lw, s_id, s_ms, s_tr, s_trev, s_sg, s_pt, s_ag, s_zc,
     s_rr, s_zb, s_hf, s_mm, s_ow, s_st) = [sems[n] for n in names]

    def hrows(buflist, h):
        return buflist[h // 2][(h % 2) * HD:(h % 2) * HD + HD, :]

    def vtile(h, kc):
        return vT[:, (h * 16 + kc) * VW: (h * 16 + kc) * VW + VW]

    with nc.Block() as block:
        @block.sync
        def _(sync):
            for c in range(NCH):
                sync.dma_start(out=qf[c][:], in_=q_b[c * P:(c + 1) * P, :]).then_inc(s_qk, 16)
                sync.dma_start(out=kf[c][:], in_=k_b[c * P:(c + 1) * P, :]).then_inc(s_qk, 16)

        @block.gpsimd
        def _(g):
            for kc in range(NCH):
                g.dma_start(out=wbuf2[:, kc * D:(kc + 1) * D], in_=woutT[kc * P:(kc + 1) * P, :]).then_inc(s_lw, 16)
            g.dma_start(out=biases[:], in_=bout).then_inc(s_lw, 16)
            g.iota(ident[:], pattern=[[-1, P]], base=0, channel_multiplier=1,
                   allow_small_or_imprecise_dtypes=True).then_inc(s_id, 1)
            for c in range(NCH):
                if c >= 1:
                    g.wait_ge(s_tr, 32 * c)
                g.dma_start(out=vsc[:], in_=v_b[c * P:(c + 1) * P, :]).then_inc(s_vs, 16)

        @block.scalar
        def _(s):
            s.wait_ge(s_id, 1)
            s.activation(ident[:], ident[:], AF.Square).then_inc(s_id, 1)

        @block.vector
        def _(v):
            v.memset(ones1[:], 1.0)
            v.memset(vT[:].rearrange("p (t c) -> p t c", c=VW)[:, :, 64:65], 1.0).then_inc(s_ms, 1)
            v.wait_ge(s_id, 2)
            v.tensor_scalar(ident[:], ident[:], -1.0, 1.0, ALU.mult, ALU.add).then_inc(s_id, 1)
            v.wait_ge(s_id, 3)
            v.tensor_relu(ident[:], ident[:]).then_inc(s_id, 1)

        @block.scalar
        def _(s):
            s.wait_ge(s_id, 4)
            s.activation(identb[:], ident[:], AF.Copy).then_inc(s_id, 1)

        @block.tensor
        def _(t):
            t.wait_ge(s_id, 5)
            t.wait_ge(s_ms, 1)
            n_tr = 0
            for c in range(NCH):
                t.wait_ge(s_vs, 16 * (c + 1))
                for half in range(2):
                    for kcg in range(2):
                        for k8 in range(8):
                            kc = kcg * 8 + k8
                            tr = t.transpose(ps_x[:, 0:256].bitcast(BF16)[:, k8 * 64:(k8 + 1) * 64],
                                             vsc[half * HD:(half + 1) * HD, kc * P:(kc + 1) * P],
                                             identb[half * HD:(half + 1) * HD, half * HD:(half + 1) * HD])
                            n_tr += 1
                        tr.then_inc(s_tr, 8)
                        t.wait_ge(s_trev, (n_tr // 8))

        @block.vector
        def _(v):
            for ng in range(32):
                v.wait_ge(s_tr, (ng + 1) * 8)
                c8, rem = divmod(ng, 4)
                half, kcg = divmod(rem, 2)
                h = 2 * c8 + half
                srcv = ps_x[:, 0:256].bitcast(BF16).rearrange("p (k e) -> p k e", e=64)
                dstv = vT[:, (h * 16 + kcg * 8) * VW: (h * 16 + kcg * 8 + 8) * VW] \
                    .rearrange("p (k e) -> p k e", e=VW)[:, :, 0:64]
                v.tensor_copy(dstv, srcv).then_inc(s_trev, 1)

        @block.tensor
        def _(t):
            t.wait_ge(s_qk, 256)
            t.wait_ge(s_trev, 32)
            for h in range(NH):
                if h >= 1:
                    t.wait_ge(s_hf, 2 * h)
                    t.wait_ge(s_zc, 4 * h)
                for qb in range(1, 16):
                    mm = t.matmul(ps_x[:, (qb - 1) * P: qb * P],
                                  hrows(kf, h)[:, qb * P:(qb + 1) * P],
                                  hrows(qf, h)[:, qb * P:(qb + 1) * P],
                                  start=True, stop=True)
                mm.then_inc(s_sg, 1)
                for lo, width in ((0, 512), (512, 512), (1024, 512), (1536, 384)):
                    mm = t.matmul(ps_y[:, lo:lo + width],
                                  hrows(kf, h)[:, 0:P],
                                  hrows(qf, h)[:, P + lo: P + lo + width],
                                  start=True, stop=True)
                mm.then_inc(s_sg, 1)
                t.wait_ge(s_pt, 3 * h + 2)
                for qb in range(1, 16):
                    t.matmul(ps_x[0:VW, (qb - 1) * P: qb * P], vtile(h, qb),
                             PTs[:, (qb - 1) * P: qb * P], start=True, stop=False)
                    mm = t.matmul(ps_x[0:VW, (qb - 1) * P: qb * P], vtile(h, 0),
                                  PTg[:, (qb - 1) * P: qb * P], start=False, stop=True)
                mm.then_inc(s_ag, 1)
                for kb in range(16):
                    mm = t.matmul(ps_y[:, kb * P:(kb + 1) * P],
                                  hrows(kf, h)[:, kb * P:(kb + 1) * P],
                                  hrows(qf, h)[:, 0:P], start=True, stop=True)
                mm.then_inc(s_sg, 1)
                t.wait_ge(s_pt, 3 * h + 3)
                for kb in range(16):
                    mm = t.matmul(ps_x[0:VW, 1920:2048], vtile(h, kb),
                                  PT0[:, kb * P:(kb + 1) * P], start=(kb == 0), stop=(kb == 15))
                mm.then_inc(s_ag, 1)
                t.wait_ge(s_rr, 2 * h + 2)
                for j in range(4):
                    mm = t.matmul(ps_y[0:64, SEGS[j]], ones1[:], rz[0:1, SEGS[j]],
                                  start=True, stop=True)
                mm.then_inc(s_zb, 1)

        @block.scalar
        def _(s):
            for h in range(NH):
                s.wait_ge(s_sg, 3 * h + 1)
                if h >= 1:
                    s.wait_ge(s_ag, 2 * h)
                s.activation(PTs[:], ps_x[:, 0:1920], AF.Exp).then_inc(s_pt, 1)
                s.wait_ge(s_sg, 3 * h + 2)
                s.activation(PTg[:], ps_y[:, 0:1920], AF.Exp).then_inc(s_pt, 1)
                s.wait_ge(s_sg, 3 * h + 3)
                s.activation(PT0[:], ps_y[:], AF.Exp).then_inc(s_pt, 1)
                s.wait_ge(s_ag, 2 * h + 2)
                if h >= 1:
                    s.wait_ge(s_rr, 2 * h - 1)
                s.activation(zrow[0:1, P:T], ps_x[64:65, 0:1920], AF.Copy).then_inc(s_zc, 1)
                s.activation(zrow[0:1, 0:P], ps_x[64:65, 1920:2048], AF.Copy).then_inc(s_zc, 1)
                s.wait_ge(s_zc, 4 * h + 2)
                if h >= 1:
                    s.wait_ge(s_zb, h)
                s.activation(rz[:], zrow[:], AF.Ln).then_inc(s_rr, 1)
                s.wait_ge(s_rr, 2 * h + 1)
                s.activation(rz[:], rz[:], AF.Exp, scale=-1.0).then_inc(s_rr, 1)
                s.wait_ge(s_zb, h + 1)
                if h >= 1:
                    s.wait_ge(s_hf, 2 * h)
                s.activation(rzb[:, 0:1024], ps_y[0:64, 0:1024], AF.Copy).then_inc(s_zc, 1)
                s.activation(rzb[:, 1024:2048], ps_y[0:64, 1024:2048], AF.Copy).then_inc(s_zc, 1)

        @block.vector
        def _(v):
            for h in range(NH):
                v.wait_ge(s_zc, 4 * h + 4)
                v.tensor_tensor(hrows(apre, h)[:, P:T], ps_x[0:64, 0:1920], rzb[:, P:T], ALU.mult).then_inc(s_hf, 1)
                v.wait_ge(s_hf, 2 * h + 1)
                v.tensor_tensor(hrows(apre, h)[:, 0:P], ps_x[0:64, 1920:2048], rzb[:, 0:P], ALU.mult).then_inc(s_hf, 1)

        @block.tensor
        def _(t):
            t.wait_ge(s_hf, 32)
            t.wait_ge(s_lw, 144)
            for mc in range(NCH):
                ps = ps_x if mc % 2 == 0 else ps_y
                if mc >= 2:
                    t.wait_ge(s_ow, mc - 1)
                for j in range(4):
                    for kc in range(NCH):
                        mm = t.matmul(ps[:, SEGS[j]],
                                      wbuf2[:, kc * D + mc * P: kc * D + (mc + 1) * P],
                                      apre[kc][:, SEGS[j]], start=(kc == 0), stop=(kc == NCH - 1))
                mm.then_inc(s_mm, 1)

        @block.vector
        def _(v):
            for mc in range(NCH):
                ps = ps_x if mc % 2 == 0 else ps_y
                v.wait_ge(s_mm, mc + 1)
                if mc >= 1:
                    v.wait_ge(s_st, 16 * mc)
                v.tensor_scalar(outw[:], ps[:], biases[:, mc:mc + 1], None,
                                ALU.add).then_inc(s_ow, 1)

        @block.sync
        def _(sync):
            for mc in range(NCH):
                sync.wait_ge(s_ow, mc + 1)
                sync.dma_start(out=o_a1[mc * P:(mc + 1) * P, :], in_=outw[:]).then_inc(s_st, 16)
            sync.wait_ge(s_st, 128)
    es.close()
    if standalone:
        return nc
    return [(s_st, 128)]



def build_s8(nc=None, pre="", io=None):
    standalone = nc is None
    if standalone:
        nc = bass.Bass("TRN2", target_bir_lowering=False)

    def dram(name, shape, dt, kind):
        return io[name] if io is not None else nc.dram_tensor(name, shape, dt, kind=kind).ap()
    xn1 = dram("xn1_b", [D, T], BF16, "ExternalInput")
    Wq = dram("Wq", [D, 2 * D], F32, "ExternalInput")
    bq = dram("bq", [P, 16], F32, "ExternalInput")
    edT = dram("edT", [D, D], F32, "ExternalInput")
    o_q = dram("q1f_b", [2 * D, T], BF16, "ExternalOutput")
    o_G = dram("G_b", [T, D], BF16, "ExternalOutput")
    o_stats = dram("stats", [P, 4], F32, "ExternalOutput")

    es = ExitStack()
    sb = lambda n, s, d: es.enter_context(nc.sbuf_tensor(pre + n, s, d))
    xnb = [sb(f"xnb{c}", [P, T], BF16) for c in range(NCH)]
    wqb = sb("wqb", [P, NCH * 2 * D], BF16)
    edb = sb("edb", [P, NCH * D], BF16)
    bqs = sb("bqs", [P, 16], F32)
    outq = [sb(f"outq{i}", [P, T], BF16) for i in range(2)]
    scrq = sb("scrq", [P, T], BF16)
    s1 = sb("s1", [P, 16], F32)
    s2 = sb("s2", [P, 16], F32)
    par = sb("par", [P, 4], F32)
    outg = [sb(f"outg{i}", [P, D], BF16) for i in range(2)]
    ps_x = es.enter_context(nc.psum_tensor(pre + "ps_x", [P, T], F32))
    ps_y = es.enter_context(nc.psum_tensor(pre + "ps_y", [P, T], F32))
    names = ("s_xn", "s_lw", "s_mm", "s_qv", "s_sq", "s_st", "s_gm", "s_ge", "s_gs")
    sems = {n: es.enter_context(nc.semaphore(pre + n)) for n in names}
    (s_xn, s_lw, s_mm, s_qv, s_sq, s_st, s_gm, s_ge, s_gs) = [sems[n] for n in names]

    with nc.Block() as block:
        @block.sync
        def _(sync):
            for c in range(NCH):
                sync.dma_start(out=xnb[c][:], in_=xn1[c * P:(c + 1) * P, :]).then_inc(s_xn, 16)

        @block.gpsimd
        def _(g):
            for kc in range(NCH):
                g.dma_start(out=wqb[:, kc * 2 * D:(kc + 1) * 2 * D], in_=Wq[kc * P:(kc + 1) * P, :]).then_inc(s_lw, 16)
            for kc in range(NCH):
                g.dma_start(out=edb[:, kc * D:(kc + 1) * D], in_=edT[kc * P:(kc + 1) * P, :]).then_inc(s_lw, 16)
            g.dma_start(out=bqs[:], in_=bq).then_inc(s_lw, 16)

        @block.tensor
        def _(t):
            t.wait_ge(s_lw, 272)
            t.wait_ge(s_xn, 128)
            for mc in range(16):
                ps = ps_x if mc % 2 == 0 else ps_y
                if mc >= 2:
                    t.wait_ge(s_sq, mc - 1)
                for j in range(4):
                    for kc in range(NCH):
                        mm = t.matmul(ps[:, SEGS[j]],
                                      wqb[:, kc * 2 * D + mc * P: kc * 2 * D + (mc + 1) * P],
                                      xnb[kc][:, SEGS[j]], start=(kc == 0), stop=(kc == NCH - 1))
                mm.then_inc(s_mm, 1)
            for tt in range(16):
                ps = ps_x if tt % 2 == 0 else ps_y
                if tt == 0:
                    t.wait_ge(s_sq, 16)
                if tt >= 2:
                    t.wait_ge(s_ge, tt - 1)
                for half in range(2):
                    for kc in range(NCH):
                        mm = t.matmul(ps[:, half * 512:(half + 1) * 512],
                                      xnb[kc][:, tt * P:(tt + 1) * P],
                                      edb[:, kc * D + half * 512: kc * D + half * 512 + 512],
                                      start=(kc == 0), stop=(kc == NCH - 1))
                mm.then_inc(s_gm, 1)

        @block.vector
        def _(v):
            for mc in range(16):
                ps = ps_x if mc % 2 == 0 else ps_y
                v.wait_ge(s_mm, mc + 1)
                if mc >= 2:
                    v.wait_ge(s_st, 16 * mc)
                v.tensor_scalar(outq[mc % 2][:], ps[:], bqs[:, mc:mc + 1], 0.0,
                                ALU.add, ALU.add, accum_out=s1[:, mc:mc + 1]).then_inc(s_qv, 1)
            v.wait_ge(s_sq, 16)
            v.wait_ge(s_qv, 16)
            v.tensor_reduce(par[:, 0:1], s1[:].rearrange("p (c two) -> p two c", two=2)[:, 0:1, :].squeeze(1), AX, ALU.add)
            v.tensor_reduce(par[:, 1:2], s1[:].rearrange("p (c two) -> p two c", two=2)[:, 1:2, :].squeeze(1), AX, ALU.add)
            v.tensor_reduce(par[:, 2:3], s2[:].rearrange("p (c two) -> p two c", two=2)[:, 0:1, :].squeeze(1), AX, ALU.add)
            v.tensor_reduce(par[:, 3:4], s2[:].rearrange("p (c two) -> p two c", two=2)[:, 1:2, :].squeeze(1), AX, ALU.add).then_inc(s_qv, 1)

        @block.scalar
        def _(s):
            for mc in range(16):
                ps = ps_x if mc % 2 == 0 else ps_y
                s.wait_ge(s_qv, mc + 1)
                if mc >= 1:
                    s.wait_ge(s_sq, mc)
                s.activation(scrq[:], ps[:], AF.Square, bias=bqs[:, mc:mc + 1],
                             accum_out=s2[:, mc:mc + 1]).then_inc(s_sq, 1)
            for tt in range(16):
                ps = ps_x if tt % 2 == 0 else ps_y
                s.wait_ge(s_gm, tt + 1)
                if tt >= 2:
                    s.wait_ge(s_gs, 16 * tt)
                s.activation(outg[tt % 2][:], ps[:, 0:D], AF.Gelu).then_inc(s_ge, 1)

        @block.sync
        def _(sync):
            for mc in range(16):
                sync.wait_ge(s_qv, mc + 1)
                sync.dma_start(out=o_q[mc * P:(mc + 1) * P, :], in_=outq[mc % 2][:]).then_inc(s_st, 16)
            sync.wait_ge(s_qv, 17)
            sync.dma_start(out=o_stats, in_=par[:]).then_inc(s_st, 16)

        @block.gpsimd
        def _(g):
            for tt in range(16):
                g.wait_ge(s_ge, tt + 1)
                g.dma_start(out=o_G[tt * P:(tt + 1) * P, :], in_=outg[tt % 2][:]).then_inc(s_gs, 16)
            g.wait_ge(s_gs, 256)

        @block.sync
        def _(sync):
            sync.wait_ge(s_st, 272)
    es.close()
    if standalone:
        return nc
    return [(s_st, 272), (s_gs, 256)]



def build_s9(nc=None, pre="", io=None):
    standalone = nc is None
    if standalone:
        nc = bass.Bass("TRN2", target_bir_lowering=False)

    def dram(name, shape, dt, kind):
        return io[name] if io is not None else nc.dram_tensor(name, shape, dt, kind=kind).ap()
    q_in = dram("q1f_b", [2 * D, T], BF16, "ExternalInput")
    G_in = dram("G_b", [T, D], BF16, "ExternalInput")
    statsall = dram("statsall", [P, 32], F32, "ExternalInput")
    bng = dram("bng", [P, 2], F32, "ExternalInput")
    bnb = dram("bnb", [P, 2], F32, "ExternalInput")
    sk0 = dram("sk0", [32, P], F32, "ExternalInput")
    sk1 = dram("sk1", [32, P], F32, "ExternalInput")
    eu = dram("eu", [D, D], F32, "ExternalInput")
    x1 = dram("x1", [D, T], F32, "ExternalInput")
    a1 = dram("attn1", [D, T], F32, "ExternalInput")
    o_x2 = dram("x2", [D, T], F32, "ExternalOutput")

    NBN = 8 * T * 8
    es = ExitStack()
    sb = lambda n, s, d: es.enter_context(nc.sbuf_tensor(pre + n, s, d))
    q = [sb(f"q{c}", [P, T], BF16) for c in range(16)]
    gT = sb("gT", [P, NCH * T], BF16)
    eub = sb("eub", [P, NCH * D], BF16)
    Gt = [sb(f"Gt{i}", [P, D], BF16) for i in range(2)]
    gt = [sb(f"gt{i}", [P, D], BF16) for i in range(2)]
    Pacc = sb("Pacc", [P, D], F32)
    obuf = sb("obuf", [P, D], F32)
    s01 = sb("s01", [P, 512], F32)
    m8t = sb("m8t", [P, P], F32)
    thr = sb("thr", [P, 512], F32)
    msk = sb("msk", [P, 512], F32)
    ee = sb("ee", [P, 512], F32)
    aa = sb("aa", [P, 512], F32)
    zz = sb("zz", [P, 16], F32)
    zp = sb("zp", [P, 8], F32)
    rzp = sb("rzp", [P, 8], F32)
    scr = sb("scr", [P, T], F32)
    rb = sb("rb", [P, T], F32)
    fin = sb("fin", [P, 4], F32)
    sts = sb("sts", [P, 32], F32)
    msc = sb("msc", [P, 2], F32)
    esc = sb("esc", [P, 2], F32)
    m2c = sb("m2c", [P, 2], F32)
    vrc = sb("vrc", [P, 2], F32)
    bngs = sb("bngs", [P, 2], F32)
    bnbs = sb("bnbs", [P, 2], F32)
    scl = sb("scl", [P, 2], F32)
    shf = sb("shf", [P, 2], F32)
    tmp2 = sb("tmp2", [P, 2], F32)
    sk0s = sb("sk0s", [32, P], F32)
    sk1s = sb("sk1s", [32, P], F32)
    kscr = sb("kscr", [32, P], F32)
    kn2 = sb("kn2", [32, 2], F32)
    k0b = sb("k0b", [32, P], BF16)
    k1b = sb("k1b", [32, P], BF16)
    k0nT = sb("k0nT", [P, 32], BF16)
    k1nT = sb("k1nT", [P, 32], BF16)
    ident = sb("ident", [P, P], F32)
    identb = sb("identb", [P, P], BF16)
    onesP = sb("onesP", [P, P], F32)
    x1p = [sb(f"x1p{i}", [P, T], F32) for i in range(2)]
    a1p = [sb(f"a1p{i}", [P, T], F32) for i in range(2)]
    x2c = [sb(f"x2c{i}", [P, T], F32) for i in range(2)]
    ps_x = es.enter_context(nc.psum_tensor(pre + "ps_x", [P, T], F32))
    ps_y = es.enter_context(nc.psum_tensor(pre + "ps_y", [P, T], F32))
    names = ("s_in", "s_q", "s_eu", "s_g0", "s_g1", "s_fs", "s_kk", "s_id",
             "s_cd", "s_ca", "s_cm", "s_td", "s_ta", "s_tm", "s_om", "s_ov",
             "s_so", "s_so1", "s_sx0", "s_sx1", "s_sa0", "s_sa1", "s_ms")
    sems = {n: es.enter_context(nc.semaphore(pre + n)) for n in names}
    (s_in, s_q, s_eu, s_g0, s_g1, s_fs, s_kk, s_id, s_cd, s_ca, s_cm, s_td,
     s_ta, s_tm, s_om, s_ov, s_so, s_so1, s_sx0, s_sx1, s_sa0, s_sa1, s_ms) = \
        [sems[n] for n in names]

    with nc.Block() as block:
        @block.sync
        def _(sync):
            sync.dma_start(out=sts[:], in_=statsall).then_inc(s_in, 16)
            sync.dma_start(out=bngs[:], in_=bng).then_inc(s_in, 16)
            sync.dma_start(out=bnbs[:], in_=bnb).then_inc(s_in, 16)
            sync.dma_start(out=sk0s[:], in_=sk0).then_inc(s_in, 16)
            sync.dma_start(out=sk1s[:], in_=sk1).then_inc(s_in, 16)
            for c in range(16):
                sync.dma_start(out=q[c][:], in_=q_in[c * P:(c + 1) * P, :]).then_inc(s_q, 16)

        @block.gpsimd
        def _(g):
            for kc in range(NCH):
                g.dma_start(out=eub[:, kc * D:(kc + 1) * D], in_=eu[kc * P:(kc + 1) * P, :]).then_inc(s_eu, 16)
            g.iota(ident[:], pattern=[[-1, P]], base=0, channel_multiplier=1,
                   allow_small_or_imprecise_dtypes=True).then_inc(s_id, 1)
            for tt in range(16):
                if tt >= 2:
                    g.wait_ge(s_td, 48 * (tt - 2) + 47)
                g.dma_start(out=Gt[tt % 2][:], in_=G_in[tt * P:(tt + 1) * P, :]).then_inc(s_g0 if tt % 2 == 0 else s_g1, 16)

        @block.vector
        def _(v):
            v.memset(onesP[:], 1.0).then_inc(s_ms, 1)
            v.wait_ge(s_in, 80)
            for j in range(4):
                v.tensor_reduce(fin[:, j:j + 1],
                                sts[:].rearrange("p (c f) -> p f c", f=4)[:, j:j + 1, :].squeeze(1),
                                AX, ALU.add).then_inc(s_fs, 1)
            v.wait_ge(s_fs, 4)
            v.tensor_scalar(msc[:], fin[:, 0:2], 1.0 / NBN, None, ALU.mult).then_inc(s_fs, 1)
            v.tensor_scalar(esc[:], fin[:, 2:4], 1.0 / NBN, None, ALU.mult).then_inc(s_fs, 1)

        @block.scalar
        def _(s):
            s.wait_ge(s_fs, 6)
            s.activation(m2c[:], msc[:], AF.Square).then_inc(s_fs, 1)

        @block.vector
        def _(v):
            v.wait_ge(s_fs, 7)
            v.tensor_sub(vrc[:], esc[:], m2c[:]).then_inc(s_fs, 1)
            v.wait_ge(s_fs, 8)
            v.tensor_scalar_add(vrc[:], vrc[:], 1e-5).then_inc(s_fs, 1)

        @block.scalar
        def _(s):
            s.wait_ge(s_fs, 9)
            s.activation(vrc[:], vrc[:], AF.Ln).then_inc(s_fs, 1)
            s.wait_ge(s_fs, 10)
            s.activation(vrc[:], vrc[:], AF.Exp, scale=-0.5).then_inc(s_fs, 1)

        @block.vector
        def _(v):
            v.wait_ge(s_fs, 11)
            v.tensor_mul(scl[:], bngs[:], vrc[:]).then_inc(s_fs, 1)
            v.wait_ge(s_fs, 12)
            v.tensor_mul(tmp2[:], msc[:], scl[:]).then_inc(s_fs, 1)
            v.wait_ge(s_fs, 13)
            v.tensor_sub(shf[:], bnbs[:], tmp2[:]).then_inc(s_fs, 1)

        @block.scalar
        def _(s):
            s.wait_ge(s_id, 1)
            s.activation(ident[:], ident[:], AF.Square).then_inc(s_id, 1)

        @block.vector
        def _(v):
            v.wait_ge(s_id, 2)
            v.tensor_scalar(ident[:], ident[:], -1.0, 1.0, ALU.mult, ALU.add).then_inc(s_id, 1)
            v.wait_ge(s_id, 3)
            v.tensor_relu(ident[:], ident[:]).then_inc(s_id, 1)

        @block.scalar
        def _(s):
            s.wait_ge(s_id, 4)
            s.activation(identb[:], ident[:], AF.Copy).then_inc(s_id, 1)
            s.wait_ge(s_in, 80)
            s.activation(kscr[:], sk0s[:], AF.Square).then_inc(s_kk, 1)

        @block.vector
        def _(v):
            v.wait_ge(s_kk, 1)
            v.tensor_reduce(kn2[:, 0:1], kscr[:], AX, ALU.add).then_inc(s_kk, 1)

        @block.scalar
        def _(s):
            s.wait_ge(s_kk, 2)
            s.activation(kscr[:], sk1s[:], AF.Square).then_inc(s_kk, 1)

        @block.vector
        def _(v):
            v.wait_ge(s_kk, 3)
            v.tensor_reduce(kn2[:, 1:2], kscr[:], AX, ALU.add).then_inc(s_kk, 1)

        @block.scalar
        def _(s):
            s.wait_ge(s_kk, 4)
            s.activation(kn2[:], kn2[:], AF.Ln).then_inc(s_kk, 1)
            s.wait_ge(s_kk, 5)
            s.activation(kn2[:], kn2[:], AF.Exp, scale=-0.5).then_inc(s_kk, 1)

        @block.vector
        def _(v):
            v.wait_ge(s_kk, 6)
            v.tensor_scalar(kscr[:], sk0s[:], kn2[:, 0:1], None, ALU.mult).then_inc(s_kk, 1)

        @block.scalar
        def _(s):
            s.wait_ge(s_kk, 7)
            s.activation(k0b[:], kscr[:], AF.Copy).then_inc(s_kk, 1)

        @block.vector
        def _(v):
            v.wait_ge(s_kk, 8)
            v.tensor_scalar(kscr[:], sk1s[:], kn2[:, 1:2], None, ALU.mult).then_inc(s_kk, 1)

        @block.scalar
        def _(s):
            s.wait_ge(s_kk, 9)
            s.activation(k1b[:], kscr[:], AF.Copy).then_inc(s_kk, 1)

        @block.tensor
        def _(t):
            t.wait_ge(s_kk, 10)
            t.wait_ge(s_id, 5)
            tr = t.transpose(ps_x[:, 0:16].bitcast(BF16)[:, 0:32], k0b[0:32, :], identb[0:32, 0:32])
            tr.then_inc(s_kk, 1)

        @block.vector
        def _(v):
            v.wait_ge(s_kk, 11)
            v.tensor_copy(k0nT[:], ps_x[:, 0:16].bitcast(BF16)[:, 0:32]).then_inc(s_kk, 1)

        @block.tensor
        def _(t):
            t.wait_ge(s_kk, 12)
            tr = t.transpose(ps_x[:, 0:16].bitcast(BF16)[:, 0:32], k1b[0:32, :], identb[0:32, 0:32])
            tr.then_inc(s_kk, 1)

        @block.vector
        def _(v):
            v.wait_ge(s_kk, 13)
            v.tensor_copy(k1nT[:], ps_x[:, 0:16].bitcast(BF16)[:, 0:32]).then_inc(s_kk, 1)

        @block.vector
        def _(v):
            v.wait_ge(s_q, 256)
            v.wait_ge(s_fs, 14)
            for mc in range(16):
                pr = mc % 2
                if mc >= 1:
                    v.wait_ge(s_cd, 2 * mc)
                v.tensor_scalar(q[mc][:], q[mc][:], scl[:, pr:pr + 1],
                                shf[:, pr:pr + 1], ALU.mult, ALU.add).then_inc(s_cd, 1)
                v.wait_ge(s_ca, 3 * mc + 3)
                v.tensor_mul(q[mc][:], q[mc][:], rb[:]).then_inc(s_cd, 1)

        @block.scalar
        def _(s):
            for mc in range(16):
                s.wait_ge(s_cd, 2 * mc + 1)
                if mc >= 1:
                    s.wait_ge(s_cm, mc)
                s.activation(scr[:], q[mc][:], AF.Square).then_inc(s_ca, 1)
                s.wait_ge(s_cm, mc + 1)
                if mc >= 1:
                    s.wait_ge(s_cd, 2 * mc)
                s.activation(rb[:], ps_y[:], AF.Ln).then_inc(s_ca, 1)
                s.wait_ge(s_ca, 3 * mc + 2)
                s.activation(rb[:], rb[:], AF.Exp, scale=-0.5).then_inc(s_ca, 1)

        @block.tensor
        def _(t):
            t.wait_ge(s_ms, 1)
            for mc in range(16):
                t.wait_ge(s_ca, 3 * mc + 1)
                if mc >= 1:
                    t.wait_ge(s_ca, 3 * mc)
                for j in range(4):
                    mm = t.matmul(ps_y[:, SEGS[j]], onesP[:], scr[:, SEGS[j]],
                                  start=True, stop=True)
                mm.then_inc(s_cm, 1)

        @block.tensor
        def _(t):
            t.wait_ge(s_cd, 32)
            t.wait_ge(s_kk, 14)
            for tt in range(16):
                pp = (tt % 2) * 512
                if tt >= 2:
                    t.wait_ge(s_td, 48 * (tt - 2) + 1)
                for h in range(8):
                    t.matmul(ps_x[:, pp + h * 64: pp + h * 64 + 32],
                             q[2 * h][:, tt * P:(tt + 1) * P], k0nT[:],
                             start=True, stop=True)
                    mm = t.matmul(ps_x[:, pp + h * 64 + 32: pp + h * 64 + 64],
                                  q[2 * h + 1][:, tt * P:(tt + 1) * P], k1nT[:],
                                  start=True, stop=True)
                mm.then_inc(s_tm, 1)
                t.wait_ge(s_td, 48 * tt + 47)
                for ec in range(NCH):
                    tr = t.transpose(ps_x[:, 1024:1536].bitcast(BF16)[:, ec * P:(ec + 1) * P],
                                     gt[tt % 2][:, ec * P:(ec + 1) * P], identb[:])
                tr.then_inc(s_tm, 1)

        @block.scalar
        def _(s):
            for tt in range(16):
                s.wait_ge(s_td, 48 * tt + 1)
                if tt >= 1:
                    s.wait_ge(s_td, 48 * (tt - 1) + 20)
                s.activation(ee[:], s01[:], AF.Exp).then_inc(s_ta, 1)

        @block.vector
        def _(v):
            ntd = 0

            def inc(op):
                nonlocal ntd
                op.then_inc(s_td, 1)
                ntd += 1
            for tt in range(16):
                pp = (tt % 2) * 512
                base = 48 * tt
                assert ntd == base, (tt, ntd, base)
                v.wait_ge(s_tm, 2 * tt + 1)
                if tt >= 1:
                    v.wait_ge(s_ta, tt)
                inc(v.tensor_copy(s01[:], ps_x[:, pp:pp + 512]))
                v.wait_ge(s_td, base + 1)
                for sseg in range(16):
                    inc(v.max(out=m8t[:, sseg * 8:(sseg + 1) * 8], in_=s01[:, sseg * 32:(sseg + 1) * 32]))
                v.wait_ge(s_td, base + 17)
                inc(v.tensor_copy(thr[:].rearrange("p (s k) -> p s k", k=32),
                                  m8t[:].rearrange("p (s e) -> p s e", e=8)[:, :, 3:4].broadcast_to([P, 16, 32])))
                v.wait_ge(s_td, base + 18)
                inc(v.tensor_tensor(msk[:], s01[:], thr[:], ALU.is_ge))
                v.wait_ge(s_ta, tt + 1)
                v.wait_ge(s_td, base + 19)
                inc(v.tensor_tensor(aa[:], ee[:], msk[:], ALU.mult))
                v.wait_ge(s_td, base + 20)
                inc(v.tensor_reduce(zz[:], aa[:].rearrange("p (s k) -> p s k", k=32), AX, ALU.add))
                v.wait_ge(s_td, base + 21)
                inc(v.tensor_tensor(zp[:], zz[:].rearrange("p (h two) -> p h two", two=2)[:, :, 0:1].squeeze(2),
                                    zz[:].rearrange("p (h two) -> p h two", two=2)[:, :, 1:2].squeeze(2),
                                    ALU.mult))
                v.wait_ge(s_td, base + 22)
                inc(v.reciprocal(rzp[:], zp[:]))
                v.wait_ge(s_td, base + 23)
                for h in range(8):
                    inc(v.tensor_scalar(aa[:, h * 64:h * 64 + 32], aa[:, h * 64:h * 64 + 32],
                                        rzp[:, h:h + 1], None, ALU.mult))
                v.wait_ge(s_td, base + 31)
                for h in range(8):
                    A0 = aa[:, h * 64:h * 64 + 32].unsqueeze(2).broadcast_to([P, 32, 32])
                    A1 = aa[:, h * 64 + 32:h * 64 + 64].unsqueeze(1).broadcast_to([P, 32, 32])
                    if h == 0:
                        inc(v.tensor_tensor(Pacc[:].rearrange("p (i j) -> p i j", j=32), A0, A1, ALU.mult))
                    else:
                        v.wait_ge(s_td, ntd)
                        inc(v.tensor_tensor(obuf[:].rearrange("p (i j) -> p i j", j=32), A0, A1, ALU.mult))
                        v.wait_ge(s_td, ntd)
                        inc(v.tensor_add(Pacc[:], Pacc[:], obuf[:]))
                v.wait_ge(s_td, base + 46)
                v.wait_ge(s_g0 if tt % 2 == 0 else s_g1, 16 * (tt // 2 + 1))
                inc(v.tensor_tensor(gt[tt % 2][:], Pacc[:], Gt[tt % 2][:], ALU.mult))
                v.wait_ge(s_tm, 2 * tt + 2)
                inc(v.tensor_copy(gT[:].rearrange("p (e t) -> p e t", t=T)[:, :, tt * P:(tt + 1) * P],
                                  ps_x[:, 1024:1536].bitcast(BF16).rearrange("p (e k) -> p e k", k=P)))

        @block.sync
        def _(sync):
            for mc in range(NCH):
                if mc >= 2:
                    sync.wait_ge(s_ov, 2 * (mc - 1))
                sync.dma_start(out=x1p[mc % 2][:], in_=x1[mc * P:(mc + 1) * P, :]).then_inc(s_sx0 if mc % 2 == 0 else s_sx1, 16)
                sync.dma_start(out=a1p[mc % 2][:], in_=a1[mc * P:(mc + 1) * P, :]).then_inc(s_sa0 if mc % 2 == 0 else s_sa1, 16)

        @block.tensor
        def _(t):
            t.wait_ge(s_td, 768)
            t.wait_ge(s_eu, 128)
            for mc in range(NCH):
                ps = ps_y if mc % 2 == 0 else ps_x
                if mc >= 2:
                    t.wait_ge(s_ov, 2 * (mc - 1))
                for j in range(4):
                    for ec in range(NCH):
                        mm = t.matmul(ps[:, SEGS[j]],
                                      eub[:, ec * D + mc * P: ec * D + (mc + 1) * P],
                                      gT[:, ec * T + j * 512: ec * T + j * 512 + 512],
                                      start=(ec == 0), stop=(ec == NCH - 1))
                mm.then_inc(s_om, 1)

        @block.vector
        def _(v):
            for mc in range(NCH):
                ps = ps_y if mc % 2 == 0 else ps_x
                v.wait_ge(s_om, mc + 1)
                v.wait_ge(s_sx0 if mc % 2 == 0 else s_sx1, 16 * (mc // 2 + 1))
                v.wait_ge(s_sa0 if mc % 2 == 0 else s_sa1, 16 * (mc // 2 + 1))
                if mc >= 2:
                    v.wait_ge(s_so if mc % 2 == 0 else s_so1, 16 * ((mc - 2) // 2 + 1))
                v.tensor_tensor(x2c[mc % 2][:], ps[:], x1p[mc % 2][:], ALU.add).then_inc(s_ov, 1)
                v.wait_ge(s_ov, 2 * mc + 1)
                v.tensor_tensor(x2c[mc % 2][:], x2c[mc % 2][:], a1p[mc % 2][:], ALU.add).then_inc(s_ov, 1)

        @block.scalar
        def _(s):
            for mc in range(NCH):
                s.wait_ge(s_ov, 2 * (mc + 1))
                s.dma_start(out=o_x2[mc * P:(mc + 1) * P, :], in_=x2c[mc % 2][:]).then_inc(s_so if mc % 2 == 0 else s_so1, 16)
            s.wait_ge(s_so, 64)
            s.wait_ge(s_so1, 64)
    es.close()
    if standalone:
        return nc
    return [(s_so, 64), (s_so1, 64)]




def _stage_barrier(nc, s_fin, k, terms):
    with nc.Block() as blk:
        @blk.sync
        def _(sync):
            for sem, v in terms:
                sync.wait_ge(sem, v)
            sync.nop().then_inc(s_fin, 1)
            sync.wait_ge(s_fin, k)

        @blk.gpsimd
        def _(g):
            g.wait_ge(s_fin, k)

        @blk.vector
        def _(v):
            v.wait_ge(s_fin, k)

        @blk.scalar
        def _(s):
            s.wait_ge(s_fin, k)

        @blk.tensor
        def _(t):
            t.wait_ge(s_fin, k)


def _build_fused(ei, eo, internal, stages):
    nc = bass.Bass("TRN2", target_bir_lowering=False)
    io = {}
    for n, s, d in ei:
        io[n] = nc.dram_tensor(n, s, d, kind="ExternalInput").ap()
    for n, s, d in eo:
        io[n] = nc.dram_tensor(n, s, d, kind="ExternalOutput").ap()
    for n, s, d in internal:
        io[n] = nc.dram_tensor(n, s, d, kind="Internal").ap()
    es = ExitStack()
    s_fin = es.enter_context(nc.semaphore("s_fin"))
    for k, (b, pre) in enumerate(stages):
        terms = b(nc=nc, pre=pre, io=io)
        if k < len(stages) - 1:
            _stage_barrier(nc, s_fin, k + 1, terms)
    es.close()
    return nc


def build_fA():
    ei = [("xT", [D, T], F32), ("Wq", [D, D], F32), ("Wk", [D, D], F32),
          ("Wv", [D, D], F32), ("Wdown", [D, 256], F32), ("Wupk", [256, D], F32),
          ("Wupv", [256, D], F32), ("Wo", [D, D], F32),
          ("lng", [P, NCH], F32), ("lnb", [P, NCH], F32), ("aw1", [D, 4 * D], F32),
          ("ab1", [P, 32], F32), ("aw2", [4 * D, D], F32), ("ab2", [P, NCH], F32),
          ("gwt", [P, 48], F32), ("gate_b", [1, 6], F32),
          ("ew1", [6 * D, 2 * D], F32), ("eb1t", [P, 96], F32),
          ("ew2", [6 * 2 * D, D], F32), ("eb2", [6, D], F32), ("G", [D, D], F32)]
    eo = [("xn_f", [D, T], F32), ("attnwo", [D, T], F32), ("abel_b", [D, T], BF16),
          ("moe", [D, T], F32), ("E", [D, D], F32), ("ET", [D, D], F32)]
    internal = [("xn_b", [D, T], BF16), ("q_b", [D, T], BF16),
                ("krec_b", [D, T], BF16), ("vrec_b", [D, T], BF16),
                ("xn2_b", [D, T], BF16), ("wT6", [8, T], F32)]
    return _build_fused(ei, eo, internal,
                        [(build_s1, "a1"), (build_s2, "a2"), (build_s3, "a3"),
                         (build_s4, "a4"), (build_s5, "a5")])


def build_fB():
    ei = [("E0T", [D, D], F32), ("E1", [D, D], F32), ("E2", [D, D], F32),
          ("E3", [D, D], F32), ("moe", [D, T], F32), ("abel_b", [D, T], BF16),
          ("xn_f", [D, T], F32), ("attnwo", [D, T], F32), ("xT", [D, T], F32),
          ("gcwt", [P, 16], F32), ("gcb", [1, 1], F32),
          ("wqkvT", [D, 3 * D], F32), ("bqkv", [P, 24], F32),
          ("woutT", [D, D], F32), ("bout", [P, NCH], F32),
          ("Wq", [D, 2 * D], F32), ("bq", [P, 16], F32), ("edT", [D, D], F32)]
    eo = [("x1", [D, T], F32), ("attn1", [D, T], F32), ("q1f_b", [2 * D, T], BF16),
          ("G_b", [T, D], BF16), ("stats", [P, 4], F32)]
    internal = [("xn1_b", [D, T], BF16), ("q1_b", [D, T], BF16),
                ("k1_b", [D, T], BF16), ("v1_b", [D, T], BF16)]
    return _build_fused(ei, eo, internal,
                        [(build_s6, "b6"), (build_s7a, "b7"), (build_s7b, "b8"),
                         (build_s8, "b9")])



_NC_CACHE = {}


def _get(builder):
    if builder.__name__ not in _NC_CACHE:
        _NC_CACHE[builder.__name__] = builder()
    return _NC_CACHE[builder.__name__]


def _run(builder, in_maps):
    nc = _get(builder)
    return run_bass_kernel_spmd(nc, in_maps, core_ids=list(range(NB))).results


def _ct(a):
    return np.ascontiguousarray(a)


def kernel(**inp):
    xT = [_ct(inp['L0_attn_x'][b].T) for b in range(NB)]

    lng = _ct(inp['L0_ffn_ln_g'].reshape(NCH, P).T)
    lnb = _ct(inp['L0_ffn_ln_b'].reshape(NCH, P).T)
    ab1 = _ct(inp['L0_ffn_ab1'].reshape(32, P).T)
    ab2 = _ct(inp['L0_ffn_ab2'].reshape(NCH, P).T)
    gwt = _ct(inp['L0_ffn_gate_w'].reshape(NCH, P, 6).transpose(1, 0, 2).reshape(P, 48))
    ew1 = inp['L0_ffn_ew1'].reshape(6 * D, 2 * D)
    eb1t = _ct(inp['L0_ffn_eb1'].reshape(6, 16, P).transpose(2, 0, 1).reshape(P, 96))
    ew2 = inp['L0_ffn_ew2'].reshape(6 * 2 * D, D)
    rA = _run(build_fA, [dict(
        xT=xT[b], Wq=inp['L0_attn_Wq'], Wk=inp['L0_attn_Wk'], Wv=inp['L0_attn_Wv'],
        Wdown=inp['L0_attn_W_down_kv'], Wupk=inp['L0_attn_W_up_k'],
        Wupv=inp['L0_attn_W_up_v'], Wo=inp['L0_attn_Wo'],
        lng=lng, lnb=lnb, aw1=inp['L0_ffn_aw1'], ab1=ab1,
        aw2=inp['L0_ffn_aw2'], ab2=ab2, gwt=gwt,
        gate_b=inp['L0_ffn_gate_b'].reshape(1, 6),
        ew1=ew1, eb1t=eb1t, ew2=ew2, eb2=inp['L0_ffn_eb2'],
        G=_ct(inp['L0_ffn_gens'][b % 4])) for b in range(NB)])

    gcwt = _ct(inp['L0_ffn_gc_w'].reshape(16, P).T)
    gcb = inp['L0_ffn_gc_b'].reshape(1, 1)
    wqkvT = _ct(inp['L1_attn_w_qkv'].T)
    bqkv = _ct(inp['L1_attn_b_qkv'].reshape(24, P).T)
    woutT = _ct(inp['L1_attn_w_out'].T)
    bout = _ct(inp['L1_attn_b_out'].reshape(NCH, P).T)
    bq = _ct(inp['L1_ffn_bq'].reshape(16, P).T)
    edT = _ct(inp['L1_ffn_expert_down'].T)
    rB = _run(build_fB, [dict(
        E0T=rA[0]['ET'], E1=rA[1]['E'], E2=rA[2]['E'], E3=rA[3]['E'],
        moe=rA[b]['moe'], abel_b=rA[b]['abel_b'], xn_f=rA[b]['xn_f'],
        attnwo=rA[b]['attnwo'], xT=xT[b], gcwt=gcwt, gcb=gcb,
        wqkvT=wqkvT, bqkv=bqkv, woutT=woutT, bout=bout,
        Wq=inp['L1_ffn_Wq'], bq=bq, edT=edT) for b in range(NB)])

    statsall = _ct(np.concatenate([rB[b]['stats'] for b in range(NB)], axis=1))
    bng = _ct(inp['L1_ffn_bn_gamma'].reshape(2, P).T)
    bnb = _ct(inp['L1_ffn_bn_beta'].reshape(2, P).T)
    r9 = _run(build_s9, [dict(
        q1f_b=rB[b]['q1f_b'], G_b=rB[b]['G_b'], statsall=statsall, bng=bng,
        bnb=bnb, sk0=inp['L1_ffn_sub_key0'], sk1=inp['L1_ffn_sub_key1'],
        eu=inp['L1_ffn_expert_up'], x1=rB[b]['x1'], attn1=rB[b]['attn1'])
        for b in range(NB)])

    return np.stack([r9[b]['x2'].T for b in range(NB)]).astype(np.float32)
